# version 08
# speedup vs baseline: 1.0821x; 1.0753x over previous
import contextlib
import numpy as np
import ml_dtypes
import concourse.bass as bass
import concourse.mybir as mybir
from concourse.bass_utils import run_bass_kernel_spmd

F32 = mybir.dt.float32
BF16 = mybir.dt.bfloat16
AF = mybir.ActivationFunctionType
ALU = mybir.AluOpType
AX = mybir.AxisListType

NCORES = 8
B_, S_, H_ = 4, 4096, 1024
T = 2048
NT = T // 128
NH = 16
HD = 64
BF = ml_dtypes.bfloat16

ENGINES = ("tensor", "vector", "scalar", "gpsimd", "sync")


class Prog:

    def __init__(self, nc, stack):
        self.nc = nc
        self.stack = stack
        self.seq = {e: [] for e in ENGINES}
        self.semh = {}
        self.semv = {}

    def sem(self, name):
        h = self.stack.enter_context(self.nc.semaphore(name))
        self.semh[name] = h
        self.semv[name] = 0
        return name

    def v(self, s):
        return self.semv[s]

    def w(self, eng, s, val=None):
        if val is None:
            val = self.semv[s]
        h = self.semh[s]
        self.seq[eng].append(lambda e, h=h, val=val: e.wait_ge(h, val))
        return val

    def i(self, eng, fn, inc=None, n=1):
        if inc is None:
            self.seq[eng].append(fn)
            return None
        h = self.semh[inc]
        self.semv[inc] += n

        def run(e, fn=fn, h=h, n=n):
            r = fn(e)
            r.then_inc(h, n)

        self.seq[eng].append(run)
        return self.semv[inc]

    def dma(self, eng, out, in_, inc):
        return self.i(eng, lambda e, o=out, i_=in_: e.dma_start(out=o, in_=i_), inc=inc, n=16)


    def barrier(self):
        vals = dict(self.semv)
        for e in ENGINES:
            for s, v in vals.items():
                if v > 0:
                    self.w(e, s, v)

    def build_block(self):
        nc = self.nc
        with nc.Block() as block:
            for name in ENGINES:
                deco = getattr(block, name)
                seq = self.seq[name]

                def section(eng, seq=seq):
                    for fn in seq:
                        fn(eng)

                deco(section)


def _ln_ab(p, hs_ap, bn_sb, ab_sb, eps, dv, ac, tag="", eps_ap=None):
    nc = p.nc
    p.i("vector", lambda v: v.bn_stats(bn_sb[:, 0:6], hs_ap[:, 0:512]))
    p.i("vector", lambda v: v.bn_stats(bn_sb[:, 6:12], hs_ap[:, 512:1024]), inc=dv)
    p.w("vector", dv)
    p.i("vector", lambda v: v.bn_aggr(bn_sb[:, 12:14], bn_sb[:, 0:12]), inc=dv)
    if eps_ap is not None:
        p.w("scalar", dv)
        p.i("scalar", lambda s: s.activation(bn_sb[:, 15:16], bn_sb[:, 13:14], AF.Ln,
                                             bias=eps_ap), inc=ac)
    else:
        p.w("vector", dv)
        p.i("vector", lambda v: v.tensor_scalar(bn_sb[:, 14:15], bn_sb[:, 13:14], eps, None,
                                                ALU.add), inc=dv)
        p.w("scalar", dv)
        p.i("scalar", lambda s: s.activation(bn_sb[:, 15:16], bn_sb[:, 14:15], AF.Ln), inc=ac)
    p.w("scalar", ac)
    p.i("scalar", lambda s: s.activation(ab_sb[:, 0:1], bn_sb[:, 15:16], AF.Exp, scale=-0.5), inc=ac)
    p.w("vector", ac)
    p.i("vector", lambda v: v.tensor_scalar(ab_sb[:, 1:2], bn_sb[:, 12:13], ab_sb[:, 0:1], -1.0,
                                            ALU.mult, op1=ALU.mult), inc=dv)
    p.w("vector", dv)


def build_l0_attn():
    nc = bass.Bass("TRN2", target_bir_lowering=False)
    x_own = nc.dram_tensor("x_own", [T, H_], F32, kind="ExternalInput").ap()
    x_oth = nc.dram_tensor("x_oth", [T, H_], F32, kind="ExternalInput").ap()
    wq = nc.dram_tensor("wq", [H_, H_], F32, kind="ExternalInput").ap()
    wk = nc.dram_tensor("wk", [H_, H_], F32, kind="ExternalInput").ap()
    wv = nc.dram_tensor("wv", [H_, H_], F32, kind="ExternalInput").ap()
    wd = nc.dram_tensor("wd", [H_, H_], F32, kind="ExternalInput").ap()
    bq = nc.dram_tensor("bq", [H_], F32, kind="ExternalInput").ap()
    bk = nc.dram_tensor("bk", [H_], F32, kind="ExternalInput").ap()
    bv = nc.dram_tensor("bv", [H_], F32, kind="ExternalInput").ap()
    bd = nc.dram_tensor("bd", [H_], F32, kind="ExternalInput").ap()
    gam = nc.dram_tensor("gam", [H_], F32, kind="ExternalInput").ap()
    bet = nc.dram_tensor("bet", [H_], F32, kind="ExternalInput").ap()
    idn = nc.dram_tensor("idn", [128, 128], F32, kind="ExternalInput").ap()
    y = nc.dram_tensor("y", [T, H_], F32, kind="ExternalOutput").ap()

    with contextlib.ExitStack() as stack:
        E = stack.enter_context
        x_t = E(nc.sbuf_tensor("x_t", [128, 2, H_], F32))
        xn_own = E(nc.sbuf_tensor("xn_own", [128, NT, H_], BF16))
        xn_fm = E(nc.sbuf_tensor("xn_fm", [128, 2, 8, 128], BF16))
        qf = E(nc.sbuf_tensor("qf", [128, 8, T], BF16))
        kh = E(nc.sbuf_tensor("kh", [128, 2, H_], BF16))
        vh = E(nc.sbuf_tensor("vh", [128, 2, NH, 65], BF16))
        kv_sb = E(nc.sbuf_tensor("kv_sb", [128, NH // 2, 65], F32))
        kv_bf = E(nc.sbuf_tensor("kv_bf", [128, NH // 2, 65], BF16))
        w_sb = {}
        for nm in ("wq", "wk", "wvd"):
            w_sb[nm] = E(nc.sbuf_tensor(nm + "_sb", [128, 8, H_], BF16))
        brow = {}
        for nm in ("bq", "bk", "bv", "bd"):
            brow[nm] = E(nc.sbuf_tensor(nm + "_r", [1, H_], BF16))
        gam_r = E(nc.sbuf_tensor("gam_r", [1, H_], F32))
        bet_r = E(nc.sbuf_tensor("bet_r", [1, H_], F32))
        gam_bc = E(nc.sbuf_tensor("gam_bc", [128, H_], F32))
        bet_bc = E(nc.sbuf_tensor("bet_bc", [128, H_], F32))
        ones_b = E(nc.sbuf_tensor("ones_b", [1, 128], BF16))
        ones_f = E(nc.sbuf_tensor("ones_f", [1, 128], F32))
        idn_b = E(nc.sbuf_tensor("idn_b", [128, 128], BF16))
        sA = E(nc.sbuf_tensor("sA", [128, H_], F32))
        sB = E(nc.sbuf_tensor("sB", [128, H_], F32))
        qtm = E(nc.sbuf_tensor("qtm", [128, H_], BF16))
        ctm = E(nc.sbuf_tensor("ctm", [128, H_], BF16))
        cfm = E(nc.sbuf_tensor("cfm", [128, 8, 128], BF16))
        hs_sb = E(nc.sbuf_tensor("hs_sb", [128, H_], F32))
        u2_sb = E(nc.sbuf_tensor("u2_sb", [128, H_], F32))
        x2_t = E(nc.sbuf_tensor("x2_t", [128, 2, H_], F32))
        out_t = E(nc.sbuf_tensor("out_t", [128, 2, H_], F32))
        bn_sb = E(nc.sbuf_tensor("bn_sb", [128, 16], F32))
        ab_sb = E(nc.sbuf_tensor("ab_sb", [128, 2], F32))
        ab2_sb = E(nc.sbuf_tensor("ab2_sb", [128, 2], F32))
        zden = E(nc.sbuf_tensor("zden", [128, NH], F32))
        tp_ps = E(nc.psum_tensor("tp_ps", [128, 8, 128], BF16))
        pkq = E(nc.psum_tensor("pkq", [128, H_], F32))
        pv = E(nc.psum_tensor("pv", [128, H_], F32))
        kvp = [E(nc.psum_tensor(f"kvp{i}", [128, 512], F32)) for i in range(2)]

        p = Prog(nc, stack)
        ld = p.sem("ld")
        lw = p.sem("lw")
        pe = p.sem("pe")
        dv = p.sem("dv")
        ac = p.sem("ac")
        st = p.sem("st")
        lx = p.sem("lx")

        for nm, src in (("wq", wq), ("wk", wk), ("wvd", wv)):
            p.dma("gpsimd", w_sb[nm][:, :, :], src.rearrange("(c p) o -> p c o", p=128), inc=lw)
        for nm, src in (("bq", bq), ("bk", bk), ("bv", bv), ("bd", bd)):
            p.dma("gpsimd", brow[nm][:, :], src[None, :], inc=lw)
        p.dma("gpsimd", gam_r[:, :], gam[None, :], inc=lw)
        p.dma("gpsimd", bet_r[:, :], bet[None, :], inc=lw)
        p.dma("gpsimd", idn_b[:, :], idn[:, :], inc=lw)
        p.i("gpsimd", lambda g: g.memset(ones_b[:, :], 1.0))
        p.i("gpsimd", lambda g: g.memset(ones_f[:, :], 1.0))
        p.i("gpsimd", lambda g: g.memset(kv_sb[:, :, :], 0.0))
        p.i("gpsimd", lambda g: g.memset(vh[:, :, :, :], 1.0), inc=lw, n=1)
        lw_done = p.v(lw)

        p.w("tensor", lw, lw_done)
        for half in range(2):
            sl = slice(half * 512, (half + 1) * 512)
            p.i("tensor", lambda t, sl=sl: t.matmul(pv[:, sl], ones_f[:, :], gam_r[:, sl],
                                                    start=True, stop=True))
        p.i("tensor", lambda t: t.matmul(pkq[:, 0:512], ones_f[:, :], bet_r[:, 0:512], start=True, stop=True))
        p.i("tensor", lambda t: t.matmul(pkq[:, 512:1024], ones_f[:, :], bet_r[:, 512:1024],
                                         start=True, stop=True), inc=pe)
        p.w("vector", pe)
        p.i("vector", lambda v: v.tensor_copy(gam_bc[:, :], pv[:, :]))
        p.i("vector", lambda v: v.tensor_copy(bet_bc[:, :], pkq[:, :]), inc=dv)
        bcast_done_dv = p.v(dv)

        p.w("vector", lw, lw_done)
        kv_last_pe = 0
        qtm_free_pe = 0
        xn_dv_hist = {}
        prev_psum_read_dv = 0
        for t in range(32):
            own = t < NT
            r = t % 2
            src = x_own if own else x_oth
            row0 = (t % NT) * 128
            if t >= 2:
                p.w("sync", dv, xn_dv_hist[t - 2])
            ld_val = p.dma("sync", x_t[:, r, :], src[row0:row0 + 128, :], inc=ld)

            p.w("vector", ld, ld_val)
            _ln_ab(p, x_t[:, r, :], bn_sb, ab_sb, 1e-5, dv, ac)
            if own:
                xn_dst = xn_own[:, t, :]
            else:
                xn_dst = qtm[:, :]
                p.w("vector", pe, qtm_free_pe)
            p.i("vector", lambda v, d=xn_dst, rr=r: v.tensor_scalar(
                d, x_t[:, rr, :], ab_sb[:, 0:1], ab_sb[:, 1:2], ALU.mult, op1=ALU.add), inc=dv)
            xn_dv = p.v(dv)
            xn_dv_hist[t] = xn_dv

            p.w("tensor", dv, xn_dv)
            xn_src = xn_own[:, t, :] if own else qtm[:, :]
            for c in range(8):
                p.i("tensor", lambda tt, c=c, s=xn_src: tt.transpose(
                    tp_ps[:, c, :], s[:, c * 128:(c + 1) * 128], idn_b[:, :]),
                    inc=pe if c == 7 else None)
            tp_pe = p.v(pe)
            if not own:
                qtm_free_pe = tp_pe
            p.w("vector", pe, tp_pe)
            p.i("vector", lambda v, rr=r: v.tensor_copy(
                xn_fm[:, rr, :, :], tp_ps[:, :, :]), inc=dv)
            fm_dv = p.v(dv)

            p.w("tensor", dv, fm_dv)
            p.w("tensor", dv, prev_psum_read_dv)
            for wn, bn_, ps in (("wk", "bk", pkq), ("wvd", "bv", pv)):
                for half in range(2):
                    sl = slice(half * 512, (half + 1) * 512)
                    for c in range(8):
                        p.i("tensor", lambda tt, c=c, sl=sl, wn=wn, ps=ps, rr=r: tt.matmul(
                            ps[:, sl], xn_fm[:, rr, c, :], w_sb[wn][:, c, sl],
                            start=(c == 0), stop=False))
                    p.i("tensor", lambda tt, sl=sl, bn_=bn_, ps=ps: tt.matmul(
                        ps[:, sl], ones_b[:, :], brow[bn_][:, sl], start=False, stop=True),
                        inc=pe)
            proj_pe = p.v(pe)

            p.w("vector", pe, proj_pe)
            p.i("vector", lambda v: v.tensor_scalar(sA[:, :], pkq[:, :], 0.0, None, ALU.min), inc=dv)
            p.w("scalar", dv)
            p.i("scalar", lambda s: s.activation(sB[:, :], sA[:, :], AF.Exp), inc=ac)
            p.i("vector", lambda v: v.tensor_scalar(sA[:, :], pkq[:, :], 0.0, None, ALU.max), inc=dv)
            k_reads_dv = p.v(dv)
            p.w("vector", ac)
            p.w("vector", dv)
            p.w("vector", pe, kv_last_pe)
            p.i("vector", lambda v, rr=r: v.tensor_tensor(kh[:, rr, :], sB[:, :], sA[:, :], ALU.add), inc=dv)
            kh_dv = p.v(dv)
            p.i("vector", lambda v, rr=r: v.tensor_copy(
                vh[:, rr, :, 0:64],
                pv.rearrange("p (h d) -> p h d", h=16)), inc=dv)
            vh_dv = p.v(dv)

            if own:
                p.w("tensor", dv, k_reads_dv)
                for half in range(2):
                    sl = slice(half * 512, (half + 1) * 512)
                    for c in range(8):
                        p.i("tensor", lambda tt, c=c, sl=sl, rr=r: tt.matmul(
                            pkq[:, sl], xn_fm[:, rr, c, :], w_sb["wq"][:, c, sl],
                            start=(c == 0), stop=False))
                    p.i("tensor", lambda tt, sl=sl: tt.matmul(
                        pkq[:, sl], ones_b[:, :], brow["bq"][:, sl], start=False, stop=True),
                        inc=pe)
                qproj_pe = p.v(pe)
                p.w("vector", pe, qproj_pe)
                p.i("vector", lambda v: v.tensor_scalar(sA[:, :], pkq[:, :], 0.0, None, ALU.min), inc=dv)
                p.w("scalar", dv)
                p.i("scalar", lambda s: s.activation(sB[:, :], sA[:, :], AF.Exp), inc=ac)
                p.i("vector", lambda v: v.tensor_scalar(sA[:, :], pkq[:, :], 0.0, None, ALU.max), inc=dv)
                p.w("vector", ac)
                p.w("vector", dv)
                p.w("vector", pe, qtm_free_pe)
                p.i("vector", lambda v: v.tensor_tensor(qtm[:, :], sB[:, :], sA[:, :], ALU.add), inc=dv)
                q_dv = p.v(dv)
                p.w("tensor", dv, q_dv)
                for hp in range(8):
                    p.i("tensor", lambda tt, hp=hp: tt.transpose(
                        tp_ps[:, hp, :], qtm[:, hp * 128:(hp + 1) * 128], idn_b[:, :]),
                        inc=pe if hp == 7 else None)
                qtm_free_pe = p.v(pe)
                p.w("vector", pe, qtm_free_pe)
                p.i("vector", lambda v, t=t: v.tensor_copy(
                    qf[:, :, t * 128:(t + 1) * 128], tp_ps[:, :, :]), inc=dv)

            p.w("tensor", dv, kh_dv)
            p.w("tensor", dv, vh_dv)
            for hp in range(8):
                hA, hB = 2 * hp, 2 * hp + 1
                ps = kvp[hp % 2]
                if t > 0 or hp >= 2:
                    p.w("tensor", dv, kv_add_dv[(hp - 2) % 8])
                p.i("tensor", lambda tt, hA=hA, rr=r, ps=ps: tt.matmul(
                    ps[0:64, 0:65], kh[:, rr, hA * 64:(hA + 1) * 64], vh[:, rr, hA, :],
                    start=True, stop=True, tile_position=(0, 0)))
                p.i("tensor", lambda tt, hB=hB, rr=r, ps=ps: tt.matmul(
                    ps[64:128, 0:65], kh[:, rr, hB * 64:(hB + 1) * 64], vh[:, rr, hB, :],
                    start=True, stop=True, tile_position=(0, 64)), inc=pe)
                p.w("vector", pe)
                p.i("vector", lambda v, hp=hp, ps=ps: v.tensor_tensor(
                    kv_sb[:, hp, :], kv_sb[:, hp, :], ps[:, 0:65], ALU.add), inc=dv)
                if t == 0 and hp == 0:
                    kv_add_dv = [None] * 8
                kv_add_dv[hp] = p.v(dv)
            kv_last_pe = p.v(pe)
            prev_psum_read_dv = p.v(dv)

        p.w("vector", dv)
        p.i("vector", lambda v: v.tensor_scalar(
            kv_sb[:, :, 64:65], kv_sb[:, :, 64:65], 1e-6, None, ALU.add), inc=dv)
        p.w("vector", dv)
        p.i("vector", lambda v: v.tensor_copy(kv_bf[:, :, :], kv_sb[:, :, :]), inc=dv)
        kvb_dv = p.v(dv)

        p.w("gpsimd", pe, kv_last_pe)
        p.dma("gpsimd", w_sb["wvd"][:, :, :], wd.rearrange("(c p) o -> p c o", p=128), inc=lw)
        lw_wd = p.v(lw)
        p.w("tensor", lw, lw_wd)

        p.w("tensor", dv, kvb_dv)
        p.w("vector", dv, bcast_done_dv)
        st_val = 0
        o_dv_hist = {}
        for t in range(NT):
            r = t % 2
            if t >= 2:
                p.w("sync", dv, o_dv_hist[t - 2])
            lx_val = p.dma("sync", x2_t[:, r, :], x_own[t * 128:(t + 1) * 128, :], inc=lx)

            for hp in range(8):
                psA_ = pkq[:, 0:65] if hp % 2 == 0 else kvp[0][:, 0:65]
                psB_ = pkq[:, 512:512 + 65] if hp % 2 == 0 else kvp[1][:, 0:65]
                if t > 0 or hp > 1:
                    p.w("tensor", dv, ctx_evac[hp % 2])
                p.i("tensor", lambda tt, hp=hp, ps=psA_, t=t: tt.matmul(
                    ps, qf[0:64, hp, t * 128:(t + 1) * 128], kv_bf[0:64, hp, :],
                    start=True, stop=True, tile_position=(0, 0)))
                p.i("tensor", lambda tt, hp=hp, ps=psB_, t=t: tt.matmul(
                    ps, qf[64:128, hp, t * 128:(t + 1) * 128], kv_bf[64:128, hp, :],
                    start=True, stop=True, tile_position=(64, 0)), inc=pe)
                ctx_pe = p.v(pe)
                p.w("vector", pe, ctx_pe)
                for half, ps in ((0, psA_), (1, psB_)):
                    h = 2 * hp + half
                    p.i("vector", lambda v, ps=ps, h=h: v.reciprocal(zden[:, h:h + 1], ps[:, 64:65]), inc=dv)
                    p.w("vector", dv)
                    p.i("vector", lambda v, ps=ps, h=h: v.tensor_scalar(
                        ctm[:, h * 64:(h + 1) * 64], ps[:, 0:64], zden[:, h:h + 1], None,
                        ALU.mult), inc=dv)
                if hp == 0 and t == 0:
                    ctx_evac = [None, None]
                ctx_evac[hp % 2] = p.v(dv)
            ctm_dv = p.v(dv)

            p.w("tensor", dv, ctm_dv)
            for c in range(8):
                p.i("tensor", lambda tt, c=c: tt.transpose(
                    tp_ps[:, c, :], ctm[:, c * 128:(c + 1) * 128], idn_b[:, :]),
                    inc=pe if c == 7 else None)
            p.w("vector", pe)
            p.i("vector", lambda v: v.tensor_copy(cfm[:, :, :], tp_ps[:, :, :]), inc=dv)
            cfm_dv = p.v(dv)

            p.w("tensor", dv, cfm_dv)
            for half in range(2):
                sl = slice(half * 512, (half + 1) * 512)
                for c in range(8):
                    p.i("tensor", lambda tt, c=c, sl=sl: tt.matmul(
                        pv[:, sl], cfm[:, c, :], w_sb["wvd"][:, c, sl], start=(c == 0), stop=False))
                p.i("tensor", lambda tt, sl=sl: tt.matmul(
                    pv[:, sl], ones_b[:, :], brow["bd"][:, sl], start=False, stop=True),
                    inc=pe if half == 1 else None)
            dp_pe = p.v(pe)

            p.w("vector", pe, dp_pe)
            p.i("vector", lambda v, t=t: v.tensor_tensor(hs_sb[:, :], pv[:, :], xn_own[:, t, :], ALU.add), inc=dv)
            p.w("vector", dv)
            _ln_ab(p, hs_sb[:, :], bn_sb, ab_sb, 1e-12, dv, ac)
            p.i("vector", lambda v: v.tensor_scalar(sA[:, :], hs_sb[:, :], ab_sb[:, 0:1], ab_sb[:, 1:2],
                                                    ALU.mult, op1=ALU.add), inc=dv)
            p.w("vector", dv)
            p.i("vector", lambda v: v.tensor_tensor(sB[:, :], sA[:, :], gam_bc[:, :], ALU.mult), inc=dv)
            p.w("vector", dv)
            p.i("vector", lambda v: v.tensor_tensor(u2_sb[:, :], sB[:, :], bet_bc[:, :], ALU.add), inc=dv)
            p.w("vector", dv)
            _ln_ab(p, u2_sb[:, :], bn_sb, ab2_sb, 1e-5, dv, ac)
            p.i("vector", lambda v: v.tensor_scalar(sA[:, :], u2_sb[:, :], ab2_sb[:, 0:1], ab2_sb[:, 1:2],
                                                    ALU.mult, op1=ALU.add), inc=dv)
            p.w("vector", lx, lx_val)
            p.w("vector", dv)
            if t >= 2:
                p.w("vector", st, st_val - 16)
            p.i("vector", lambda v, r=r: v.tensor_tensor(out_t[:, r, :], sA[:, :], x2_t[:, r, :], ALU.add), inc=dv)
            o_dv = p.v(dv)
            o_dv_hist[t] = o_dv
            p.w("sync", dv, o_dv)
            st_val = p.dma("sync", y[t * 128:(t + 1) * 128, :], out_t[:, r, :], inc=st)

        p.w("sync", st, st_val)
        p.build_block()
    return nc


_NC_CACHE = {}


def _get_nc(name, builder):
    if name not in _NC_CACHE:
        _NC_CACHE[name] = builder()
    return _NC_CACHE[name]


def run_l0_attn(x, wq, bq, wk, bk, wv, bv, wd, bd, gam, bet):
    nc = _get_nc("l0a", build_l0_attn)
    idn = np.eye(128, dtype=np.float32)
    com = dict(
        wq=np.ascontiguousarray(wq.T), wk=np.ascontiguousarray(wk.T),
        wv=np.ascontiguousarray(wv.T), wd=np.ascontiguousarray(wd.T),
        bq=bq, bk=bk, bv=bv, bd=bd, gam=gam, bet=bet, idn=idn,
    )
    in_maps = []
    for c in range(NCORES):
        b, h = c // 2, c % 2
        m = dict(com)
        m["x_own"] = np.ascontiguousarray(x[b, h * T:(h + 1) * T])
        m["x_oth"] = np.ascontiguousarray(x[b, (1 - h) * T:(2 - h) * T])
        in_maps.append(m)
    res = run_bass_kernel_spmd(nc, in_maps, core_ids=list(range(NCORES)))
    out = np.empty((B_, S_, H_), np.float32)
    for c in range(NCORES):
        b, h = c // 2, c % 2
        out[b, h * T:(h + 1) * T] = res.results[c]["y"]
    return out


def build_l0_ffn():
    nc = bass.Bass("TRN2", target_bir_lowering=False)
    x_in = nc.dram_tensor("x_in", [T, H_], F32, kind="ExternalInput").ap()
    gam = nc.dram_tensor("gam", [H_], F32, kind="ExternalInput").ap()
    bet = nc.dram_tensor("bet", [H_], F32, kind="ExternalInput").ap()
    wr = nc.dram_tensor("wr", [H_, 8], F32, kind="ExternalInput").ap()
    br = nc.dram_tensor("br", [8], F32, kind="ExternalInput").ap()
    w1 = nc.dram_tensor("w1", [8, H_, 4096], F32, kind="ExternalInput").ap()
    b1 = nc.dram_tensor("b1", [8, 4096], F32, kind="ExternalInput").ap()
    w2 = nc.dram_tensor("w2", [8, 4096, H_], F32, kind="ExternalInput").ap()
    b2 = nc.dram_tensor("b2", [8, H_], F32, kind="ExternalInput").ap()
    idn = nc.dram_tensor("idn", [128, 128], F32, kind="ExternalInput").ap()
    y = nc.dram_tensor("y", [T, H_], F32, kind="ExternalOutput").ap()

    NSL = 4
    FS = 4096 // NSL
    with contextlib.ExitStack() as stack:
        E = stack.enter_context
        x_t = E(nc.sbuf_tensor("x_t", [128, 2, H_], F32))
        ab_all = E(nc.sbuf_tensor("ab_all", [128, NT, 2], F32))
        xn_tm = E(nc.sbuf_tensor("xn_tm", [128, H_], BF16))
        xn_fm = E(nc.sbuf_tensor("xn_fm", [128, 8, T], BF16))
        acc = E(nc.sbuf_tensor("acc", [128, 8, H_], F32))
        w1s = E(nc.sbuf_tensor("w1s", [128, 8, FS], BF16))
        w2s = E(nc.sbuf_tensor("w2s", [128, 8, H_], BF16))
        u_sb = E(nc.sbuf_tensor("u_sb", [128, 8, 512], BF16))
        gates = E(nc.sbuf_tensor("gates", [128, 128], F32))
        s_all = E(nc.sbuf_tensor("s_all", [128, NT], F32))
        gT = E(nc.sbuf_tensor("gT", [9, T], BF16))
        dmat = E(nc.sbuf_tensor("dmat", [9, H_], BF16))
        dcol = E(nc.sbuf_tensor("dcol", [128, 8, 8], F32))
        b1_sb = E(nc.sbuf_tensor("b1_sb", [8, 2048], F32))
        b1_pt = E(nc.sbuf_tensor("b1_pt", [128, 8, 32], F32))
        brelu = E(nc.sbuf_tensor("brelu", [128, 8, 32], BF16))
        b2_sb = E(nc.sbuf_tensor("b2_sb", [8, H_], F32))
        wr_sb = E(nc.sbuf_tensor("wr_sb", [128, 8, 8], BF16))
        br_r = E(nc.sbuf_tensor("br_r", [1, 8], BF16))
        gam_r = E(nc.sbuf_tensor("gam_r", [1, H_], F32))
        bet_r = E(nc.sbuf_tensor("bet_r", [1, H_], F32))
        gam_bc = E(nc.sbuf_tensor("gam_bc", [128, H_], F32))
        bet_bc = E(nc.sbuf_tensor("bet_bc", [128, H_], F32))
        ones_b = E(nc.sbuf_tensor("ones_b", [1, 128], BF16))
        ones_f = E(nc.sbuf_tensor("ones_f", [1, 128], F32))
        ones8 = E(nc.sbuf_tensor("ones8", [8, 1], F32))
        idn_b = E(nc.sbuf_tensor("idn_b", [128, 128], BF16))
        idn_f = E(nc.sbuf_tensor("idn_f", [128, 128], F32))
        sA = E(nc.sbuf_tensor("sA", [128, H_], F32))
        sB = E(nc.sbuf_tensor("sB", [128, H_], F32))
        x2_t = E(nc.sbuf_tensor("x2_t", [128, 2, H_], F32))
        out_t = E(nc.sbuf_tensor("out_t", [128, 2, H_], F32))
        bn_sb = E(nc.sbuf_tensor("bn_sb", [128, 16], F32))
        ab_sb = E(nc.sbuf_tensor("ab_sb", [128, 2], F32))
        rt = E(nc.sbuf_tensor("rt", [128, 40], F32))
        tp_ps = E(nc.psum_tensor("tp_ps", [128, 8, 128], BF16))
        h_ps = [E(nc.psum_tensor(f"h_ps{i}", [128, 512], F32)) for i in range(2)]
        y_ps = [E(nc.psum_tensor(f"y_ps{i}", [128, 512], F32)) for i in range(4)]
        d_ps = E(nc.psum_tensor("d_ps", [128, 512], F32))

        p = Prog(nc, stack)
        ld = p.sem("ld")
        lw = p.sem("lw")
        pe = p.sem("pe")
        dv = p.sem("dv")
        ac = p.sem("ac")
        st = p.sem("st")
        lx = p.sem("lx")

        p.dma("gpsimd", wr_sb[:, :, :], wr.rearrange("(c p) o -> p c o", p=128), inc=lw)
        p.dma("gpsimd", br_r[:, :], br[None, :], inc=lw)
        p.dma("gpsimd", gam_r[:, :], gam[None, :], inc=lw)
        p.dma("gpsimd", bet_r[:, :], bet[None, :], inc=lw)
        p.dma("gpsimd", idn_b[:, :], idn[:, :], inc=lw)
        p.dma("gpsimd", idn_f[:, :], idn[:, :], inc=lw)
        p.dma("gpsimd", b2_sb[:, :], b2[:, :], inc=lw)
        p.i("gpsimd", lambda g: g.memset(ones_b[:, :], 1.0))
        p.i("gpsimd", lambda g: g.memset(ones_f[:, :], 1.0))
        p.i("gpsimd", lambda g: g.memset(ones8[:, :], 1.0))
        p.i("gpsimd", lambda g: g.memset(acc[:, :, :], 0.0), inc=lw, n=1)
        lw0 = p.v(lw)

        p.w("tensor", lw, lw0)
        for nm_r, nm_bc in ((gam_r, gam_bc), (bet_r, bet_bc)):
            for half in range(2):
                sl = slice(half * 512, (half + 1) * 512)
                mm = p.i("tensor", lambda t, sl=sl, r_=nm_r: t.matmul(
                    d_ps[:, 0:512], ones_f[:, :], r_[:, sl], start=True, stop=True), inc=pe)
                p.w("vector", pe, mm)
                cp = p.i("vector", lambda v, sl=sl, bc=nm_bc: v.tensor_copy(bc[:, sl], d_ps[:, 0:512]), inc=dv)
                p.w("tensor", dv, cp)

        for half in range(2):
            bl = p.dma("gpsimd", b1_sb[:, :], b1[:, half * 2048:(half + 1) * 2048], inc=lw)
            p.w("tensor", lw, bl)
            for c in range(16):
                p.i("tensor", lambda t, c=c: t.transpose(
                    d_ps[:, c * 8:(c + 1) * 8], b1_sb[0:8, c * 128:(c + 1) * 128], idn_f[0:8, 0:8]),
                    inc=pe if c == 15 else None)
            p.w("vector", pe)
            for c in range(16):
                cg = half * 16 + c
                p.i("vector", lambda v, c=c, cg=cg: v.tensor_copy(
                    b1_pt[:, :, cg], d_ps[:, c * 8:(c + 1) * 8]), inc=dv if c == 15 else None)
            p.w("gpsimd", dv)
            p.w("tensor", dv)
        p.i("vector", lambda v: v.tensor_scalar(brelu[:, :, :], b1_pt[:, :, :], 0.0, None, ALU.max), inc=dv)
        brelu_dv = p.v(dv)

        b2s_r = sA
        for half in range(2):
            sl = slice(half * 512, (half + 1) * 512)
            mm = p.i("tensor", lambda t, sl=sl: t.matmul(
                d_ps[0:1, 0:512], ones8[:, :], b2_sb[:, sl], start=True, stop=True), inc=pe)
            p.w("vector", pe, mm)
            cp = p.i("vector", lambda v, sl=sl: v.tensor_copy(sA[0:1, sl], d_ps[0:1, 0:512]), inc=dv)
            p.w("tensor", dv, cp)
        p.i("vector", lambda v: v.tensor_copy(xn_tm[0:1, :], sA[0:1, :]), inc=dv)
        p.w("gpsimd", dv)
        b2s_lw = p.dma("gpsimd", dmat[8:9, :], xn_tm[0:1, :], inc=lw)

        xn_read_dv = {}
        for t in range(NT):
            r = t % 2
            if t >= 2:
                p.w("sync", dv, xn_read_dv[t - 2])
            ldv = p.dma("sync", x_t[:, r, :], x_in[t * 128:(t + 1) * 128, :], inc=ld)
            p.w("vector", ld, ldv)
            _ln_ab(p, x_t[:, r, :], bn_sb, ab_sb, 1e-5, dv, ac)
            p.i("vector", lambda v, t=t: v.tensor_copy(ab_all[:, t, :], ab_sb[:, :]))
            p.i("vector", lambda v, rr=r: v.tensor_scalar(
                sB[:, :], x_t[:, rr, :], ab_sb[:, 0:1], ab_sb[:, 1:2], ALU.mult, op1=ALU.add), inc=dv)
            xn_read_dv[t] = p.v(dv)
            p.w("vector", dv)
            _ln_ab(p, sB[:, :], bn_sb, ab_sb, 1e-5, dv, ac)
            p.i("vector", lambda v: v.tensor_scalar(
                sA[:, :], sB[:, :], ab_sb[:, 0:1], ab_sb[:, 1:2], ALU.mult, op1=ALU.add), inc=dv)
            p.w("vector", dv)
            p.i("vector", lambda v: v.tensor_tensor(sB[:, :], sA[:, :], gam_bc[:, :], ALU.mult), inc=dv)
            p.w("vector", dv)
            if t >= 1:
                p.w("vector", pe, xn_tp_pe)
            p.i("vector", lambda v: v.tensor_tensor(xn_tm[:, :], sB[:, :], bet_bc[:, :], ALU.add), inc=dv)
            xn_dv = p.v(dv)
            p.w("tensor", dv, xn_dv)
            if t >= 1:
                p.w("tensor", dv, xnfm_dv)
            for c in range(8):
                p.i("tensor", lambda tt, c=c: tt.transpose(
                    tp_ps[:, c, :], xn_tm[:, c * 128:(c + 1) * 128], idn_b[:, :]),
                    inc=pe if c == 7 else None)
            xn_tp_pe = p.v(pe)
            p.w("vector", pe, xn_tp_pe)
            p.i("vector", lambda v, t=t: v.tensor_copy(
                xn_fm[:, :, t * 128:(t + 1) * 128], tp_ps[:, :, :]), inc=dv)
            xnfm_dv = p.v(dv)

            p.w("tensor", dv, xnfm_dv)
            if t >= 1:
                p.w("tensor", ac, rt_exp_ac)
                p.w("tensor", dv, rt_m1_dv)
            for c in range(8):
                p.i("tensor", lambda tt, c=c, t=t: tt.matmul(
                    d_ps[:, 0:8], xn_fm[:, c, t * 128:(t + 1) * 128], wr_sb[:, c, :],
                    start=(c == 0), stop=False))
            rmm = p.i("tensor", lambda tt: tt.matmul(
                d_ps[:, 0:8], ones_b[:, :], br_r[:, :], start=False, stop=True), inc=pe)
            p.w("vector", pe, rmm)
            rt_m1_dv = p.i("vector", lambda v: v.tensor_reduce(rt[:, 8:9], d_ps[:, 0:8], AX.X, ALU.max), inc=dv)
            p.w("vector", dv)
            p.i("vector", lambda v: v.tensor_scalar(rt[:, 9:10], rt[:, 8:9], -1.0, None, ALU.mult), inc=dv)
            p.w("scalar", dv)
            rt_exp_ac = p.i("scalar", lambda s: s.activation(rt[:, 0:8], d_ps[:, 0:8], AF.Exp,
                                                 bias=rt[:, 9:10], accum_out=rt[:, 10:11]), inc=ac)
            p.w("vector", ac)
            p.i("vector", lambda v: v.reciprocal(rt[:, 11:12], rt[:, 10:11]), inc=dv)
            p.w("vector", dv)
            p.i("vector", lambda v: v.tensor_scalar(rt[:, 0:8], rt[:, 0:8], rt[:, 11:12], None,
                                                    ALU.mult), inc=dv)
            p.w("vector", dv)
            p.i("vector", lambda v: v.tensor_reduce(rt[:, 12:13], rt[:, 0:8], AX.X, ALU.max), inc=dv)
            p.w("vector", dv)
            p.i("vector", lambda v: v.tensor_scalar(rt[:, 16:24], rt[:, 0:8], rt[:, 12:13], None,
                                                    ALU.is_ge), inc=dv)
            p.w("vector", dv)
            p.i("vector", lambda v: v.tensor_reduce(rt[:, 13:14], rt[:, 16:24], AX.X, ALU.add))
            p.i("vector", lambda v: v.tensor_scalar(rt[:, 24:32], rt[:, 16:24], -1e30, None,
                                                    ALU.mult), inc=dv)
            p.w("vector", dv)
            p.i("vector", lambda v: v.tensor_tensor(rt[:, 24:32], rt[:, 0:8], rt[:, 24:32], ALU.add), inc=dv)
            p.w("vector", dv)
            p.i("vector", lambda v: v.tensor_reduce(rt[:, 14:15], rt[:, 24:32], AX.X, ALU.max))
            p.i("vector", lambda v: v.tensor_scalar(rt[:, 15:16], rt[:, 13:14], 2.0, None,
                                                    ALU.is_lt), inc=dv)
            p.w("vector", dv)
            p.i("vector", lambda v: v.tensor_scalar(rt[:, 32:40], rt[:, 0:8], rt[:, 14:15],
                                                    rt[:, 15:16], ALU.is_ge, op1=ALU.mult), inc=dv)
            p.w("vector", dv)
            p.i("vector", lambda v: v.tensor_tensor(rt[:, 32:40], rt[:, 16:24], rt[:, 32:40], ALU.max), inc=dv)
            p.w("vector", dv)
            p.i("vector", lambda v: v.tensor_tensor(rt[:, 32:40], rt[:, 0:8], rt[:, 32:40], ALU.mult), inc=dv)
            p.w("vector", dv)
            p.i("vector", lambda v, t=t: v.tensor_copy(gates[:, t::16], rt[:, 32:40]))
            p.i("vector", lambda v, t=t: v.tensor_reduce(s_all[:, t:t + 1], rt[:, 32:40], AX.X, ALU.add), inc=dv)
            p.w("vector", dv)
            p.w("vector", pe, xn_tp_pe)
            p.i("vector", lambda v, t=t: v.tensor_scalar(
                xn_tm[:, 0:8], rt[:, 32:40], s_all[:, t:t + 1], -1.0, ALU.subtract, op1=ALU.mult), inc=dv)
            p.i("vector", lambda v, t=t: v.tensor_copy(xn_tm[:, 8:9], s_all[:, t:t + 1]), inc=dv)
            gaug_dv = p.v(dv)
            p.w("tensor", dv, gaug_dv)
            gt_mm = p.i("tensor", lambda tt: tt.transpose(
                tp_ps[0:16, 0, :], xn_tm[:, 0:16], idn_b[:, :]), inc=pe)
            xn_tp_pe = gt_mm
            p.w("vector", pe, gt_mm)
            p.i("vector", lambda v, t=t: v.tensor_copy(
                gT[:, t * 128:(t + 1) * 128], tp_ps[0:9, 0, :]), inc=dv)
            gT_dv = p.v(dv)
            p.w("tensor", dv, gT_dv)
            rt_read_dv = p.v(dv)

        pass1_dv = p.v(dv)
        pass1_pe = p.v(pe)
        p.w("scalar", dv, pass1_dv)
        p.w("vector", lw, lw0)
        p.w("tensor", ac, p.v(ac))

        st_val = 0
        o_dv_hist = {}
        w1_lw = w2_lw = None
        dcol_dv = None
        relu_ac = {}
        w1s_free_pe = pass1_pe
        w2s_free_pe = pass1_pe
        yps_free_dv = [pass1_dv] * 4
        dps_free_dv = pass1_dv
        hps_free_ac = [0, 0]
        u_free_pe = pass1_pe
        for Hh in range(2):
            for e in range(8):
                for s in range(NSL):
                    fsl = slice(s * FS, (s + 1) * FS)
                    p.w("gpsimd", pe, w1s_free_pe)
                    w1_lw = p.dma("gpsimd", w1s[:, :, :],
                                  w1[e, :, fsl].rearrange("(c p) f -> p c f", p=128), inc=lw)
                    p.w("gpsimd", pe, w2s_free_pe)
                    w2_lw = p.dma("gpsimd", w2s[:, :, :],
                                  w2[e, fsl, :].rearrange("(c p) o -> p c o", p=128), inc=lw)
                    for g in range(2):
                        gcols = slice(Hh * 1024 + g * 512, Hh * 1024 + (g + 1) * 512)
                        p.w("tensor", lw, w1_lw)
                        p.w("tensor", pe, u_free_pe)
                        for ft in range(FS // 128):
                            hb = ft % 2
                            p.w("tensor", ac, hps_free_ac[hb])
                            for c in range(8):
                                p.i("tensor", lambda tt, c=c, ft=ft, hb=hb, gcols=gcols: tt.matmul(
                                    h_ps[hb][:, 0:512], w1s[:, c, ft * 128:(ft + 1) * 128],
                                    xn_fm[:, c, gcols], start=(c == 0), stop=(c == 7)),
                                    inc=pe if c == 7 else None)
                            hmm = p.v(pe)
                            p.w("scalar", pe, hmm)
                            cg = s * (FS // 128) + ft
                            hps_free_ac[hb] = p.i("scalar", lambda sc, ft=ft, hb=hb, e=e, cg=cg: sc.activation(
                                u_sb[:, ft, :], h_ps[hb][:, 0:512], AF.Relu,
                                bias=b1_pt[:, e, cg:cg + 1]), inc=ac)
                        relu_done = p.v(ac)
                        if g == 1:
                            w1s_free_pe = p.v(pe)
                        p.w("tensor", ac, relu_done)
                        p.w("tensor", lw, w2_lw)
                        for half in range(2):
                            osl = slice(half * 512, (half + 1) * 512)
                            for tt_ in range(4):
                                tile = Hh * 8 + g * 4 + tt_
                                p.w("tensor", dv, yps_free_dv[tt_])
                                for fc in range(FS // 128):
                                    p.i("tensor", lambda tt, fc=fc, tt_=tt_, osl=osl: tt.matmul(
                                        y_ps[tt_][:, 0:512], u_sb[:, fc, tt_ * 128:(tt_ + 1) * 128],
                                        w2s[:, fc, osl], start=(fc == 0), stop=(fc == FS // 128 - 1)),
                                        inc=pe if fc == FS // 128 - 1 else None)
                                ymm = p.v(pe)
                                p.w("vector", pe, ymm)
                                gcol = gates[:, e * 16 + tile:e * 16 + tile + 1]
                                p.i("vector", lambda v, tt_=tt_, gcol=gcol: v.tensor_scalar(
                                    sB[:, 0:512], y_ps[tt_][:, 0:512], gcol, None, ALU.mult), inc=dv)
                                p.w("vector", dv)
                                p.i("vector", lambda v, tile=tile, osl=osl: v.tensor_tensor(
                                    acc[:, tile % 8, osl], acc[:, tile % 8, osl], sB[:, 0:512],
                                    ALU.add), inc=dv)
                                yps_free_dv[tt_] = p.v(dv)
                        u_free_pe = p.v(pe)
                    if Hh == 0:
                        for ot in range(8):
                            p.w("tensor", dv, dps_free_dv)
                            for fc in range(FS // 128):
                                p.i("tensor", lambda tt, fc=fc, ot=ot, e=e, s=s: tt.matmul(
                                    d_ps[:, 0:1], w2s[:, fc, ot * 128:(ot + 1) * 128],
                                    brelu[:, e, s * (FS // 128) + fc:s * (FS // 128) + fc + 1],
                                    start=(fc == 0), stop=(fc == FS // 128 - 1)),
                                    inc=pe if fc == FS // 128 - 1 else None)
                            dmm = p.v(pe)
                            p.w("vector", pe, dmm)
                            if s == 0:
                                p.i("vector", lambda v, ot=ot, e=e: v.tensor_copy(
                                    dcol[:, ot, e:e + 1], d_ps[:, 0:1]), inc=dv)
                            else:
                                p.i("vector", lambda v, ot=ot, e=e: v.tensor_tensor(
                                    dcol[:, ot, e:e + 1], dcol[:, ot, e:e + 1], d_ps[:, 0:1],
                                    ALU.add), inc=dv)
                            dps_free_dv = p.v(dv)
                    w2s_free_pe = p.v(pe)

            if Hh == 0:
                p.w("tensor", dv, p.v(dv))
                for ot in range(8):
                    dt_mm = p.i("tensor", lambda tt, ot=ot: tt.transpose(
                        d_ps[0:8, 0:128], dcol[:, ot, :], idn_f[:, :]), inc=pe)
                    p.w("vector", pe, dt_mm)
                    p.i("vector", lambda v, ot=ot: v.tensor_copy(
                        dmat[0:8, ot * 128:(ot + 1) * 128], d_ps[0:8, 0:128]), inc=dv)
                    p.w("tensor", dv)
                dmat_dv = p.v(dv)

            p.w("tensor", lw, b2s_lw)
            for ti in range(8):
                tile = Hh * 8 + ti
                r = ti % 2
                if ti >= 2:
                    p.w("sync", dv, o_dv_hist[tile - 2])
                lxv = p.dma("sync", x2_t[:, r, :], x_in[tile * 128:(tile + 1) * 128, :], inc=lx)
                if ti == 0:
                    p.w("tensor", dv, dmat_dv)
                p.w("tensor", dv, yps_free_dv[0])
                p.w("tensor", dv, yps_free_dv[1])
                for half in range(2):
                    osl = slice(half * 512, (half + 1) * 512)
                    p.i("tensor", lambda tt, half=half, tile=tile, osl=osl: tt.matmul(
                        y_ps[half][:, 0:512], gT[:, tile * 128:(tile + 1) * 128], dmat[:, osl],
                        start=True, stop=True), inc=pe if half == 1 else None)
                cmm = p.v(pe)
                p.w("vector", pe, cmm)
                p.w("vector", lx, lxv)
                p.i("vector", lambda v, r=r, tile=tile: v.tensor_scalar(
                    sA[:, :], x2_t[:, r, :], ab_all[:, tile, 0:1], ab_all[:, tile, 1:2],
                    ALU.mult, op1=ALU.add), inc=dv)
                p.w("vector", dv)
                for half in range(2):
                    osl = slice(half * 512, (half + 1) * 512)
                    p.i("vector", lambda v, half=half, osl=osl, ti=ti: v.tensor_tensor(
                        sB[:, osl], acc[:, ti, osl], y_ps[half][:, 0:512], ALU.add),
                        inc=dv if half == 1 else None)
                yps_free_dv[0] = yps_free_dv[1] = p.v(dv)
                p.w("vector", dv)
                p.i("vector", lambda v: v.tensor_tensor(sB[:, :], sA[:, :], sB[:, :], ALU.add), inc=dv)
                p.w("vector", dv)
                _ln_ab(p, sB[:, :], bn_sb, ab_sb, 1e-5, dv, ac)
                p.i("vector", lambda v: v.tensor_scalar(
                    sA[:, :], sB[:, :], ab_sb[:, 0:1], ab_sb[:, 1:2], ALU.mult, op1=ALU.add), inc=dv)
                p.w("vector", dv)
                if tile >= 2:
                    p.w("vector", st, st_val - 16)
                p.i("vector", lambda v, r=r: v.tensor_tensor(
                    out_t[:, r, :], sA[:, :], x2_t[:, r, :], ALU.add), inc=dv)
                o_dv = p.v(dv)
                o_dv_hist[tile] = o_dv
                p.w("sync", dv, o_dv)
                st_val = p.dma("sync", y[tile * 128:(tile + 1) * 128, :], out_t[:, r, :], inc=st)
            if Hh == 0:
                p.w("gpsimd", dv, p.v(dv))
                p.i("gpsimd", lambda g: g.memset(acc[:, :, :], 0.0), inc=lw, n=1)
                acc_zero_lw = p.v(lw)
                p.w("vector", lw, acc_zero_lw)

        p.w("sync", st, st_val)
        p.build_block()
    return nc


def run_l0_ffn(x, gam, bet, wr, br, w1, b1, w2, b2):
    nc = _get_nc("l0f", build_l0_ffn)
    idn = np.eye(128, dtype=np.float32)
    com = dict(gam=gam, bet=bet, wr=np.ascontiguousarray(wr), br=br,
               w1=np.ascontiguousarray(w1), b1=b1, w2=np.ascontiguousarray(w2), b2=b2, idn=idn)
    in_maps = []
    for c in range(NCORES):
        b_, h = c // 2, c % 2
        m = dict(com)
        m["x_in"] = np.ascontiguousarray(x[b_, h * T:(h + 1) * T])
        in_maps.append(m)
    res = run_bass_kernel_spmd(nc, in_maps, core_ids=list(range(NCORES)))
    out = np.empty((B_, S_, H_), np.float32)
    for c in range(NCORES):
        b_, h = c // 2, c % 2
        out[b_, h * T:(h + 1) * T] = res.results[c]["y"]
    return out

def build_l1_attn():
    nc = bass.Bass("TRN2", target_bir_lowering=False)
    x_own = nc.dram_tensor("x_own", [T, H_], F32, kind="ExternalInput").ap()
    x_oth = nc.dram_tensor("x_oth", [T, H_], F32, kind="ExternalInput").ap()
    wqkv = nc.dram_tensor("wqkv", [H_, 3 * H_], F32, kind="ExternalInput").ap()
    bqkv = nc.dram_tensor("bqkv", [3 * H_], F32, kind="ExternalInput").ap()
    qn = nc.dram_tensor("qn", [HD], F32, kind="ExternalInput").ap()
    kn = nc.dram_tensor("kn", [HD], F32, kind="ExternalInput").ap()
    wpr = nc.dram_tensor("wpr", [H_, H_], F32, kind="ExternalInput").ap()
    bpr = nc.dram_tensor("bpr", [H_], F32, kind="ExternalInput").ap()
    idn = nc.dram_tensor("idn", [128, 128], F32, kind="ExternalInput").ap()
    y = nc.dram_tensor("y", [T, H_], F32, kind="ExternalOutput").ap()
    kscr = nc.dram_tensor("kscr", [8, 128, S_], BF16, kind="Internal").ap()

    NQB = 4
    with contextlib.ExitStack() as stack:
        E = stack.enter_context
        vh = E(nc.sbuf_tensor("vh", [128, 32, NH, 65], BF16))
        qf = E(nc.sbuf_tensor("qf", [128, 8, T], BF16))
        kf = E(nc.sbuf_tensor("kf", [128, 1, S_], BF16))
        cmax = E(nc.sbuf_tensor("cmax", [128, NT, NH], F32))
        cbias = E(nc.sbuf_tensor("cbias", [128, 32], F32))
        cb_r = E(nc.sbuf_tensor("cb_r", [1, 32], F32))
        idn_b = E(nc.sbuf_tensor("idn_b", [128, 128], BF16))
        ones_f = E(nc.sbuf_tensor("ones_f", [1, 128], F32))
        ones_b = E(nc.sbuf_tensor("ones_b", [1, 128], BF16))
        w2col = E(nc.sbuf_tensor("w2col", [128, 1], F32))
        w2sq = E(nc.sbuf_tensor("w2sq", [128, H_], F32))
        qnr = E(nc.sbuf_tensor("qnr", [1, 2 * HD], F32))
        knr = E(nc.sbuf_tensor("knr", [1, 2 * HD], F32))
        wsh = E(nc.sbuf_tensor("wsh", [128, 8, 2048], BF16))
        bq_r = E(nc.sbuf_tensor("bq_r", [1, 3 * H_], BF16))
        bpr_r = E(nc.sbuf_tensor("bpr_r", [1, H_], BF16))
        x_t = E(nc.sbuf_tensor("x_t", [128, 1, H_], F32))
        xn_tm = E(nc.sbuf_tensor("xn_tm", [128, H_], BF16))
        xn_fm = E(nc.sbuf_tensor("xn_fm", [128, 8, 128], BF16))
        sA = E(nc.sbuf_tensor("sA", [128, H_], F32))
        sB = E(nc.sbuf_tensor("sB", [128, H_], F32))
        ktm = E(nc.sbuf_tensor("ktm", [128, H_], BF16))
        kfm_t = E(nc.sbuf_tensor("kfm_t", [128, 8, 128], BF16))
        rq = E(nc.sbuf_tensor("rq", [128, NH], F32))
        rsc = E(nc.sbuf_tensor("rsc", [128, NH * 2], F32))
        bn_sb = E(nc.sbuf_tensor("bn_sb", [128, 16], F32))
        ab_sb = E(nc.sbuf_tensor("ab_sb", [128, 2], F32))
        u_sb = E(nc.sbuf_tensor("u_sb", [128, 2, H_], BF16))
        ahm = [E(nc.sbuf_tensor(f"ahm{h}", [65, 512], BF16)) for h in range(NH)]
        at_aug = E(nc.sbuf_tensor("at_aug", [128, NH, 65], BF16))
        at2 = E(nc.sbuf_tensor("at2", [128, H_], BF16))
        a_fm = E(nc.sbuf_tensor("a_fm", [128, 8, 128], BF16))
        den_r = E(nc.sbuf_tensor("den_r", [128, NH], F32))
        x2_t = E(nc.sbuf_tensor("x2_t", [128, 1, H_], F32))
        out_t = E(nc.sbuf_tensor("out_t", [128, 1, H_], F32))
        sc_ps = [E(nc.psum_tensor(f"sc_ps{i}", [128, 1024], F32)) for i in range(2)]
        o_psA = E(nc.psum_tensor("o_psA", [65, 512], F32))
        o_psB = E(nc.psum_tensor("o_psB", [65, 512], F32))
        tp_ps = E(nc.psum_tensor("tp_ps", [128, 8, 128], BF16))
        tp2_ps = E(nc.psum_tensor("tp2_ps", [128, 8, 66], BF16))

        p = Prog(nc, stack)
        ld = p.sem("ld")
        lw = p.sem("lw")
        pe = p.sem("pe")
        dv = p.sem("dv")
        ac = p.sem("ac")
        st = p.sem("st")
        lx = p.sem("lx")
        po = p.sem("po")
        ks = p.sem("ks")

        p.dma("gpsimd", idn_b[:, :], idn[:, :], inc=lw)
        p.dma("gpsimd", qnr[:, 0:HD], qn[None, :], inc=lw)
        p.dma("gpsimd", qnr[:, HD:2 * HD], qn[None, :], inc=lw)
        p.dma("gpsimd", knr[:, 0:HD], kn[None, :], inc=lw)
        p.dma("gpsimd", knr[:, HD:2 * HD], kn[None, :], inc=lw)
        p.dma("gpsimd", bq_r[:, :], bqkv[None, :], inc=lw)
        p.i("gpsimd", lambda g: g.memset(ones_f[:, :], 1.0))
        p.i("gpsimd", lambda g: g.memset(ones_b[:, :], 1.0))
        p.i("gpsimd", lambda g: g.memset(vh[:, :, :, :], 1.0), inc=lw, n=1)
        lw0 = p.v(lw)
        p.w("vector", lw, lw0)
        p.i("vector", lambda v: v.tensor_tensor(qnr[:, :], qnr[:, :], knr[:, :], ALU.mult), inc=dv)
        p.w("vector", dv)
        p.i("vector", lambda v: v.tensor_scalar(qnr[:, :], qnr[:, :], 0.125, None, ALU.mult), inc=dv)
        w2r_dv = p.v(dv)
        p.w("tensor", dv, w2r_dv)
        mm = p.i("tensor", lambda t: t.transpose(sc_ps[0][:, 0:1], qnr[:, :], ones_f[0:1, 0:1]), inc=pe)
        p.w("vector", pe, mm)
        p.i("vector", lambda v: v.tensor_copy(w2col[:, :], sc_ps[0][:, 0:1]), inc=dv)
        p.w("vector", dv)
        p.i("vector", lambda v: v.tensor_tensor(qnr[:, :], qnr[:, :], qnr[:, :], ALU.mult), inc=dv)
        p.w("tensor", dv)
        for c in range(8):
            p.i("tensor", lambda t, c=c: t.matmul(sc_ps[0][:, c * 128:(c + 1) * 128],
                                                  ones_f[:, :], qnr[:, :],
                                                  start=True, stop=True), inc=pe if c == 7 else None)
        p.w("vector", pe)
        p.i("vector", lambda v: v.tensor_copy(w2sq[:, :], sc_ps[0][:, 0:1024]), inc=dv)
        w2sq_dv = p.v(dv)
        p.w("tensor", dv, w2sq_dv)

        wl = p.dma("gpsimd", wsh[:, :, :], wqkv[:, H_:3 * H_].rearrange("(c p) o -> p c o", p=128), inc=lw)
        p.w("tensor", lw, wl)
        xn_read_dv = {}
        kv_pe = 0
        ksp_st = {}
        xnfm_dv = tp_pe = 0
        prev_ps_read = 0
        for t in range(32):
            own = t < NT
            r = t % 2
            src = x_own if own else x_oth
            row0 = (t % NT) * 128
            if t >= 1:
                p.w("sync", dv, xn_read_dv[t - 1])
            ldv = p.dma("sync", x_t[:, 0, :], src[row0:row0 + 128, :], inc=ld)
            p.w("vector", ld, ldv)
            _ln_ab(p, x_t[:, 0, :], bn_sb, ab_sb, 1e-5, dv, ac)
            if t >= 1:
                p.w("vector", pe, tp_pe)
            p.i("vector", lambda v, rr=r: v.tensor_scalar(
                xn_tm[:, :], x_t[:, 0, :], ab_sb[:, 0:1], ab_sb[:, 1:2], ALU.mult, op1=ALU.add), inc=dv)
            xn_dv = p.v(dv)
            xn_read_dv[t] = xn_dv
            p.w("tensor", dv, xn_dv)
            for c in range(8):
                p.i("tensor", lambda tt, c=c: tt.transpose(
                    tp_ps[:, c, :], xn_tm[:, c * 128:(c + 1) * 128], idn_b[:, :]),
                    inc=pe if c == 7 else None)
            tp_pe = p.v(pe)
            p.w("vector", pe, tp_pe)
            p.i("vector", lambda v: v.tensor_copy(xn_fm[:, :, :], tp_ps[:, :, :]), inc=dv)
            xnfm_dv = p.v(dv)
            p.w("tensor", dv, xnfm_dv)
            p.w("tensor", dv, prev_ps_read)
            for half, ps in ((0, sc_ps[0]), (1, sc_ps[1])):
                for nh in range(2):
                    sl = slice(half * 1024 + nh * 512, half * 1024 + (nh + 1) * 512)
                    osl = slice(nh * 512, (nh + 1) * 512)
                    for c in range(8):
                        p.i("tensor", lambda tt, c=c, sl=sl, osl=osl, ps=ps: tt.matmul(
                            ps[:, osl], xn_fm[:, c, :], wsh[:, c, sl], start=(c == 0), stop=False))
                    p.i("tensor", lambda tt, sl=sl, osl=osl, ps=ps: tt.matmul(
                        ps[:, osl], ones_b[:, :], bq_r[:, H_ + sl.start:H_ + sl.stop],
                        start=False, stop=True), inc=pe)
            proj_pe = p.v(pe)
            p.w("scalar", pe, proj_pe)
            p.i("scalar", lambda s: s.activation(sA[:, :], sc_ps[0][:, :], AF.Square), inc=ac)
            p.w("vector", ac)
            p.i("vector", lambda v: v.tensor_reduce(
                rsc[:, 0:NH], sA.rearrange("p (h d) -> p h d", h=NH), AX.X, ALU.add), inc=dv)
            p.w("vector", dv)
            p.i("vector", lambda v: v.tensor_scalar(rsc[:, NH:2 * NH], rsc[:, 0:NH], 1.0 / HD, 1e-6,
                                                    ALU.mult, op1=ALU.add), inc=dv)
            p.w("scalar", dv)
            p.i("scalar", lambda s: s.activation(rsc[:, NH:2 * NH], rsc[:, NH:2 * NH], AF.Ln), inc=ac)
            p.w("scalar", ac)
            p.i("scalar", lambda s: s.activation(rq[:, :], rsc[:, NH:2 * NH], AF.Exp,
                                                 scale=-0.5), inc=ac)
            p.w("vector", ac)
            if t >= 1:
                p.w("vector", pe, ktp_pe)
            for h in range(NH):
                p.i("vector", lambda v, h=h: v.tensor_scalar(
                    ktm[:, h * HD:(h + 1) * HD], sc_ps[0][:, h * HD:(h + 1) * HD],
                    rq[:, h:h + 1], None, ALU.mult), inc=dv if h == NH - 1 else None)
            ktm_dv = p.v(dv)
            p.w("tensor", dv, ktm_dv)
            for c in range(8):
                p.i("tensor", lambda tt, c=c: tt.transpose(
                    tp_ps[:, c, :], ktm[:, c * 128:(c + 1) * 128], idn_b[:, :]),
                    inc=pe if c == 7 else None)
            ktp_pe = p.v(pe)
            p.w("vector", pe, ktp_pe)
            if t >= 2:
                p.w("vector", ks, ksp_st[t - 2])
            p.i("vector", lambda v: v.tensor_copy(kfm_t[:, :, :], tp_ps[:, :, :]), inc=dv)
            kfm_dv = p.v(dv)
            p.w("sync", dv, kfm_dv)
            ksp_st[t] = p.dma("sync",
                              kscr[:, :, t * 128:(t + 1) * 128].rearrange("h p c -> p h c"),
                              kfm_t[:, :, :], inc=ks)
            p.i("vector", lambda v, t=t: v.tensor_copy(
                vh[:, t, :, 0:64], sc_ps[1].rearrange("p (h d) -> p h d", h=NH)), inc=dv)
            prev_ps_read = p.v(dv)

        p.w("gpsimd", pe, p.v(pe))
        p.w("gpsimd", dv, p.v(dv))
        wl = p.dma("gpsimd", wsh[:, :, 0:1024], wqkv[:, 0:H_].rearrange("(c p) o -> p c o", p=128), inc=lw)
        p.dma("gpsimd", wsh[:, :, 1024:2048], wpr.rearrange("(c p) o -> p c o", p=128), inc=lw)
        wl = p.dma("gpsimd", bpr_r[:, :], bpr[None, :], inc=lw)
        p.w("tensor", lw, wl)
        qtp_pe = 0
        for t in range(NT):
            r = t % 2
            p.w("sync", dv, xn_read_dv[31 if t == 0 else 32 + t - 1])
            ldv = p.dma("sync", x_t[:, 0, :], x_own[t * 128:(t + 1) * 128, :], inc=ld)
            p.w("vector", ld, ldv)
            _ln_ab(p, x_t[:, 0, :], bn_sb, ab_sb, 1e-5, dv, ac)
            p.w("vector", pe, tp_pe)
            p.i("vector", lambda v, rr=r: v.tensor_scalar(
                xn_tm[:, :], x_t[:, 0, :], ab_sb[:, 0:1], ab_sb[:, 1:2], ALU.mult, op1=ALU.add), inc=dv)
            xn_dv = p.v(dv)
            xn_read_dv[32 + t] = xn_dv
            p.w("tensor", dv, xn_dv)
            for c in range(8):
                p.i("tensor", lambda tt, c=c: tt.transpose(
                    tp_ps[:, c, :], xn_tm[:, c * 128:(c + 1) * 128], idn_b[:, :]),
                    inc=pe if c == 7 else None)
            tp_pe = p.v(pe)
            p.w("vector", pe, tp_pe)
            p.i("vector", lambda v: v.tensor_copy(xn_fm[:, :, :], tp_ps[:, :, :]), inc=dv)
            xnfm_dv = p.v(dv)
            p.w("tensor", dv, xnfm_dv)
            p.w("tensor", dv, prev_ps_read)
            for nh in range(2):
                osl = slice(nh * 512, (nh + 1) * 512)
                for c in range(8):
                    p.i("tensor", lambda tt, c=c, osl=osl: tt.matmul(
                        sc_ps[0][:, osl], xn_fm[:, c, :], wsh[:, c, osl], start=(c == 0), stop=False))
                p.i("tensor", lambda tt, osl=osl: tt.matmul(
                    sc_ps[0][:, osl], ones_b[:, :], bq_r[:, osl], start=False, stop=True), inc=pe)
            proj_pe = p.v(pe)
            p.w("scalar", pe, proj_pe)
            p.i("scalar", lambda s: s.activation(sA[:, :], sc_ps[0][:, :], AF.Square), inc=ac)
            p.w("vector", ac)
            p.i("vector", lambda v: v.tensor_reduce(
                rsc[:, 0:NH], sA.rearrange("p (h d) -> p h d", h=NH), AX.X, ALU.add), inc=dv)
            p.w("vector", dv)
            p.i("vector", lambda v: v.tensor_scalar(rsc[:, NH:2 * NH], rsc[:, 0:NH], 1.0 / HD, 1e-6,
                                                    ALU.mult, op1=ALU.add), inc=dv)
            p.w("scalar", dv)
            p.i("scalar", lambda s: s.activation(rsc[:, NH:2 * NH], rsc[:, NH:2 * NH], AF.Ln), inc=ac)
            p.w("scalar", ac)
            p.i("scalar", lambda s: s.activation(rq[:, :], rsc[:, NH:2 * NH], AF.Exp,
                                                 scale=-0.5), inc=ac)
            p.w("vector", ac)
            for h in range(NH):
                p.i("vector", lambda v, h=h: v.tensor_scalar(
                    sB[:, h * HD:(h + 1) * HD], sc_ps[0][:, h * HD:(h + 1) * HD],
                    rq[:, h:h + 1], None, ALU.mult), inc=dv if h == NH - 1 else None)
            qh_dv = p.v(dv)
            p.w("scalar", dv, qh_dv)
            p.i("scalar", lambda s: s.activation(sA[:, :], sB[:, :], AF.Square), inc=ac)
            p.w("vector", ac)
            p.i("vector", lambda v: v.tensor_tensor(sA[:, :], sA[:, :], w2sq[:, :], ALU.mult), inc=dv)
            p.w("vector", dv)
            p.i("vector", lambda v, t=t: v.tensor_reduce(
                cmax[:, t, :], sA.rearrange("p (h d) -> p h d", h=NH), AX.X, ALU.add), inc=dv)
            p.w("scalar", dv)
            p.i("scalar", lambda s, t=t: s.activation(cmax[:, t, :], cmax[:, t, :], AF.Ln,
                                                      scale=float(HD)), inc=ac)
            p.w("scalar", ac)
            p.i("scalar", lambda s, t=t: s.activation(cmax[:, t, :], cmax[:, t, :], AF.Exp,
                                                      scale=0.5), inc=ac)
            cm_ac = p.v(ac)
            p.w("vector", pe, qtp_pe)
            p.i("vector", lambda v: v.tensor_copy(ktm[:, :], sB[:, :]), inc=dv)
            q_dv = p.v(dv)
            p.w("tensor", dv, q_dv)
            for hp in range(8):
                p.i("tensor", lambda tt, hp=hp: tt.transpose(
                    tp_ps[:, hp, :], ktm[:, hp * 128:(hp + 1) * 128], idn_b[:, :]),
                    inc=pe if hp == 7 else None)
            qtp_pe = p.v(pe)
            p.w("vector", pe, qtp_pe)
            p.i("vector", lambda v, t=t: v.tensor_scalar(
                qf[:, :, t * 128:(t + 1) * 128], tp_ps[:, :, :], w2col[:, :], None, ALU.mult), inc=dv)
            prev_ps_read = p.v(dv)
        pass1_dv = p.v(dv)
        pass1_ac = p.v(ac)

        p.w("gpsimd", ac, pass1_ac)
        p.w("gpsimd", dv, pass1_dv)
        for hp in range(8):
            for qb in range(NQB):
                p.i("gpsimd", lambda g, hp=hp, qb=qb: g.tensor_reduce(
                    cb_r[:, hp * 4 + qb:hp * 4 + qb + 1],
                    cmax[:, qb * 4:(qb + 1) * 4, 2 * hp:2 * hp + 2], AX.XYZWC, ALU.max),
                    inc=po if (hp == 7 and qb == NQB - 1) else None)
        p.w("vector", po)
        p.i("vector", lambda v: v.tensor_scalar(cb_r[:, :], cb_r[:, :], -1.0, None, ALU.mult), inc=dv)
        p.w("tensor", dv)
        cbm = p.i("tensor", lambda t: t.matmul(sc_ps[0][:, 0:32], ones_f[:, :], cb_r[:, :],
                                               start=True, stop=True), inc=pe)
        p.w("vector", pe, cbm)
        p.i("vector", lambda v: v.tensor_copy(cbias[:, :], sc_ps[0][:, 0:32]), inc=dv)
        cb_dv = p.v(dv)
        p.w("tensor", dv, cb_dv)

        p.w("sync", ks, p.v(ks))
        st_val = 0
        o_dv_hist = {}
        prj_evac_dv = None
        u_free_pe = [0, 0]
        exp_ac = [0, 0]
        sc_free = [0, 0]
        o_read_dv = 0
        ahm_read_pe = 0
        kl_val = {}
        prev_hp_parity_exp = {}
        for qb in range(NQB):
            for hp in range(8):
                kr = 0
                if qb * 8 + hp >= 1:
                    p.w("sync", pe, kf_read_pe[kr])
                kl_val[(qb, hp)] = p.dma("sync", kf[:, kr, :], kscr[hp, :, :], inc=ks)
                p.w("tensor", ks, kl_val[(qb, hp)])
                if qb == 0 and hp == 0:
                    kf_read_pe = [0, 0]
                for kc in range(32):
                    sb = kc % 2
                    p.w("tensor", ac, sc_free[sb])
                    if kc == 0 and hp == 0 and prj_evac_dv is not None:
                        p.w("tensor", dv, prj_evac_dv)
                    p.i("tensor", lambda tt, hp=hp, kc=kc, qb=qb, kr=kr, sb=sb: tt.matmul(
                        sc_ps[sb][:, 0:512], kf[0:64, kr, kc * 128:(kc + 1) * 128],
                        qf[0:64, hp, qb * 512:(qb + 1) * 512],
                        start=True, stop=True, tile_position=(0, 0)))
                    smm = p.i("tensor", lambda tt, hp=hp, kc=kc, qb=qb, kr=kr, sb=sb: tt.matmul(
                        sc_ps[sb][:, 512:1024], kf[64:128, kr, kc * 128:(kc + 1) * 128],
                        qf[64:128, hp, qb * 512:(qb + 1) * 512],
                        start=True, stop=True, tile_position=(64, 0)), inc=pe)
                    p.w("scalar", pe, smm)
                    p.w("scalar", pe, u_free_pe[sb])
                    sc_free[sb] = p.i("scalar", lambda s, sb=sb, hp=hp, qb=qb: s.activation(
                        u_sb[:, sb, :], sc_ps[sb][:, :], AF.Exp,
                        bias=cbias[:, hp * 4 + qb:hp * 4 + qb + 1]), inc=ac)
                    p.w("tensor", ac, sc_free[sb])
                    if kc == 0:
                        p.w("tensor", dv, o_read_dv)
                    p.i("tensor", lambda tt, hp=hp, kc=kc, sb=sb: tt.matmul(
                        o_psA[:, :], vh[:, kc, 2 * hp, :], u_sb[:, sb, 0:512],
                        start=(kc == 0), stop=(kc == 31)))
                    mmb = p.i("tensor", lambda tt, hp=hp, kc=kc, sb=sb: tt.matmul(
                        o_psB[:, :], vh[:, kc, 2 * hp + 1, :], u_sb[:, sb, 512:1024],
                        start=(kc == 0), stop=(kc == 31)), inc=pe)
                    u_free_pe[sb] = mmb
                kf_read_pe[kr] = p.v(pe)
                p.w("vector", pe, p.v(pe))
                p.w("vector", pe, ahm_read_pe)
                p.i("vector", lambda v, hp=hp: v.tensor_copy(ahm[2 * hp][:, :], o_psA[:, :]))
                p.i("vector", lambda v, hp=hp: v.tensor_copy(ahm[2 * hp + 1][:, :], o_psB[:, :]), inc=dv)
                o_read_dv = p.v(dv)

            for ti in range(4):
                tile = qb * 4 + ti
                r = ti % 2
                if tile >= 1:
                    p.w("sync", dv, o_dv_hist[tile - 1])
                lxv = p.dma("sync", x2_t[:, 0, :], x_own[tile * 128:(tile + 1) * 128, :], inc=lx)
                p.w("tensor", dv, o_read_dv)
                if tile >= 1:
                    p.w("tensor", dv, aug_read_dv)
                for hh in range(2):
                    for h8 in range(8):
                        h = hh * 8 + h8
                        p.i("tensor", lambda tt, h=h, h8=h8, ti=ti: tt.transpose(
                            tp2_ps[:, h8, 0:65], ahm[h][:, ti * 128:(ti + 1) * 128], idn_b[0:65, 0:65]),
                            inc=pe if h8 == 7 else None)
                    hv = p.v(pe)
                    p.w("vector", pe, hv)
                    p.i("vector", lambda v, hh=hh: v.tensor_copy(
                        at_aug[:, hh * 8:(hh + 1) * 8, :], tp2_ps[:, :, 0:65]), inc=dv)
                    tpd = p.v(dv)
                    p.w("tensor", dv, tpd)
                ahm_read_pe = p.v(pe)
                p.w("vector", dv)
                for h in range(NH):
                    p.i("vector", lambda v, h=h: v.reciprocal(den_r[:, h:h + 1], at_aug[:, h, 64:65]),
                        inc=dv if h == NH - 1 else None)
                p.w("vector", dv)
                for h in range(NH):
                    p.i("vector", lambda v, h=h: v.tensor_scalar(
                        at2[:, h * 64:(h + 1) * 64], at_aug[:, h, 0:64], den_r[:, h:h + 1],
                        None, ALU.mult), inc=dv if h == NH - 1 else None)
                at2_dv = p.v(dv)
                aug_read_dv = at2_dv
                p.w("tensor", dv, at2_dv)
                for c in range(8):
                    p.i("tensor", lambda tt, c=c: tt.transpose(
                        tp_ps[:, c, :], at2[:, c * 128:(c + 1) * 128], idn_b[:, :]),
                        inc=pe if c == 7 else None)
                atp = p.v(pe)
                p.w("vector", pe, atp)
                p.i("vector", lambda v: v.tensor_copy(a_fm[:, :, :], tp_ps[:, :, :]), inc=dv)
                afm_dv = p.v(dv)
                p.w("tensor", dv, afm_dv)
                if prj_evac_dv is not None:
                    p.w("tensor", dv, prj_evac_dv)
                for nh in range(2):
                    osl = slice(nh * 512, (nh + 1) * 512)
                    for c in range(8):
                        p.i("tensor", lambda tt, c=c, osl=osl: tt.matmul(
                            sc_ps[0][:, osl], a_fm[:, c, :], wsh[:, c, 1024 + osl.start:1024 + osl.stop],
                            start=(c == 0), stop=False))
                    p.i("tensor", lambda tt, osl=osl: tt.matmul(
                        sc_ps[0][:, osl], ones_b[:, :], bpr_r[:, osl], start=False, stop=True),
                        inc=pe)
                prj = p.v(pe)
                p.w("vector", pe, prj)
                p.i("vector", lambda v: v.tensor_copy(sB[:, :], sc_ps[0][:, :]), inc=dv)
                prj_evac_dv = p.v(dv)
                p.w("vector", dv)
                _ln_ab(p, sB[:, :], bn_sb, ab_sb, 1e-5, dv, ac)
                p.i("vector", lambda v: v.tensor_scalar(
                    sA[:, :], sB[:, :], ab_sb[:, 0:1], ab_sb[:, 1:2], ALU.mult, op1=ALU.add), inc=dv)
                p.w("vector", lx, lxv)
                p.w("vector", dv)
                if tile >= 1:
                    p.w("vector", st, st_val)
                p.i("vector", lambda v, r=r: v.tensor_tensor(
                    out_t[:, 0, :], sA[:, :], x2_t[:, 0, :], ALU.add), inc=dv)
                o_dv = p.v(dv)
                o_dv_hist[tile] = o_dv
                p.w("sync", dv, o_dv)
                st_val = p.dma("sync", y[tile * 128:(tile + 1) * 128, :], out_t[:, 0, :], inc=st)

        p.w("sync", st, st_val)
        p.build_block()
    return nc


def run_l1_attn(x, w_qkv, b_qkv, qn_w, kn_w, w_proj, b_proj):
    nc = _get_nc("l1a", build_l1_attn)
    idn = np.eye(128, dtype=np.float32)
    com = dict(wqkv=np.ascontiguousarray(w_qkv.T), bqkv=b_qkv, qn=qn_w, kn=kn_w,
               wpr=np.ascontiguousarray(w_proj.T), bpr=b_proj, idn=idn)
    in_maps = []
    for c in range(NCORES):
        b_, h = c // 2, c % 2
        m = dict(com)
        m["x_own"] = np.ascontiguousarray(x[b_, h * T:(h + 1) * T])
        m["x_oth"] = np.ascontiguousarray(x[b_, (1 - h) * T:(2 - h) * T])
        in_maps.append(m)
    res = run_bass_kernel_spmd(nc, in_maps, core_ids=list(range(NCORES)))
    out = np.empty((B_, S_, H_), np.float32)
    for c in range(NCORES):
        b_, h = c // 2, c % 2
        out[b_, h * T:(h + 1) * T] = res.results[c]["y"]
    return out

def build_l1_ffn():
    nc = bass.Bass("TRN2", target_bir_lowering=False)
    x_in = nc.dram_tensor("x_in", [T, H_], F32, kind="ExternalInput").ap()
    xb = nc.dram_tensor("xb", [1, H_], F32, kind="ExternalInput").ap()
    wr = nc.dram_tensor("wr", [H_, H_], F32, kind="ExternalInput").ap()
    wk = nc.dram_tensor("wk", [H_, H_], F32, kind="ExternalInput").ap()
    wv = nc.dram_tensor("wv", [H_, H_], F32, kind="ExternalInput").ap()
    mur = nc.dram_tensor("mur", [H_], F32, kind="ExternalInput").ap()
    muk = nc.dram_tensor("muk", [H_], F32, kind="ExternalInput").ap()
    sh1 = nc.dram_tensor("sh1", [128, 128], F32, kind="ExternalInput").ap()
    sh2 = nc.dram_tensor("sh2", [128, 128], F32, kind="ExternalInput").ap()
    idn = nc.dram_tensor("idn", [128, 128], F32, kind="ExternalInput").ap()
    y = nc.dram_tensor("y", [T, H_], F32, kind="ExternalOutput").ap()

    with contextlib.ExitStack() as stack:
        E = stack.enter_context
        x_t = E(nc.sbuf_tensor("x_t", [128, 2, H_], F32))
        p_bf = E(nc.sbuf_tensor("p_bf", [128, 2, H_], BF16))
        pb0 = E(nc.sbuf_tensor("pb0", [128, H_], BF16))
        p_f = E(nc.sbuf_tensor("p_f", [128, H_], F32))
        w_sb = {}
        for nm in ("wr", "wk", "wv"):
            w_sb[nm] = E(nc.sbuf_tensor(nm + "_sb", [128, 8, H_], BF16))
        mur_bc = E(nc.sbuf_tensor("mur_bc", [128, H_], F32))
        muk_bc = E(nc.sbuf_tensor("muk_bc", [128, H_], F32))
        mur_r = E(nc.sbuf_tensor("mur_r", [1, H_], F32))
        muk_r = E(nc.sbuf_tensor("muk_r", [1, H_], F32))
        sh1_b = E(nc.sbuf_tensor("sh1_b", [128, 128], BF16))
        sh2_b = E(nc.sbuf_tensor("sh2_b", [128, 128], BF16))
        idn_b = E(nc.sbuf_tensor("idn_b", [128, 128], BF16))
        ones_f = E(nc.sbuf_tensor("ones_f", [1, 128], F32))
        xb_sb = E(nc.sbuf_tensor("xb_sb", [1, H_], F32))
        sA = E(nc.sbuf_tensor("sA", [128, H_], F32))
        sB = E(nc.sbuf_tensor("sB", [128, H_], F32))
        r_tm = E(nc.sbuf_tensor("r_tm", [128, H_], F32))
        xr_tm = E(nc.sbuf_tensor("xr_tm", [128, H_], BF16))
        xk_tm = E(nc.sbuf_tensor("xk_tm", [128, H_], BF16))
        k2_tm = E(nc.sbuf_tensor("k2_tm", [128, H_], BF16))
        fm = E(nc.sbuf_tensor("fm", [128, 8, 128], BF16))
        out_t = E(nc.sbuf_tensor("out_t", [128, 2, H_], F32))
        bn_sb = E(nc.sbuf_tensor("bn_sb", [128, 16], F32))
        ab_sb = E(nc.sbuf_tensor("ab_sb", [128, 2], F32))
        xp_ps = E(nc.psum_tensor("xp_ps", [128, H_], F32))
        pa_ps = E(nc.psum_tensor("pa_ps", [128, H_], F32))
        pb_ps = E(nc.psum_tensor("pb_ps", [128, H_], F32))
        tp_ps = E(nc.psum_tensor("tp_ps", [128, 8, 128], BF16))

        p = Prog(nc, stack)
        ld = p.sem("ld")
        lw = p.sem("lw")
        pe = p.sem("pe")
        dv = p.sem("dv")
        ac = p.sem("ac")
        st = p.sem("st")

        for nm, srcw in (("wr", wr), ("wk", wk), ("wv", wv)):
            p.dma("gpsimd", w_sb[nm][:, :, :], srcw.rearrange("(c p) o -> p c o", p=128), inc=lw)
        p.dma("gpsimd", sh1_b[:, :], sh1[:, :], inc=lw)
        p.dma("gpsimd", sh2_b[:, :], sh2[:, :], inc=lw)
        p.dma("gpsimd", idn_b[:, :], idn[:, :], inc=lw)
        p.dma("gpsimd", mur_r[:, :], mur[None, :], inc=lw)
        p.dma("gpsimd", muk_r[:, :], muk[None, :], inc=lw)
        p.dma("gpsimd", xb_sb[:, :], xb[:, :], inc=lw)
        p.i("gpsimd", lambda g: g.memset(ones_f[:, :], 1.0))
        p.i("gpsimd", lambda g: g.memset(pb0[:, :], 0.0), inc=lw, n=1)
        lw0 = p.v(lw)
        p.w("tensor", lw, lw0)
        for r_, bc in ((mur_r, mur_bc), (muk_r, muk_bc)):
            for half in range(2):
                sl = slice(half * 512, (half + 1) * 512)
                mm = p.i("tensor", lambda t, sl=sl, r_=r_: t.matmul(
                    pa_ps[:, 0:512], ones_f[:, :], r_[:, sl], start=True, stop=True), inc=pe)
                p.w("vector", pe, mm)
                cp = p.i("vector", lambda v, sl=sl, bc=bc: v.tensor_copy(bc[:, sl], pa_ps[:, 0:512]), inc=dv)
                p.w("tensor", dv, cp)
        p.w("vector", lw, lw0)
        _ln_ab(p, xb_sb[:, :], bn_sb[0:1, :], ab_sb[0:1, :], 1e-5, dv, ac)
        p.i("vector", lambda v: v.tensor_scalar(
            sA[0:1, :], xb_sb[:, :], ab_sb[0:1, 0:1], ab_sb[0:1, 1:2], ALU.mult, op1=ALU.add), inc=dv)
        p.w("vector", dv)
        p.i("vector", lambda v: v.tensor_copy(xr_tm[0:1, :], sA[0:1, :]), inc=dv)
        p.w("gpsimd", dv)
        pb0_lw = p.dma("gpsimd", pb0[127:128, :], xr_tm[0:1, :], inc=lw)

        st_val = 0
        o_dv_hist = {}
        p_read_pe = 0
        prev_ps_read = 0
        xr_read_pe = 0
        p.w("vector", lw, pb0_lw)
        for t in range(NT):
            r = t % 2
            if t >= 2:
                p.w("sync", dv, o_dv_hist[t - 2])
            ldv = p.dma("sync", x_t[:, r, :], x_in[t * 128:(t + 1) * 128, :], inc=ld)
            p.w("vector", ld, ldv)
            _ln_ab(p, x_t[:, r, :], bn_sb, ab_sb, 1e-5, dv, ac)
            p.i("vector", lambda v, rr=r: v.tensor_scalar(
                p_f[:, :], x_t[:, rr, :], ab_sb[:, 0:1], ab_sb[:, 1:2], ALU.mult, op1=ALU.add), inc=dv)
            p.w("vector", dv)
            if t >= 2:
                p.w("vector", pe, p_read_pe)
            p.i("vector", lambda v, rr=r: v.tensor_copy(p_bf[:, rr, :], p_f[:, :]), inc=dv)
            pbf_dv = p.v(dv)
            if t == 0:
                p.w("tensor", lw, pb0_lw)
            p.w("tensor", dv, pbf_dv)
            p.w("tensor", dv, prev_ps_read)
            prev_ap = pb0[:, :] if t == 0 else p_bf[:, (t - 1) % 2, :]
            for half in range(2):
                sl = slice(half * 512, (half + 1) * 512)
                p.i("tensor", lambda tt, sl=sl, rr=r: tt.matmul(
                    xp_ps[:, sl], sh1_b[:, :], p_bf[:, rr, sl], start=True, stop=False))
                p.i("tensor", lambda tt, sl=sl, pa=prev_ap: tt.matmul(
                    xp_ps[:, sl], sh2_b[:, :], pa[:, sl], start=False, stop=True),
                    inc=pe if half == 1 else None)
            p_read_pe = p.v(pe)
            xp_pe = p_read_pe
            p.w("vector", pe, xp_pe)
            p.i("vector", lambda v: v.tensor_tensor(sA[:, :], p_f[:, :], xp_ps[:, :], ALU.subtract), inc=dv)
            p.w("vector", dv)
            p.i("vector", lambda v: v.tensor_tensor(sB[:, :], sA[:, :], mur_bc[:, :], ALU.mult), inc=dv)
            p.w("vector", dv)
            p.w("vector", pe, xr_read_pe)
            p.i("vector", lambda v: v.tensor_tensor(xr_tm[:, :], sB[:, :], xp_ps[:, :], ALU.add), inc=dv)
            p.w("vector", dv)
            p.i("vector", lambda v: v.tensor_tensor(sB[:, :], sA[:, :], muk_bc[:, :], ALU.mult), inc=dv)
            p.w("vector", dv)
            p.i("vector", lambda v: v.tensor_tensor(xk_tm[:, :], sB[:, :], xp_ps[:, :], ALU.add), inc=dv)
            xrk_dv = p.v(dv)
            p.w("tensor", dv, xrk_dv)
            for c in range(8):
                p.i("tensor", lambda tt, c=c: tt.transpose(
                    tp_ps[:, c, :], xr_tm[:, c * 128:(c + 1) * 128], idn_b[:, :]),
                    inc=pe if c == 7 else None)
            p.w("vector", pe)
            p.i("vector", lambda v: v.tensor_copy(fm[:, :, :], tp_ps[:, :, :]), inc=dv)
            fm_dv = p.v(dv)
            p.w("tensor", dv, fm_dv)
            for half in range(2):
                sl = slice(half * 512, (half + 1) * 512)
                for c in range(8):
                    p.i("tensor", lambda tt, c=c, sl=sl: tt.matmul(
                        pa_ps[:, sl], fm[:, c, :], w_sb["wr"][:, c, sl], start=(c == 0), stop=(c == 7)),
                        inc=pe if (c == 7 and half == 1) else None)
            rp_pe = p.v(pe)
            for c in range(8):
                p.i("tensor", lambda tt, c=c: tt.transpose(
                    tp_ps[:, c, :], xk_tm[:, c * 128:(c + 1) * 128], idn_b[:, :]),
                    inc=pe if c == 7 else None)
            ktp_pe = p.v(pe)
            xr_read_pe = ktp_pe
            p.w("vector", pe, ktp_pe)
            p.i("vector", lambda v: v.tensor_copy(fm[:, :, :], tp_ps[:, :, :]), inc=dv)
            fmk_dv = p.v(dv)
            p.w("tensor", dv, fmk_dv)
            for half in range(2):
                sl = slice(half * 512, (half + 1) * 512)
                for c in range(8):
                    p.i("tensor", lambda tt, c=c, sl=sl: tt.matmul(
                        pb_ps[:, sl], fm[:, c, :], w_sb["wk"][:, c, sl], start=(c == 0), stop=(c == 7)),
                        inc=pe if (c == 7 and half == 1) else None)
            kp_pe = p.v(pe)
            p.w("scalar", pe, rp_pe)
            p.i("scalar", lambda s: s.activation(r_tm[:, :], pa_ps[:, :], AF.Sigmoid), inc=ac)
            r_ac = p.v(ac)
            p.w("vector", pe, kp_pe)
            p.i("vector", lambda v: v.tensor_scalar(sA[:, :], pb_ps[:, :], 0.0, None, ALU.max), inc=dv)
            p.w("scalar", dv)
            p.i("scalar", lambda s: s.activation(k2_tm[:, :], sA[:, :], AF.Square), inc=ac)
            k2_ac = p.v(ac)
            p.w("tensor", ac, k2_ac)
            for c in range(8):
                p.i("tensor", lambda tt, c=c: tt.transpose(
                    tp_ps[:, c, :], k2_tm[:, c * 128:(c + 1) * 128], idn_b[:, :]),
                    inc=pe if c == 7 else None)
            p.w("vector", pe)
            p.i("vector", lambda v: v.tensor_copy(fm[:, :, :], tp_ps[:, :, :]), inc=dv)
            fmv_dv = p.v(dv)
            p.w("tensor", dv, fmv_dv)
            p.w("tensor", ac, r_ac)
            for half in range(2):
                sl = slice(half * 512, (half + 1) * 512)
                for c in range(8):
                    p.i("tensor", lambda tt, c=c, sl=sl: tt.matmul(
                        pa_ps[:, sl], fm[:, c, :], w_sb["wv"][:, c, sl], start=(c == 0), stop=(c == 7)),
                        inc=pe if (c == 7 and half == 1) else None)
            vp_pe = p.v(pe)
            p.w("vector", pe, vp_pe)
            p.w("vector", ac, r_ac)
            p.i("vector", lambda v: v.tensor_tensor(sB[:, :], r_tm[:, :], pa_ps[:, :], ALU.mult), inc=dv)
            prev_ps_read = p.v(dv)
            p.w("vector", dv)
            _ln_ab(p, sB[:, :], bn_sb, ab_sb, 1e-5, dv, ac)
            p.i("vector", lambda v: v.tensor_scalar(
                sA[:, :], sB[:, :], ab_sb[:, 0:1], ab_sb[:, 1:2], ALU.mult, op1=ALU.add), inc=dv)
            p.w("vector", dv)
            if t >= 2:
                p.w("vector", st, st_val - 16)
            p.i("vector", lambda v, rr=r: v.tensor_tensor(
                out_t[:, rr, :], sA[:, :], x_t[:, rr, :], ALU.add), inc=dv)
            o_dv = p.v(dv)
            o_dv_hist[t] = o_dv
            p.w("sync", dv, o_dv)
            st_val = p.dma("sync", y[t * 128:(t + 1) * 128, :], out_t[:, r, :], inc=st)

        p.w("sync", st, st_val)
        p.build_block()
    return nc


def run_l1_ffn(x, W_r, W_k, W_v, mu_r, mu_k):
    nc = _get_nc("l1f", build_l1_ffn)
    idn = np.eye(128, dtype=np.float32)
    sh1 = np.eye(128, k=1, dtype=np.float32)
    sh2 = np.zeros((128, 128), dtype=np.float32)
    sh2[127, 0] = 1.0
    com = dict(wr=np.ascontiguousarray(W_r.T), wk=np.ascontiguousarray(W_k.T),
               wv=np.ascontiguousarray(W_v.T), mur=mu_r, muk=mu_k, sh1=sh1, sh2=sh2, idn=idn)
    in_maps = []
    for c in range(NCORES):
        b_, h = c // 2, c % 2
        m = dict(com)
        m["x_in"] = np.ascontiguousarray(x[b_, h * T:(h + 1) * T])
        if h == 1:
            m["xb"] = np.ascontiguousarray(x[b_, T - 1:T])
        else:
            m["xb"] = np.zeros((1, H_), np.float32)
        in_maps.append(m)
    res = run_bass_kernel_spmd(nc, in_maps, core_ids=list(range(NCORES)))
    out = np.empty((B_, S_, H_), np.float32)
    for c in range(NCORES):
        b_, h = c // 2, c % 2
        out[b_, h * T:(h + 1) * T] = res.results[c]["y"]
    return out


def kernel(**inputs):
    return kernel_fused(inputs)


def emit_l0_attn(p, nc, E, x_own, x_oth, wq, wk, wv, wd, bq, bk, bv, bd, gam, bet, idn, y):
    ld, lw, pe, dv, ac, st, lx, ks, po = "ld", "lw", "pe", "dv", "ac", "st", "lx", "ks", "po"
    if True:
        x_t = E(nc.sbuf_tensor("Ax_t", [128, 2, H_], F32))
        xn_own = E(nc.sbuf_tensor("Axn_own", [128, NT, H_], BF16))
        xn_fm = E(nc.sbuf_tensor("Axn_fm", [128, 2, 8, 128], BF16))
        qf = E(nc.sbuf_tensor("Aqf", [128, 8, T], BF16))
        kh = E(nc.sbuf_tensor("Akh", [128, 2, H_], BF16))
        vh = E(nc.sbuf_tensor("Avh", [128, 2, NH, 65], BF16))
        kv_sb = E(nc.sbuf_tensor("Akv_sb", [128, NH // 2, 65], F32))
        kv_bf = E(nc.sbuf_tensor("Akv_bf", [128, NH // 2, 65], BF16))
        w_sb = {}
        for nm in ("wq", "wk", "wvd"):
            w_sb[nm] = E(nc.sbuf_tensor(nm + "_sb" + "A", [128, 8, H_], BF16))
        brow = {}
        for nm in ("bq", "bk", "bv", "bd"):
            brow[nm] = E(nc.sbuf_tensor(nm + "_r", [1, H_], BF16))
        gam_r = E(nc.sbuf_tensor("Agam_r", [1, H_], F32))
        bet_r = E(nc.sbuf_tensor("Abet_r", [1, H_], F32))
        gam_bc = E(nc.sbuf_tensor("Agam_bc", [128, H_], F32))
        bet_bc = E(nc.sbuf_tensor("Abet_bc", [128, H_], F32))
        ones_b = E(nc.sbuf_tensor("Aones_b", [1, 128], BF16))
        ones_f = E(nc.sbuf_tensor("Aones_f", [1, 128], F32))
        idn_b = E(nc.sbuf_tensor("Aidn_b", [128, 128], BF16))
        sA = E(nc.sbuf_tensor("AsA", [128, H_], F32))
        sB = E(nc.sbuf_tensor("AsB", [128, H_], F32))
        qtm = E(nc.sbuf_tensor("Aqtm", [128, H_], BF16))
        ctm = E(nc.sbuf_tensor("Actm", [128, H_], BF16))
        cfm = E(nc.sbuf_tensor("Acfm", [128, 8, 128], BF16))
        hs_sb = E(nc.sbuf_tensor("Ahs_sb", [128, H_], F32))
        u2_sb = E(nc.sbuf_tensor("Au2_sb", [128, H_], F32))
        x2_t = E(nc.sbuf_tensor("Ax2_t", [128, 2, H_], F32))
        out_t = E(nc.sbuf_tensor("Aout_t", [128, 2, H_], F32))
        bn_sb = E(nc.sbuf_tensor("Abn_sb", [128, 16], F32))
        bnst_all = E(nc.sbuf_tensor("Abnst_all", [128, 32, 12], F32))
        mv_all = E(nc.sbuf_tensor("Amv_all", [128, 32, 2], F32))
        ap_all = E(nc.sbuf_tensor("Aap_all", [128, 32, 2], F32))
        ab_sb = E(nc.sbuf_tensor("Aab_sb", [128, 2], F32))
        ab2_sb = E(nc.sbuf_tensor("Aab2_sb", [128, 2], F32))
        zden = E(nc.sbuf_tensor("Azden", [128, NH], F32))
        caug = E(nc.sbuf_tensor("Acaug", [128, NH, 65], F32))
        tp_ps = E(nc.psum_tensor("Atp_ps", [128, 8, 128], BF16))
        pkq = E(nc.psum_tensor("Apkq", [128, H_], F32))
        pv = E(nc.psum_tensor("Apv", [128, H_], F32))
        kvp = [E(nc.psum_tensor(f"Akvp{i}", [128, 512], F32)) for i in range(2)]


        for nm, src in (("wq", wq), ("wk", wk), ("wvd", wv)):
            p.dma("gpsimd", w_sb[nm][:, :, :], src.rearrange("(c p) o -> p c o", p=128), inc=lw)
        for nm, src in (("bq", bq), ("bk", bk), ("bv", bv), ("bd", bd)):
            p.dma("gpsimd", brow[nm][:, :], src[None, :], inc=lw)
        p.dma("gpsimd", gam_r[:, :], gam[None, :], inc=lw)
        p.dma("gpsimd", bet_r[:, :], bet[None, :], inc=lw)
        p.dma("gpsimd", idn_b[:, :], idn[:, :], inc=lw)
        p.i("gpsimd", lambda g: g.memset(ones_b[:, :], 1.0))
        p.i("gpsimd", lambda g: g.memset(ones_f[:, :], 1.0))
        p.i("gpsimd", lambda g: g.memset(kv_sb[:, :, :], 0.0))
        p.i("gpsimd", lambda g: g.memset(vh[:, :, :, :], 1.0), inc=lw, n=1)
        lw_done = p.v(lw)

        p.w("tensor", lw, lw_done)
        for half in range(2):
            sl = slice(half * 512, (half + 1) * 512)
            p.i("tensor", lambda t, sl=sl: t.matmul(pv[:, sl], ones_f[:, :], gam_r[:, sl],
                                                    start=True, stop=True))
        p.i("tensor", lambda t: t.matmul(pkq[:, 0:512], ones_f[:, :], bet_r[:, 0:512], start=True, stop=True))
        p.i("tensor", lambda t: t.matmul(pkq[:, 512:1024], ones_f[:, :], bet_r[:, 512:1024],
                                         start=True, stop=True), inc=pe)
        p.w("vector", pe)
        p.i("vector", lambda v: v.tensor_copy(gam_bc[:, :], pv[:, :]))
        p.i("vector", lambda v: v.tensor_copy(bet_bc[:, :], pkq[:, :]), inc=dv)
        bcast_done_dv = p.v(dv)

        p.w("vector", lw, lw_done)
        kv_last_pe = 0
        qtm_free_pe = 0
        xn_dv_hist = {}
        prev_psum_read_dv = 0
        for t in range(32):
            own = t < NT
            r = t % 2
            src = x_own if own else x_oth
            row0 = (t % NT) * 128
            if t >= 2:
                p.w("sync", dv, xn_dv_hist[t - 2])
            ld_val = p.dma("sync", x_t[:, r, :], src[row0:row0 + 128, :], inc=ld)
            p.w("vector", ld, ld_val)
            p.i("vector", lambda v, t=t, rr=r: v.bn_stats(bnst_all[:, t, 0:6], x_t[:, rr, 0:512]))
            p.i("vector", lambda v, t=t, rr=r: v.bn_stats(bnst_all[:, t, 6:12], x_t[:, rr, 512:1024]), inc=dv)
            xn_dv_hist[t] = p.v(dv)
        p.w("vector", dv)
        for t in range(32):
            p.i("vector", lambda v, t=t: v.bn_aggr(mv_all[:, t, :], bnst_all[:, t, :]),
                inc=dv if t == 31 else None)
        p.w("vector", dv)
        p.i("vector", lambda v: v.tensor_scalar(bnst_all[:, :, 0], mv_all[:, :, 1], 1.0, 1e-5,
                                                ALU.mult, op1=ALU.add), inc=dv)
        p.w("scalar", dv)
        p.i("scalar", lambda s: s.activation(bnst_all[:, :, 0], bnst_all[:, :, 0], AF.Ln), inc=ac)
        p.w("scalar", ac)
        p.i("scalar", lambda s: s.activation(ap_all[:, :, 0], bnst_all[:, :, 0], AF.Exp, scale=-0.5), inc=ac)
        p.w("vector", ac)
        p.i("vector", lambda v: v.tensor_tensor(ap_all[:, :, 1], mv_all[:, :, 0], ap_all[:, :, 0], ALU.mult), inc=dv)
        p.w("vector", dv)
        p.i("vector", lambda v: v.tensor_scalar(ap_all[:, :, 1], ap_all[:, :, 1], -1.0, None, ALU.mult), inc=dv)
        pre_dv = p.v(dv)
        p.w("vector", dv, pre_dv)
        for t in range(32):
            own = t < NT
            r = t % 2
            src = x_own if own else x_oth
            row0 = (t % NT) * 128
            p.w("sync", dv, pre_dv if t < 2 else xn_dv_hist[32 + t - 2])
            ld_val = p.dma("sync", x_t[:, r, :], src[row0:row0 + 128, :], inc=ld)
            p.w("vector", ld, ld_val)
            if own:
                xn_dst = xn_own[:, t, :]
            else:
                xn_dst = qtm[:, :]
                p.w("vector", pe, qtm_free_pe)
            p.i("vector", lambda v, d=xn_dst, rr=r, t=t: v.tensor_scalar(
                d, x_t[:, rr, :], ap_all[:, t, 0:1], ap_all[:, t, 1:2], ALU.mult, op1=ALU.add), inc=dv)
            xn_dv = p.v(dv)
            xn_dv_hist[32 + t] = xn_dv

            p.w("tensor", dv, xn_dv)
            xn_src = xn_own[:, t, :] if own else qtm[:, :]
            for c in range(8):
                p.i("tensor", lambda tt, c=c, s=xn_src: tt.transpose(
                    tp_ps[:, c, :], s[:, c * 128:(c + 1) * 128], idn_b[:, :]),
                    inc=pe if c == 7 else None)
            tp_pe = p.v(pe)
            if not own:
                qtm_free_pe = tp_pe
            p.w("vector", pe, tp_pe)
            p.i("vector", lambda v, rr=r: v.tensor_copy(
                xn_fm[:, rr, :, :], tp_ps[:, :, :]), inc=dv)
            fm_dv = p.v(dv)

            p.w("tensor", dv, fm_dv)
            p.w("tensor", dv, prev_psum_read_dv)
            for wn, bn_, ps in (("wk", "bk", pkq), ("wvd", "bv", pv)):
                for half in range(2):
                    sl = slice(half * 512, (half + 1) * 512)
                    for c in range(8):
                        p.i("tensor", lambda tt, c=c, sl=sl, wn=wn, ps=ps, rr=r: tt.matmul(
                            ps[:, sl], xn_fm[:, rr, c, :], w_sb[wn][:, c, sl],
                            start=(c == 0), stop=False))
                    p.i("tensor", lambda tt, sl=sl, bn_=bn_, ps=ps: tt.matmul(
                        ps[:, sl], ones_b[:, :], brow[bn_][:, sl], start=False, stop=True),
                        inc=pe)
            proj_pe = p.v(pe)

            p.w("vector", pe, proj_pe)
            p.i("vector", lambda v: v.tensor_scalar(sA[:, :], pkq[:, :], 0.0, None, ALU.min), inc=dv)
            p.w("scalar", dv)
            p.i("scalar", lambda s: s.activation(sB[:, :], sA[:, :], AF.Exp), inc=ac)
            p.i("vector", lambda v: v.tensor_scalar(sA[:, :], pkq[:, :], 0.0, None, ALU.max), inc=dv)
            k_reads_dv = p.v(dv)
            p.w("vector", ac)
            p.w("vector", dv)
            p.w("vector", pe, kv_last_pe)
            p.i("vector", lambda v, rr=r: v.tensor_tensor(kh[:, rr, :], sB[:, :], sA[:, :], ALU.add), inc=dv)
            kh_dv = p.v(dv)
            p.i("vector", lambda v, rr=r: v.tensor_copy(
                vh[:, rr, :, 0:64],
                pv.rearrange("p (h d) -> p h d", h=16)), inc=dv)
            vh_dv = p.v(dv)

            if own:
                p.w("tensor", dv, k_reads_dv)
                for half in range(2):
                    sl = slice(half * 512, (half + 1) * 512)
                    for c in range(8):
                        p.i("tensor", lambda tt, c=c, sl=sl, rr=r: tt.matmul(
                            pkq[:, sl], xn_fm[:, rr, c, :], w_sb["wq"][:, c, sl],
                            start=(c == 0), stop=False))
                    p.i("tensor", lambda tt, sl=sl: tt.matmul(
                        pkq[:, sl], ones_b[:, :], brow["bq"][:, sl], start=False, stop=True),
                        inc=pe)
                qproj_pe = p.v(pe)
                p.w("vector", pe, qproj_pe)
                p.i("vector", lambda v: v.tensor_scalar(sA[:, :], pkq[:, :], 0.0, None, ALU.min), inc=dv)
                p.w("scalar", dv)
                p.i("scalar", lambda s: s.activation(sB[:, :], sA[:, :], AF.Exp), inc=ac)
                p.i("vector", lambda v: v.tensor_scalar(sA[:, :], pkq[:, :], 0.0, None, ALU.max), inc=dv)
                p.w("vector", ac)
                p.w("vector", dv)
                p.w("vector", pe, qtm_free_pe)
                p.i("vector", lambda v: v.tensor_tensor(qtm[:, :], sB[:, :], sA[:, :], ALU.add), inc=dv)
                q_dv = p.v(dv)
                p.w("tensor", dv, q_dv)
                for hp in range(8):
                    p.i("tensor", lambda tt, hp=hp: tt.transpose(
                        tp_ps[:, hp, :], qtm[:, hp * 128:(hp + 1) * 128], idn_b[:, :]),
                        inc=pe if hp == 7 else None)
                qtm_free_pe = p.v(pe)
                p.w("vector", pe, qtm_free_pe)
                p.i("vector", lambda v, t=t: v.tensor_copy(
                    qf[:, :, t * 128:(t + 1) * 128], tp_ps[:, :, :]), inc=dv)

            p.w("tensor", dv, kh_dv)
            p.w("tensor", dv, vh_dv)
            for hp in range(8):
                hA, hB = 2 * hp, 2 * hp + 1
                ps = kvp[hp % 2]
                if t > 0 or hp >= 2:
                    p.w("tensor", dv, kv_add_dv[(hp - 2) % 8])
                p.i("tensor", lambda tt, hA=hA, rr=r, ps=ps: tt.matmul(
                    ps[0:64, 0:65], kh[:, rr, hA * 64:(hA + 1) * 64], vh[:, rr, hA, :],
                    start=True, stop=True, tile_position=(0, 0)))
                p.i("tensor", lambda tt, hB=hB, rr=r, ps=ps: tt.matmul(
                    ps[64:128, 0:65], kh[:, rr, hB * 64:(hB + 1) * 64], vh[:, rr, hB, :],
                    start=True, stop=True, tile_position=(0, 64)), inc=pe)
                p.w("vector", pe)
                p.i("vector", lambda v, hp=hp, ps=ps: v.tensor_tensor(
                    kv_sb[:, hp, :], kv_sb[:, hp, :], ps[:, 0:65], ALU.add), inc=dv)
                if t == 0 and hp == 0:
                    kv_add_dv = [None] * 8
                kv_add_dv[hp] = p.v(dv)
            kv_last_pe = p.v(pe)
            prev_psum_read_dv = p.v(dv)

        p.w("vector", dv)
        p.i("vector", lambda v: v.tensor_scalar(
            kv_sb[:, :, 64:65], kv_sb[:, :, 64:65], 1e-6, None, ALU.add), inc=dv)
        p.w("vector", dv)
        p.i("vector", lambda v: v.tensor_copy(kv_bf[:, :, :], kv_sb[:, :, :]), inc=dv)
        kvb_dv = p.v(dv)

        p.w("gpsimd", pe, kv_last_pe)
        p.dma("gpsimd", w_sb["wvd"][:, :, :], wd.rearrange("(c p) o -> p c o", p=128), inc=lw)
        lw_wd = p.v(lw)
        p.w("tensor", lw, lw_wd)

        p.w("tensor", dv, kvb_dv)
        p.w("vector", dv, bcast_done_dv)
        st_val = 0
        o_dv_hist = {}
        for t in range(NT):
            r = t % 2
            if t >= 2:
                p.w("sync", dv, o_dv_hist[t - 2])
            lx_val = p.dma("sync", x2_t[:, r, :], x_own[t * 128:(t + 1) * 128, :], inc=lx)

            for hp in range(8):
                psA_ = pkq[:, 0:65] if hp % 2 == 0 else kvp[0][:, 0:65]
                psB_ = pkq[:, 512:512 + 65] if hp % 2 == 0 else kvp[1][:, 0:65]
                if t > 0 or hp > 1:
                    p.w("tensor", dv, ctx_evac[hp % 2])
                p.i("tensor", lambda tt, hp=hp, ps=psA_, t=t: tt.matmul(
                    ps, qf[0:64, hp, t * 128:(t + 1) * 128], kv_bf[0:64, hp, :],
                    start=True, stop=True, tile_position=(0, 0)))
                p.i("tensor", lambda tt, hp=hp, ps=psB_, t=t: tt.matmul(
                    ps, qf[64:128, hp, t * 128:(t + 1) * 128], kv_bf[64:128, hp, :],
                    start=True, stop=True, tile_position=(64, 0)), inc=pe)
                ctx_pe = p.v(pe)
                p.w("vector", pe, ctx_pe)
                p.i("vector", lambda v, ps=psA_, hp=hp: v.tensor_copy(caug[:, 2 * hp, :], ps[:, 0:65]))
                p.i("vector", lambda v, ps=psB_, hp=hp: v.tensor_copy(caug[:, 2 * hp + 1, :], ps[:, 0:65]), inc=dv)
                if hp == 0 and t == 0:
                    ctx_evac = [None, None]
                ctx_evac[hp % 2] = p.v(dv)
            p.w("vector", dv)
            p.i("vector", lambda v: v.reciprocal(zden[:, :], caug[:, :, 64]), inc=dv)
            p.w("vector", dv)
            for h in range(NH):
                p.i("vector", lambda v, h=h: v.tensor_scalar(
                    ctm[:, h * 64:(h + 1) * 64], caug[:, h, 0:64], zden[:, h:h + 1], None,
                    ALU.mult), inc=dv if h == NH - 1 else None)
            ctm_dv = p.v(dv)

            p.w("tensor", dv, ctm_dv)
            for c in range(8):
                p.i("tensor", lambda tt, c=c: tt.transpose(
                    tp_ps[:, c, :], ctm[:, c * 128:(c + 1) * 128], idn_b[:, :]),
                    inc=pe if c == 7 else None)
            p.w("vector", pe)
            p.i("vector", lambda v: v.tensor_copy(cfm[:, :, :], tp_ps[:, :, :]), inc=dv)
            cfm_dv = p.v(dv)

            p.w("tensor", dv, cfm_dv)
            for half in range(2):
                sl = slice(half * 512, (half + 1) * 512)
                for c in range(8):
                    p.i("tensor", lambda tt, c=c, sl=sl: tt.matmul(
                        pv[:, sl], cfm[:, c, :], w_sb["wvd"][:, c, sl], start=(c == 0), stop=False))
                p.i("tensor", lambda tt, sl=sl: tt.matmul(
                    pv[:, sl], ones_b[:, :], brow["bd"][:, sl], start=False, stop=True),
                    inc=pe if half == 1 else None)
            dp_pe = p.v(pe)

            p.w("vector", pe, dp_pe)
            p.i("vector", lambda v, t=t: v.tensor_tensor(hs_sb[:, :], pv[:, :], xn_own[:, t, :], ALU.add), inc=dv)
            p.w("vector", dv)
            _ln_ab(p, hs_sb[:, :], bn_sb, ab_sb, 1e-12, dv, ac)
            p.i("vector", lambda v: v.tensor_scalar(sA[:, :], hs_sb[:, :], ab_sb[:, 0:1], ab_sb[:, 1:2],
                                                    ALU.mult, op1=ALU.add), inc=dv)
            p.w("vector", dv)
            p.i("vector", lambda v: v.tensor_tensor(sB[:, :], sA[:, :], gam_bc[:, :], ALU.mult), inc=dv)
            p.w("vector", dv)
            p.i("vector", lambda v: v.tensor_tensor(u2_sb[:, :], sB[:, :], bet_bc[:, :], ALU.add), inc=dv)
            p.w("vector", dv)
            _ln_ab(p, u2_sb[:, :], bn_sb, ab2_sb, 1e-5, dv, ac)
            p.i("vector", lambda v: v.tensor_scalar(sA[:, :], u2_sb[:, :], ab2_sb[:, 0:1], ab2_sb[:, 1:2],
                                                    ALU.mult, op1=ALU.add), inc=dv)
            p.w("vector", lx, lx_val)
            p.w("vector", dv)
            if t >= 2:
                p.w("vector", st, st_val - 16)
            p.i("vector", lambda v, r=r: v.tensor_tensor(out_t[:, r, :], sA[:, :], x2_t[:, r, :], ALU.add), inc=dv)
            o_dv = p.v(dv)
            o_dv_hist[t] = o_dv
            p.w("sync", dv, o_dv)
            st_val = p.dma("sync", y[t * 128:(t + 1) * 128, :], out_t[:, r, :], inc=st)

        p.w("sync", st, st_val)


def emit_l0_ffn(p, nc, E, x_in, gam, bet, wr, br, w1, b1, w2, b2, idn, y):
    ld, lw, pe, dv, ac, st, lx, ks, po = "ld", "lw", "pe", "dv", "ac", "st", "lx", "ks", "po"
    NSL = 4
    FS = 4096 // NSL
    if True:
        x_t = E(nc.sbuf_tensor("Bx_t", [128, 2, H_], F32))
        ab_all = E(nc.sbuf_tensor("Bab_all", [128, NT, 2], F32))
        xn_tm = E(nc.sbuf_tensor("Bxn_tm", [128, H_], BF16))
        xn_fm = E(nc.sbuf_tensor("Bxn_fm", [128, 8, T], BF16))
        acc = E(nc.sbuf_tensor("Bacc", [128, 8, H_], F32))
        w1s = E(nc.sbuf_tensor("Bw1s", [128, 8, FS], BF16))
        w2s = E(nc.sbuf_tensor("Bw2s", [128, 8, H_], BF16))
        u_sb = E(nc.sbuf_tensor("Bu_sb", [128, 8, 512], BF16))
        gates = E(nc.sbuf_tensor("Bgates", [128, 128], F32))
        s_all = E(nc.sbuf_tensor("Bs_all", [128, NT], F32))
        gT = E(nc.sbuf_tensor("BgT", [9, T], BF16))
        dmat = E(nc.sbuf_tensor("Bdmat", [9, H_], BF16))
        dcol = E(nc.sbuf_tensor("Bdcol", [128, 8, 8], F32))
        b1_sb = E(nc.sbuf_tensor("Bb1_sb", [8, 2048], F32))
        b1_pt = E(nc.sbuf_tensor("Bb1_pt", [128, 8, 32], F32))
        brelu = E(nc.sbuf_tensor("Bbrelu", [128, 8, 32], BF16))
        b2_sb = E(nc.sbuf_tensor("Bb2_sb", [8, H_], F32))
        wr_sb = E(nc.sbuf_tensor("Bwr_sb", [128, 8, 8], BF16))
        br_r = E(nc.sbuf_tensor("Bbr_r", [1, 8], BF16))
        gam_r = E(nc.sbuf_tensor("Bgam_r", [1, H_], F32))
        bet_r = E(nc.sbuf_tensor("Bbet_r", [1, H_], F32))
        gam_bc = E(nc.sbuf_tensor("Bgam_bc", [128, H_], F32))
        bet_bc = E(nc.sbuf_tensor("Bbet_bc", [128, H_], F32))
        ones_b = E(nc.sbuf_tensor("Bones_b", [1, 128], BF16))
        ones_f = E(nc.sbuf_tensor("Bones_f", [1, 128], F32))
        ones8 = E(nc.sbuf_tensor("Bones8", [8, 1], F32))
        idn_b = E(nc.sbuf_tensor("Bidn_b", [128, 128], BF16))
        idn_f = E(nc.sbuf_tensor("Bidn_f", [128, 128], F32))
        sA = E(nc.sbuf_tensor("BsA", [128, H_], F32))
        sB = E(nc.sbuf_tensor("BsB", [128, H_], F32))
        x2_t = E(nc.sbuf_tensor("Bx2_t", [128, 2, H_], F32))
        out_t = E(nc.sbuf_tensor("Bout_t", [128, 2, H_], F32))
        bn_sb = E(nc.sbuf_tensor("Bbn_sb", [128, 16], F32))
        ab_sb = E(nc.sbuf_tensor("Bab_sb", [128, 2], F32))
        rt = E(nc.sbuf_tensor("Brt", [128, 40], F32))
        tp_ps = E(nc.psum_tensor("Btp_ps", [128, 8, 128], BF16))
        h_ps = [E(nc.psum_tensor(f"Bh_ps{i}", [128, 512], F32)) for i in range(2)]
        y_ps = [E(nc.psum_tensor(f"By_ps{i}", [128, 512], F32)) for i in range(4)]
        d_ps = E(nc.psum_tensor("Bd_ps", [128, 512], F32))


        p.dma("gpsimd", wr_sb[:, :, :], wr.rearrange("(c p) o -> p c o", p=128), inc=lw)
        p.dma("gpsimd", br_r[:, :], br[None, :], inc=lw)
        p.dma("gpsimd", gam_r[:, :], gam[None, :], inc=lw)
        p.dma("gpsimd", bet_r[:, :], bet[None, :], inc=lw)
        p.dma("gpsimd", idn_b[:, :], idn[:, :], inc=lw)
        p.dma("gpsimd", idn_f[:, :], idn[:, :], inc=lw)
        p.dma("gpsimd", b2_sb[:, :], b2[:, :], inc=lw)
        p.i("gpsimd", lambda g: g.memset(ones_b[:, :], 1.0))
        p.i("gpsimd", lambda g: g.memset(ones_f[:, :], 1.0))
        p.i("gpsimd", lambda g: g.memset(ones8[:, :], 1.0))
        p.i("gpsimd", lambda g: g.memset(acc[:, :, :], 0.0), inc=lw, n=1)
        lw0 = p.v(lw)

        p.w("tensor", lw, lw0)
        for nm_r, nm_bc in ((gam_r, gam_bc), (bet_r, bet_bc)):
            for half in range(2):
                sl = slice(half * 512, (half + 1) * 512)
                mm = p.i("tensor", lambda t, sl=sl, r_=nm_r: t.matmul(
                    d_ps[:, 0:512], ones_f[:, :], r_[:, sl], start=True, stop=True), inc=pe)
                p.w("vector", pe, mm)
                cp = p.i("vector", lambda v, sl=sl, bc=nm_bc: v.tensor_copy(bc[:, sl], d_ps[:, 0:512]), inc=dv)
                p.w("tensor", dv, cp)

        for half in range(2):
            bl = p.dma("gpsimd", b1_sb[:, :], b1[:, half * 2048:(half + 1) * 2048], inc=lw)
            p.w("tensor", lw, bl)
            for c in range(16):
                p.i("tensor", lambda t, c=c: t.transpose(
                    d_ps[:, c * 8:(c + 1) * 8], b1_sb[0:8, c * 128:(c + 1) * 128], idn_f[0:8, 0:8]),
                    inc=pe if c == 15 else None)
            p.w("vector", pe)
            for c in range(16):
                cg = half * 16 + c
                p.i("vector", lambda v, c=c, cg=cg: v.tensor_copy(
                    b1_pt[:, :, cg], d_ps[:, c * 8:(c + 1) * 8]), inc=dv if c == 15 else None)
            p.w("gpsimd", dv)
            p.w("tensor", dv)
        p.i("vector", lambda v: v.tensor_scalar(brelu[:, :, :], b1_pt[:, :, :], 0.0, None, ALU.max), inc=dv)
        brelu_dv = p.v(dv)

        b2s_r = sA
        for half in range(2):
            sl = slice(half * 512, (half + 1) * 512)
            mm = p.i("tensor", lambda t, sl=sl: t.matmul(
                d_ps[0:1, 0:512], ones8[:, :], b2_sb[:, sl], start=True, stop=True), inc=pe)
            p.w("vector", pe, mm)
            cp = p.i("vector", lambda v, sl=sl: v.tensor_copy(sA[0:1, sl], d_ps[0:1, 0:512]), inc=dv)
            p.w("tensor", dv, cp)
        p.i("vector", lambda v: v.tensor_copy(xn_tm[0:1, :], sA[0:1, :]), inc=dv)
        p.w("gpsimd", dv)
        b2s_lw = p.dma("gpsimd", dmat[8:9, :], xn_tm[0:1, :], inc=lw)

        xn_read_dv = {}
        for t in range(NT):
            r = t % 2
            if t >= 2:
                p.w("sync", dv, xn_read_dv[t - 2])
            ldv = p.dma("sync", x_t[:, r, :], x_in[t * 128:(t + 1) * 128, :], inc=ld)
            p.w("vector", ld, ldv)
            _ln_ab(p, x_t[:, r, :], bn_sb, ab_sb, 1e-5, dv, ac)
            p.i("vector", lambda v, t=t: v.tensor_copy(ab_all[:, t, :], ab_sb[:, :]))
            p.i("vector", lambda v, rr=r: v.tensor_scalar(
                sB[:, :], x_t[:, rr, :], ab_sb[:, 0:1], ab_sb[:, 1:2], ALU.mult, op1=ALU.add), inc=dv)
            xn_read_dv[t] = p.v(dv)
            p.w("vector", dv)
            _ln_ab(p, sB[:, :], bn_sb, ab_sb, 1e-5, dv, ac)
            p.i("vector", lambda v: v.tensor_scalar(
                sA[:, :], sB[:, :], ab_sb[:, 0:1], ab_sb[:, 1:2], ALU.mult, op1=ALU.add), inc=dv)
            p.w("vector", dv)
            p.i("vector", lambda v: v.tensor_tensor(sB[:, :], sA[:, :], gam_bc[:, :], ALU.mult), inc=dv)
            p.w("vector", dv)
            if t >= 1:
                p.w("vector", pe, xn_tp_pe)
            p.i("vector", lambda v: v.tensor_tensor(xn_tm[:, :], sB[:, :], bet_bc[:, :], ALU.add), inc=dv)
            xn_dv = p.v(dv)
            p.w("tensor", dv, xn_dv)
            if t >= 1:
                p.w("tensor", dv, xnfm_dv)
            for c in range(8):
                p.i("tensor", lambda tt, c=c: tt.transpose(
                    tp_ps[:, c, :], xn_tm[:, c * 128:(c + 1) * 128], idn_b[:, :]),
                    inc=pe if c == 7 else None)
            xn_tp_pe = p.v(pe)
            p.w("vector", pe, xn_tp_pe)
            p.i("vector", lambda v, t=t: v.tensor_copy(
                xn_fm[:, :, t * 128:(t + 1) * 128], tp_ps[:, :, :]), inc=dv)
            xnfm_dv = p.v(dv)

            p.w("tensor", dv, xnfm_dv)
            if t >= 1:
                p.w("tensor", ac, rt_exp_ac)
                p.w("tensor", dv, rt_m1_dv)
            for c in range(8):
                p.i("tensor", lambda tt, c=c, t=t: tt.matmul(
                    d_ps[:, 0:8], xn_fm[:, c, t * 128:(t + 1) * 128], wr_sb[:, c, :],
                    start=(c == 0), stop=False))
            rmm = p.i("tensor", lambda tt: tt.matmul(
                d_ps[:, 0:8], ones_b[:, :], br_r[:, :], start=False, stop=True), inc=pe)
            p.w("vector", pe, rmm)
            rt_m1_dv = p.i("vector", lambda v: v.tensor_reduce(rt[:, 8:9], d_ps[:, 0:8], AX.X, ALU.max), inc=dv)
            p.w("vector", dv)
            p.i("vector", lambda v: v.tensor_scalar(rt[:, 9:10], rt[:, 8:9], -1.0, None, ALU.mult), inc=dv)
            p.w("scalar", dv)
            rt_exp_ac = p.i("scalar", lambda s: s.activation(rt[:, 0:8], d_ps[:, 0:8], AF.Exp,
                                                 bias=rt[:, 9:10], accum_out=rt[:, 10:11]), inc=ac)
            p.w("vector", ac)
            p.i("vector", lambda v: v.reciprocal(rt[:, 11:12], rt[:, 10:11]), inc=dv)
            p.w("vector", dv)
            p.i("vector", lambda v: v.tensor_scalar(rt[:, 0:8], rt[:, 0:8], rt[:, 11:12], None,
                                                    ALU.mult), inc=dv)
            p.w("vector", dv)
            p.i("vector", lambda v: v.tensor_reduce(rt[:, 12:13], rt[:, 0:8], AX.X, ALU.max), inc=dv)
            p.w("vector", dv)
            p.i("vector", lambda v: v.tensor_scalar(rt[:, 16:24], rt[:, 0:8], rt[:, 12:13], None,
                                                    ALU.is_ge), inc=dv)
            p.w("vector", dv)
            p.i("vector", lambda v: v.tensor_reduce(rt[:, 13:14], rt[:, 16:24], AX.X, ALU.add))
            p.i("vector", lambda v: v.tensor_scalar(rt[:, 24:32], rt[:, 16:24], -1e30, None,
                                                    ALU.mult), inc=dv)
            p.w("vector", dv)
            p.i("vector", lambda v: v.tensor_tensor(rt[:, 24:32], rt[:, 0:8], rt[:, 24:32], ALU.add), inc=dv)
            p.w("vector", dv)
            p.i("vector", lambda v: v.tensor_reduce(rt[:, 14:15], rt[:, 24:32], AX.X, ALU.max))
            p.i("vector", lambda v: v.tensor_scalar(rt[:, 15:16], rt[:, 13:14], 2.0, None,
                                                    ALU.is_lt), inc=dv)
            p.w("vector", dv)
            p.i("vector", lambda v: v.tensor_scalar(rt[:, 32:40], rt[:, 0:8], rt[:, 14:15],
                                                    rt[:, 15:16], ALU.is_ge, op1=ALU.mult), inc=dv)
            p.w("vector", dv)
            p.i("vector", lambda v: v.tensor_tensor(rt[:, 32:40], rt[:, 16:24], rt[:, 32:40], ALU.max), inc=dv)
            p.w("vector", dv)
            p.i("vector", lambda v: v.tensor_tensor(rt[:, 32:40], rt[:, 0:8], rt[:, 32:40], ALU.mult), inc=dv)
            p.w("vector", dv)
            p.i("vector", lambda v, t=t: v.tensor_copy(gates[:, t::16], rt[:, 32:40]))
            p.i("vector", lambda v, t=t: v.tensor_reduce(s_all[:, t:t + 1], rt[:, 32:40], AX.X, ALU.add), inc=dv)
            p.w("vector", dv)
            p.w("vector", pe, xn_tp_pe)
            p.i("vector", lambda v, t=t: v.tensor_scalar(
                xn_tm[:, 0:8], rt[:, 32:40], s_all[:, t:t + 1], -1.0, ALU.subtract, op1=ALU.mult), inc=dv)
            p.i("vector", lambda v, t=t: v.tensor_copy(xn_tm[:, 8:9], s_all[:, t:t + 1]), inc=dv)
            gaug_dv = p.v(dv)
            p.w("tensor", dv, gaug_dv)
            gt_mm = p.i("tensor", lambda tt: tt.transpose(
                tp_ps[0:16, 0, :], xn_tm[:, 0:16], idn_b[:, :]), inc=pe)
            xn_tp_pe = gt_mm
            p.w("vector", pe, gt_mm)
            p.i("vector", lambda v, t=t: v.tensor_copy(
                gT[:, t * 128:(t + 1) * 128], tp_ps[0:9, 0, :]), inc=dv)
            gT_dv = p.v(dv)
            p.w("tensor", dv, gT_dv)
            rt_read_dv = p.v(dv)

        pass1_dv = p.v(dv)
        pass1_pe = p.v(pe)
        p.w("scalar", dv, pass1_dv)
        p.w("vector", lw, lw0)
        p.w("tensor", ac, p.v(ac))

        st_val = 0
        o_dv_hist = {}
        w1_lw = w2_lw = None
        dcol_dv = None
        relu_ac = {}
        w1s_free_pe = pass1_pe
        w2s_free_pe = pass1_pe
        yps_free_dv = [pass1_dv] * 4
        dps_free_dv = pass1_dv
        hps_free_ac = [0, 0]
        u_free_pe = pass1_pe
        for Hh in range(2):
            for e in range(8):
                for s in range(NSL):
                    fsl = slice(s * FS, (s + 1) * FS)
                    p.w("gpsimd", pe, w1s_free_pe)
                    w1_lw = p.dma("gpsimd", w1s[:, :, :],
                                  w1[e, :, fsl].rearrange("(c p) f -> p c f", p=128), inc=lw)
                    p.w("gpsimd", pe, w2s_free_pe)
                    w2_lw = p.dma("gpsimd", w2s[:, :, :],
                                  w2[e, fsl, :].rearrange("(c p) o -> p c o", p=128), inc=lw)
                    for g in range(2):
                        gcols = slice(Hh * 1024 + g * 512, Hh * 1024 + (g + 1) * 512)
                        p.w("tensor", lw, w1_lw)
                        p.w("tensor", pe, u_free_pe)
                        for ft in range(FS // 128):
                            hb = ft % 2
                            p.w("tensor", ac, hps_free_ac[hb])
                            for c in range(8):
                                p.i("tensor", lambda tt, c=c, ft=ft, hb=hb, gcols=gcols: tt.matmul(
                                    h_ps[hb][:, 0:512], w1s[:, c, ft * 128:(ft + 1) * 128],
                                    xn_fm[:, c, gcols], start=(c == 0), stop=(c == 7)),
                                    inc=pe if c == 7 else None)
                            hmm = p.v(pe)
                            p.w("scalar", pe, hmm)
                            cg = s * (FS // 128) + ft
                            hps_free_ac[hb] = p.i("scalar", lambda sc, ft=ft, hb=hb, e=e, cg=cg: sc.activation(
                                u_sb[:, ft, :], h_ps[hb][:, 0:512], AF.Relu,
                                bias=b1_pt[:, e, cg:cg + 1]), inc=ac)
                        relu_done = p.v(ac)
                        if g == 1:
                            w1s_free_pe = p.v(pe)
                        p.w("tensor", ac, relu_done)
                        p.w("tensor", lw, w2_lw)
                        for half in range(2):
                            osl = slice(half * 512, (half + 1) * 512)
                            for tt_ in range(4):
                                tile = Hh * 8 + g * 4 + tt_
                                p.w("tensor", dv, yps_free_dv[tt_])
                                for fc in range(FS // 128):
                                    p.i("tensor", lambda tt, fc=fc, tt_=tt_, osl=osl: tt.matmul(
                                        y_ps[tt_][:, 0:512], u_sb[:, fc, tt_ * 128:(tt_ + 1) * 128],
                                        w2s[:, fc, osl], start=(fc == 0), stop=(fc == FS // 128 - 1)),
                                        inc=pe if fc == FS // 128 - 1 else None)
                                ymm = p.v(pe)
                                p.w("vector", pe, ymm)
                                gcol = gates[:, e * 16 + tile:e * 16 + tile + 1]
                                p.i("vector", lambda v, tt_=tt_, gcol=gcol: v.tensor_scalar(
                                    sB[:, 0:512], y_ps[tt_][:, 0:512], gcol, None, ALU.mult), inc=dv)
                                p.w("vector", dv)
                                p.i("vector", lambda v, tile=tile, osl=osl: v.tensor_tensor(
                                    acc[:, tile % 8, osl], acc[:, tile % 8, osl], sB[:, 0:512],
                                    ALU.add), inc=dv)
                                yps_free_dv[tt_] = p.v(dv)
                        u_free_pe = p.v(pe)
                    if Hh == 0:
                        for ot in range(8):
                            p.w("tensor", dv, dps_free_dv)
                            for fc in range(FS // 128):
                                p.i("tensor", lambda tt, fc=fc, ot=ot, e=e, s=s: tt.matmul(
                                    d_ps[:, 0:1], w2s[:, fc, ot * 128:(ot + 1) * 128],
                                    brelu[:, e, s * (FS // 128) + fc:s * (FS // 128) + fc + 1],
                                    start=(fc == 0), stop=(fc == FS // 128 - 1)),
                                    inc=pe if fc == FS // 128 - 1 else None)
                            dmm = p.v(pe)
                            p.w("vector", pe, dmm)
                            if s == 0:
                                p.i("vector", lambda v, ot=ot, e=e: v.tensor_copy(
                                    dcol[:, ot, e:e + 1], d_ps[:, 0:1]), inc=dv)
                            else:
                                p.i("vector", lambda v, ot=ot, e=e: v.tensor_tensor(
                                    dcol[:, ot, e:e + 1], dcol[:, ot, e:e + 1], d_ps[:, 0:1],
                                    ALU.add), inc=dv)
                            dps_free_dv = p.v(dv)
                    w2s_free_pe = p.v(pe)

            if Hh == 0:
                p.w("tensor", dv, p.v(dv))
                for ot in range(8):
                    dt_mm = p.i("tensor", lambda tt, ot=ot: tt.transpose(
                        d_ps[0:8, 0:128], dcol[:, ot, :], idn_f[:, :]), inc=pe)
                    p.w("vector", pe, dt_mm)
                    p.i("vector", lambda v, ot=ot: v.tensor_copy(
                        dmat[0:8, ot * 128:(ot + 1) * 128], d_ps[0:8, 0:128]), inc=dv)
                    p.w("tensor", dv)
                dmat_dv = p.v(dv)

            p.w("tensor", lw, b2s_lw)
            for ti in range(8):
                tile = Hh * 8 + ti
                r = ti % 2
                if ti >= 2:
                    p.w("sync", dv, o_dv_hist[tile - 2])
                lxv = p.dma("sync", x2_t[:, r, :], x_in[tile * 128:(tile + 1) * 128, :], inc=lx)
                if ti == 0:
                    p.w("tensor", dv, dmat_dv)
                p.w("tensor", dv, yps_free_dv[0])
                p.w("tensor", dv, yps_free_dv[1])
                for half in range(2):
                    osl = slice(half * 512, (half + 1) * 512)
                    p.i("tensor", lambda tt, half=half, tile=tile, osl=osl: tt.matmul(
                        y_ps[half][:, 0:512], gT[:, tile * 128:(tile + 1) * 128], dmat[:, osl],
                        start=True, stop=True), inc=pe if half == 1 else None)
                cmm = p.v(pe)
                p.w("vector", pe, cmm)
                p.w("vector", lx, lxv)
                p.i("vector", lambda v, r=r, tile=tile: v.tensor_scalar(
                    sA[:, :], x2_t[:, r, :], ab_all[:, tile, 0:1], ab_all[:, tile, 1:2],
                    ALU.mult, op1=ALU.add), inc=dv)
                p.w("vector", dv)
                for half in range(2):
                    osl = slice(half * 512, (half + 1) * 512)
                    p.i("vector", lambda v, half=half, osl=osl, ti=ti: v.tensor_tensor(
                        sB[:, osl], acc[:, ti, osl], y_ps[half][:, 0:512], ALU.add),
                        inc=dv if half == 1 else None)
                yps_free_dv[0] = yps_free_dv[1] = p.v(dv)
                p.w("vector", dv)
                p.i("vector", lambda v: v.tensor_tensor(sB[:, :], sA[:, :], sB[:, :], ALU.add), inc=dv)
                p.w("vector", dv)
                _ln_ab(p, sB[:, :], bn_sb, ab_sb, 1e-5, dv, ac)
                p.i("vector", lambda v: v.tensor_scalar(
                    sA[:, :], sB[:, :], ab_sb[:, 0:1], ab_sb[:, 1:2], ALU.mult, op1=ALU.add), inc=dv)
                p.w("vector", dv)
                if tile >= 2:
                    p.w("vector", st, st_val - 16)
                p.i("vector", lambda v, r=r: v.tensor_tensor(
                    out_t[:, r, :], sA[:, :], x2_t[:, r, :], ALU.add), inc=dv)
                o_dv = p.v(dv)
                o_dv_hist[tile] = o_dv
                p.w("sync", dv, o_dv)
                st_val = p.dma("sync", y[tile * 128:(tile + 1) * 128, :], out_t[:, r, :], inc=st)
            if Hh == 0:
                p.w("gpsimd", dv, p.v(dv))
                p.i("gpsimd", lambda g: g.memset(acc[:, :, :], 0.0), inc=lw, n=1)
                acc_zero_lw = p.v(lw)
                p.w("vector", lw, acc_zero_lw)

        p.w("sync", st, st_val)


def emit_l1_ffn(p, nc, E, x_in, xb, msel, wr, wk, wv, mur, muk, sh1, sh2, idn, y):
    ld, lw, pe, dv, ac, st, lx, ks, po = "ld", "lw", "pe", "dv", "ac", "st", "lx", "ks", "po"
    if True:
        x_t = E(nc.sbuf_tensor("Dx_t", [128, 2, H_], F32))
        p_bf = E(nc.sbuf_tensor("Dp_bf", [128, 2, H_], BF16))
        pb0 = E(nc.sbuf_tensor("Dpb0", [128, H_], BF16))
        p_f = E(nc.sbuf_tensor("Dp_f", [128, H_], F32))
        w_sb = {}
        for nm in ("wr", "wk", "wv"):
            w_sb[nm] = E(nc.sbuf_tensor(nm + "_sb" + "D", [128, 8, H_], BF16))
        mur_bc = E(nc.sbuf_tensor("Dmur_bc", [128, H_], F32))
        muk_bc = E(nc.sbuf_tensor("Dmuk_bc", [128, H_], F32))
        mur_r = E(nc.sbuf_tensor("Dmur_r", [1, H_], F32))
        muk_r = E(nc.sbuf_tensor("Dmuk_r", [1, H_], F32))
        sh1_b = E(nc.sbuf_tensor("Dsh1_b", [128, 128], BF16))
        sh2_b = E(nc.sbuf_tensor("Dsh2_b", [128, 128], BF16))
        idn_b = E(nc.sbuf_tensor("Didn_b", [128, 128], BF16))
        ones_f = E(nc.sbuf_tensor("Dones_f", [1, 128], F32))
        xb_sb = E(nc.sbuf_tensor("Dxb_sb", [1, H_], F32))
        msel_sb = E(nc.sbuf_tensor("Dmsel_sb", [1, 1], F32))
        sA = E(nc.sbuf_tensor("DsA", [128, H_], F32))
        sB = E(nc.sbuf_tensor("DsB", [128, H_], F32))
        r_tm = E(nc.sbuf_tensor("Dr_tm", [128, H_], F32))
        xr_tm = E(nc.sbuf_tensor("Dxr_tm", [128, H_], BF16))
        xk_tm = E(nc.sbuf_tensor("Dxk_tm", [128, H_], BF16))
        k2_tm = E(nc.sbuf_tensor("Dk2_tm", [128, H_], BF16))
        fm = E(nc.sbuf_tensor("Dfm", [128, 8, 128], BF16))
        out_t = E(nc.sbuf_tensor("Dout_t", [128, 2, H_], F32))
        bn_sb = E(nc.sbuf_tensor("Dbn_sb", [128, 16], F32))
        ab_sb = E(nc.sbuf_tensor("Dab_sb", [128, 2], F32))
        xp_ps = E(nc.psum_tensor("Dxp_ps", [128, H_], F32))
        pa_ps = E(nc.psum_tensor("Dpa_ps", [128, H_], F32))
        pb_ps = E(nc.psum_tensor("Dpb_ps", [128, H_], F32))
        tp_ps = E(nc.psum_tensor("Dtp_ps", [128, 8, 128], BF16))


        for nm, srcw in (("wr", wr), ("wk", wk), ("wv", wv)):
            p.dma("gpsimd", w_sb[nm][:, :, :], srcw.rearrange("(c p) o -> p c o", p=128), inc=lw)
        p.dma("gpsimd", sh1_b[:, :], sh1[:, :], inc=lw)
        p.dma("gpsimd", sh2_b[:, :], sh2[:, :], inc=lw)
        p.dma("gpsimd", idn_b[:, :], idn[:, :], inc=lw)
        p.dma("gpsimd", mur_r[:, :], mur[None, :], inc=lw)
        p.dma("gpsimd", muk_r[:, :], muk[None, :], inc=lw)
        p.dma("gpsimd", xb_sb[:, :], xb[:, :], inc=lw)
        p.dma("gpsimd", msel_sb[:, :], msel[:, :], inc=lw)
        p.i("gpsimd", lambda g: g.memset(ones_f[:, :], 1.0))
        p.i("gpsimd", lambda g: g.memset(pb0[:, :], 0.0), inc=lw, n=1)
        lw0 = p.v(lw)
        p.w("tensor", lw, lw0)
        for r_, bc in ((mur_r, mur_bc), (muk_r, muk_bc)):
            for half in range(2):
                sl = slice(half * 512, (half + 1) * 512)
                mm = p.i("tensor", lambda t, sl=sl, r_=r_: t.matmul(
                    pa_ps[:, 0:512], ones_f[:, :], r_[:, sl], start=True, stop=True), inc=pe)
                p.w("vector", pe, mm)
                cp = p.i("vector", lambda v, sl=sl, bc=bc: v.tensor_copy(bc[:, sl], pa_ps[:, 0:512]), inc=dv)
                p.w("tensor", dv, cp)
        p.w("vector", lw, lw0)
        _ln_ab(p, xb_sb[:, :], bn_sb[0:1, :], ab_sb[0:1, :], 1e-5, dv, ac)
        p.i("vector", lambda v: v.tensor_scalar(
            sA[0:1, :], xb_sb[:, :], ab_sb[0:1, 0:1], ab_sb[0:1, 1:2], ALU.mult, op1=ALU.add), inc=dv)
        p.w("vector", dv)
        p.i("vector", lambda v: v.tensor_scalar(
            sA[0:1, :], sA[0:1, :], msel_sb[0:1, 0:1], None, ALU.mult), inc=dv)
        p.w("vector", dv)
        p.i("vector", lambda v: v.tensor_copy(xr_tm[0:1, :], sA[0:1, :]), inc=dv)
        p.w("gpsimd", dv)
        pb0_lw = p.dma("gpsimd", pb0[127:128, :], xr_tm[0:1, :], inc=lw)

        st_val = 0
        o_dv_hist = {}
        p_read_pe = 0
        prev_ps_read = 0
        xr_read_pe = 0
        p.w("vector", lw, pb0_lw)
        for t in range(NT):
            r = t % 2
            if t >= 2:
                p.w("sync", dv, o_dv_hist[t - 2])
            ldv = p.dma("sync", x_t[:, r, :], x_in[t * 128:(t + 1) * 128, :], inc=ld)
            p.w("vector", ld, ldv)
            _ln_ab(p, x_t[:, r, :], bn_sb, ab_sb, 1e-5, dv, ac)
            p.i("vector", lambda v, rr=r: v.tensor_scalar(
                p_f[:, :], x_t[:, rr, :], ab_sb[:, 0:1], ab_sb[:, 1:2], ALU.mult, op1=ALU.add), inc=dv)
            p.w("vector", dv)
            if t >= 2:
                p.w("vector", pe, p_read_pe)
            p.i("vector", lambda v, rr=r: v.tensor_copy(p_bf[:, rr, :], p_f[:, :]), inc=dv)
            pbf_dv = p.v(dv)
            if t == 0:
                p.w("tensor", lw, pb0_lw)
            p.w("tensor", dv, pbf_dv)
            p.w("tensor", dv, prev_ps_read)
            prev_ap = pb0[:, :] if t == 0 else p_bf[:, (t - 1) % 2, :]
            for half in range(2):
                sl = slice(half * 512, (half + 1) * 512)
                p.i("tensor", lambda tt, sl=sl, rr=r: tt.matmul(
                    xp_ps[:, sl], sh1_b[:, :], p_bf[:, rr, sl], start=True, stop=False))
                p.i("tensor", lambda tt, sl=sl, pa=prev_ap: tt.matmul(
                    xp_ps[:, sl], sh2_b[:, :], pa[:, sl], start=False, stop=True),
                    inc=pe if half == 1 else None)
            p_read_pe = p.v(pe)
            xp_pe = p_read_pe
            p.w("vector", pe, xp_pe)
            p.i("vector", lambda v: v.tensor_tensor(sA[:, :], p_f[:, :], xp_ps[:, :], ALU.subtract), inc=dv)
            p.w("vector", dv)
            p.i("vector", lambda v: v.tensor_tensor(sB[:, :], sA[:, :], mur_bc[:, :], ALU.mult), inc=dv)
            p.w("vector", dv)
            p.w("vector", pe, xr_read_pe)
            p.i("vector", lambda v: v.tensor_tensor(xr_tm[:, :], sB[:, :], xp_ps[:, :], ALU.add), inc=dv)
            p.w("vector", dv)
            p.i("vector", lambda v: v.tensor_tensor(sB[:, :], sA[:, :], muk_bc[:, :], ALU.mult), inc=dv)
            p.w("vector", dv)
            p.i("vector", lambda v: v.tensor_tensor(xk_tm[:, :], sB[:, :], xp_ps[:, :], ALU.add), inc=dv)
            xrk_dv = p.v(dv)
            p.w("tensor", dv, xrk_dv)
            for c in range(8):
                p.i("tensor", lambda tt, c=c: tt.transpose(
                    tp_ps[:, c, :], xr_tm[:, c * 128:(c + 1) * 128], idn_b[:, :]),
                    inc=pe if c == 7 else None)
            p.w("vector", pe)
            p.i("vector", lambda v: v.tensor_copy(fm[:, :, :], tp_ps[:, :, :]), inc=dv)
            fm_dv = p.v(dv)
            p.w("tensor", dv, fm_dv)
            for half in range(2):
                sl = slice(half * 512, (half + 1) * 512)
                for c in range(8):
                    p.i("tensor", lambda tt, c=c, sl=sl: tt.matmul(
                        pa_ps[:, sl], fm[:, c, :], w_sb["wr"][:, c, sl], start=(c == 0), stop=(c == 7)),
                        inc=pe if (c == 7 and half == 1) else None)
            rp_pe = p.v(pe)
            for c in range(8):
                p.i("tensor", lambda tt, c=c: tt.transpose(
                    tp_ps[:, c, :], xk_tm[:, c * 128:(c + 1) * 128], idn_b[:, :]),
                    inc=pe if c == 7 else None)
            ktp_pe = p.v(pe)
            xr_read_pe = ktp_pe
            p.w("vector", pe, ktp_pe)
            p.i("vector", lambda v: v.tensor_copy(fm[:, :, :], tp_ps[:, :, :]), inc=dv)
            fmk_dv = p.v(dv)
            p.w("tensor", dv, fmk_dv)
            for half in range(2):
                sl = slice(half * 512, (half + 1) * 512)
                for c in range(8):
                    p.i("tensor", lambda tt, c=c, sl=sl: tt.matmul(
                        pb_ps[:, sl], fm[:, c, :], w_sb["wk"][:, c, sl], start=(c == 0), stop=(c == 7)),
                        inc=pe if (c == 7 and half == 1) else None)
            kp_pe = p.v(pe)
            p.w("scalar", pe, rp_pe)
            p.i("scalar", lambda s: s.activation(r_tm[:, :], pa_ps[:, :], AF.Sigmoid), inc=ac)
            r_ac = p.v(ac)
            p.w("vector", pe, kp_pe)
            p.i("vector", lambda v: v.tensor_scalar(sA[:, :], pb_ps[:, :], 0.0, None, ALU.max), inc=dv)
            p.w("scalar", dv)
            p.i("scalar", lambda s: s.activation(k2_tm[:, :], sA[:, :], AF.Square), inc=ac)
            k2_ac = p.v(ac)
            p.w("tensor", ac, k2_ac)
            for c in range(8):
                p.i("tensor", lambda tt, c=c: tt.transpose(
                    tp_ps[:, c, :], k2_tm[:, c * 128:(c + 1) * 128], idn_b[:, :]),
                    inc=pe if c == 7 else None)
            p.w("vector", pe)
            p.i("vector", lambda v: v.tensor_copy(fm[:, :, :], tp_ps[:, :, :]), inc=dv)
            fmv_dv = p.v(dv)
            p.w("tensor", dv, fmv_dv)
            p.w("tensor", ac, r_ac)
            for half in range(2):
                sl = slice(half * 512, (half + 1) * 512)
                for c in range(8):
                    p.i("tensor", lambda tt, c=c, sl=sl: tt.matmul(
                        pa_ps[:, sl], fm[:, c, :], w_sb["wv"][:, c, sl], start=(c == 0), stop=(c == 7)),
                        inc=pe if (c == 7 and half == 1) else None)
            vp_pe = p.v(pe)
            p.w("vector", pe, vp_pe)
            p.w("vector", ac, r_ac)
            p.i("vector", lambda v: v.tensor_tensor(sB[:, :], r_tm[:, :], pa_ps[:, :], ALU.mult), inc=dv)
            prev_ps_read = p.v(dv)
            p.w("vector", dv)
            _ln_ab(p, sB[:, :], bn_sb, ab_sb, 1e-5, dv, ac)
            p.i("vector", lambda v: v.tensor_scalar(
                sA[:, :], sB[:, :], ab_sb[:, 0:1], ab_sb[:, 1:2], ALU.mult, op1=ALU.add), inc=dv)
            p.w("vector", dv)
            if t >= 2:
                p.w("vector", st, st_val - 16)
            p.i("vector", lambda v, rr=r: v.tensor_tensor(
                out_t[:, rr, :], sA[:, :], x_t[:, rr, :], ALU.add), inc=dv)
            o_dv = p.v(dv)
            o_dv_hist[t] = o_dv
            p.w("sync", dv, o_dv)
            st_val = p.dma("sync", y[t * 128:(t + 1) * 128, :], out_t[:, r, :], inc=st)

        p.w("sync", st, st_val)


def emit_l1_attn2(p, nc, E, x_own, kfull, vfull, wqkv, bqkv, qn, kn, wpr, bpr, pbrow, idn, bndd, y):
    ld, lw, pe, dv, ac, st, lx, ks, po = "ld", "lw", "pe", "dv", "ac", "st", "lx", "ks", "po"
    NQB = 4
    if True:
        vh = E(nc.sbuf_tensor("Cvh", [128, 32, NH, 65], BF16))
        qf = E(nc.sbuf_tensor("Cqf", [128, 8, T], BF16))
        kf = E(nc.sbuf_tensor("Ckf", [128, 1, S_], BF16))
        cmax = E(nc.sbuf_tensor("Ccmax", [128, NT, NH], F32))
        cbias = E(nc.sbuf_tensor("Ccbias", [128, 32], F32))
        cb_r = E(nc.sbuf_tensor("Ccb_r", [1, 32], F32))
        idn_b = E(nc.sbuf_tensor("Cidn_b", [128, 128], BF16))
        ones_f = E(nc.sbuf_tensor("Cones_f", [1, 128], F32))
        ones_b = E(nc.sbuf_tensor("Cones_b", [1, 128], BF16))
        ones_b2 = E(nc.sbuf_tensor("Cones_b2", [1, 1], BF16))
        w2col = E(nc.sbuf_tensor("Cw2col", [128, 1], F32))
        w2sq = E(nc.sbuf_tensor("Cw2sq", [128, H_], F32))
        qnr = E(nc.sbuf_tensor("Cqnr", [1, 2 * HD], F32))
        knr = E(nc.sbuf_tensor("Cknr", [1, 2 * HD], F32))
        wsh = E(nc.sbuf_tensor("Cwsh", [128, 8, 2048], BF16))
        bq_r = E(nc.sbuf_tensor("Cbq_r", [1, H_], BF16))
        bpr_r = E(nc.sbuf_tensor("Cbpr_r", [1, H_], BF16))
        x_t = E(nc.sbuf_tensor("Cx_t", [128, 1, H_], F32))
        xn_tm = E(nc.sbuf_tensor("Cxn_tm", [128, H_], BF16))
        xn_fm = E(nc.sbuf_tensor("Cxn_fm", [128, 8, 128], BF16))
        sA = E(nc.sbuf_tensor("CsA", [128, H_], F32))
        sB = E(nc.sbuf_tensor("CsB", [128, H_], F32))
        ktm = E(nc.sbuf_tensor("Cktm", [128, H_], BF16))
        rq = E(nc.sbuf_tensor("Crq", [128, NH], F32))
        rsc = E(nc.sbuf_tensor("Crsc", [128, NH * 2], F32))
        bn_sb = E(nc.sbuf_tensor("Cbn_sb", [128, 16], F32))
        ab_sb = E(nc.sbuf_tensor("Cab_sb", [128, 2], F32))
        u_sb = E(nc.sbuf_tensor("Cu_sb", [128, 2, H_], BF16))
        pb_sb = E(nc.sbuf_tensor("Cpb_sb", [1, H_], F32))
        bnd_fm = E(nc.sbuf_tensor("Cbnd_fm", [128, 8, 1], BF16))
        arow_fm = E(nc.sbuf_tensor("Carow_fm", [128, 8, 1], BF16))
        ubnd = E(nc.sbuf_tensor("Cubnd", [128, 64], BF16))
        obnd = E(nc.sbuf_tensor("Cobnd", [65, NH], BF16))
        arow = E(nc.sbuf_tensor("Carow", [1, H_], BF16))
        hrow = E(nc.sbuf_tensor("Chrow", [1, 65], F32))
        cbb = E(nc.sbuf_tensor("Ccbb", [128, 2], F32))
        ahm = [E(nc.sbuf_tensor(f"Cahm{h}", [65, 512], BF16)) for h in range(NH)]
        at_aug = E(nc.sbuf_tensor("Cat_aug", [128, NH, 65], BF16))
        at2 = E(nc.sbuf_tensor("Cat2", [128, H_], BF16))
        a_fm = E(nc.sbuf_tensor("Ca_fm", [128, 8, 128], BF16))
        den_r = E(nc.sbuf_tensor("Cden_r", [128, NH], F32))
        x2_t = E(nc.sbuf_tensor("Cx2_t", [128, 1, H_], F32))
        out_t = E(nc.sbuf_tensor("Cout_t", [128, 1, H_], F32))
        sc_ps = [E(nc.psum_tensor(f"Csc_ps{i}", [128, 1024], F32)) for i in range(2)]
        o_psA = E(nc.psum_tensor("Co_psA", [128, 512], F32))
        o_psB = E(nc.psum_tensor("Co_psB", [128, 512], F32))
        tp_ps = E(nc.psum_tensor("Ctp_ps", [128, 8, 128], BF16))


        p.dma("gpsimd", idn_b[:, :], idn[:, :], inc=lw)
        p.dma("gpsimd", qnr[:, 0:HD], qn[None, :], inc=lw)
        p.dma("gpsimd", qnr[:, HD:2 * HD], qn[None, :], inc=lw)
        p.dma("gpsimd", knr[:, 0:HD], kn[None, :], inc=lw)
        p.dma("gpsimd", knr[:, HD:2 * HD], kn[None, :], inc=lw)
        p.dma("gpsimd", bq_r[:, :], bqkv[None, 0:H_], inc=lw)
        p.dma("gpsimd", pb_sb[:, :], pbrow[:, :], inc=lw)
        p.i("gpsimd", lambda g: g.memset(ones_f[:, :], 1.0))
        p.i("gpsimd", lambda g: g.memset(ones_b[:, :], 1.0))
        p.dma("gpsimd", vh[:, :, :, :], vfull[:, :, :, :], inc=lw)
        p.i("gpsimd", lambda g: g.memset(ones_b2[:, :], 1.0), inc=lw, n=1)
        lw0 = p.v(lw)
        p.w("vector", lw, lw0)
        p.i("vector", lambda v: v.tensor_tensor(qnr[:, :], qnr[:, :], knr[:, :], ALU.mult), inc=dv)
        p.w("vector", dv)
        p.i("vector", lambda v: v.tensor_scalar(qnr[:, :], qnr[:, :], 0.125, None, ALU.mult), inc=dv)
        w2r_dv = p.v(dv)
        p.w("tensor", dv, w2r_dv)
        mm = p.i("tensor", lambda t: t.transpose(sc_ps[0][:, 0:1], qnr[:, :], ones_f[0:1, 0:1]), inc=pe)
        p.w("vector", pe, mm)
        p.i("vector", lambda v: v.tensor_copy(w2col[:, :], sc_ps[0][:, 0:1]), inc=dv)
        p.w("vector", dv)
        p.i("vector", lambda v: v.tensor_tensor(qnr[:, :], qnr[:, :], qnr[:, :], ALU.mult), inc=dv)
        p.w("tensor", dv)
        for c in range(8):
            p.i("tensor", lambda t, c=c: t.matmul(sc_ps[0][:, c * 128:(c + 1) * 128],
                                                  ones_f[:, :], qnr[:, :],
                                                  start=True, stop=True), inc=pe if c == 7 else None)
        p.w("vector", pe)
        p.i("vector", lambda v: v.tensor_copy(w2sq[:, :], sc_ps[0][:, 0:1024]), inc=dv)
        w2sq_dv = p.v(dv)
        p.w("tensor", dv, w2sq_dv)

        p.w("gpsimd", pe, p.v(pe))
        p.w("gpsimd", dv, p.v(dv))
        xn_read_dv = {}
        tp_pe = 0
        prev_ps_read = 0
        wl = p.dma("gpsimd", wsh[:, :, 0:1024], wqkv[:, 0:H_].rearrange("(c p) o -> p c o", p=128), inc=lw)
        p.dma("gpsimd", wsh[:, :, 1024:2048], wpr.rearrange("(c p) o -> p c o", p=128), inc=lw)
        wl = p.dma("gpsimd", bpr_r[:, :], bpr[None, :], inc=lw)
        p.w("tensor", lw, wl)
        qtp_pe = 0
        for t in range(NT):
            r = t % 2
            if t >= 1:
                p.w("sync", dv, xn_read_dv[t - 1])
            ldv = p.dma("sync", x_t[:, 0, :], x_own[t * 128:(t + 1) * 128, :], inc=ld)
            p.w("vector", ld, ldv)
            _ln_ab(p, x_t[:, 0, :], bn_sb, ab_sb, 1e-5, dv, ac)
            p.w("vector", pe, tp_pe)
            p.i("vector", lambda v, rr=r: v.tensor_scalar(
                xn_tm[:, :], x_t[:, 0, :], ab_sb[:, 0:1], ab_sb[:, 1:2], ALU.mult, op1=ALU.add), inc=dv)
            xn_dv = p.v(dv)
            xn_read_dv[t] = xn_dv
            p.w("tensor", dv, xn_dv)
            for c in range(8):
                p.i("tensor", lambda tt, c=c: tt.transpose(
                    tp_ps[:, c, :], xn_tm[:, c * 128:(c + 1) * 128], idn_b[:, :]),
                    inc=pe if c == 7 else None)
            tp_pe = p.v(pe)
            p.w("vector", pe, tp_pe)
            p.i("vector", lambda v: v.tensor_copy(xn_fm[:, :, :], tp_ps[:, :, :]), inc=dv)
            xnfm_dv = p.v(dv)
            p.w("tensor", dv, xnfm_dv)
            p.w("tensor", dv, prev_ps_read)
            for nh in range(2):
                osl = slice(nh * 512, (nh + 1) * 512)
                for c in range(8):
                    p.i("tensor", lambda tt, c=c, osl=osl: tt.matmul(
                        sc_ps[0][:, osl], xn_fm[:, c, :], wsh[:, c, osl], start=(c == 0), stop=False))
                p.i("tensor", lambda tt, osl=osl: tt.matmul(
                    sc_ps[0][:, osl], ones_b[:, :], bq_r[:, osl], start=False, stop=True), inc=pe)
            proj_pe = p.v(pe)
            p.w("scalar", pe, proj_pe)
            p.i("scalar", lambda s: s.activation(sA[:, :], sc_ps[0][:, :], AF.Square), inc=ac)
            p.w("vector", ac)
            p.i("vector", lambda v: v.tensor_reduce(
                rsc[:, 0:NH], sA.rearrange("p (h d) -> p h d", h=NH), AX.X, ALU.add), inc=dv)
            p.w("vector", dv)
            p.i("vector", lambda v: v.tensor_scalar(rsc[:, NH:2 * NH], rsc[:, 0:NH], 1.0 / HD, 1e-6,
                                                    ALU.mult, op1=ALU.add), inc=dv)
            p.w("scalar", dv)
            p.i("scalar", lambda s: s.activation(rsc[:, NH:2 * NH], rsc[:, NH:2 * NH], AF.Ln), inc=ac)
            p.w("scalar", ac)
            p.i("scalar", lambda s: s.activation(rq[:, :], rsc[:, NH:2 * NH], AF.Exp,
                                                 scale=-0.5), inc=ac)
            p.w("vector", ac)
            for h in range(NH):
                p.i("vector", lambda v, h=h: v.tensor_scalar(
                    sB[:, h * HD:(h + 1) * HD], sc_ps[0][:, h * HD:(h + 1) * HD],
                    rq[:, h:h + 1], None, ALU.mult), inc=dv if h == NH - 1 else None)
            qh_dv = p.v(dv)
            p.w("scalar", dv, qh_dv)
            p.i("scalar", lambda s: s.activation(sA[:, :], sB[:, :], AF.Square), inc=ac)
            p.w("vector", ac)
            p.i("vector", lambda v: v.tensor_tensor(sA[:, :], sA[:, :], w2sq[:, :], ALU.mult), inc=dv)
            p.w("vector", dv)
            p.i("vector", lambda v, t=t: v.tensor_reduce(
                cmax[:, t, :], sA.rearrange("p (h d) -> p h d", h=NH), AX.X, ALU.add), inc=dv)
            p.w("scalar", dv)
            p.i("scalar", lambda s, t=t: s.activation(cmax[:, t, :], cmax[:, t, :], AF.Ln,
                                                      scale=float(HD)), inc=ac)
            p.w("scalar", ac)
            p.i("scalar", lambda s, t=t: s.activation(cmax[:, t, :], cmax[:, t, :], AF.Exp,
                                                      scale=0.5), inc=ac)
            cm_ac = p.v(ac)
            p.w("vector", pe, qtp_pe)
            p.i("vector", lambda v: v.tensor_copy(ktm[:, :], sB[:, :]), inc=dv)
            q_dv = p.v(dv)
            p.w("tensor", dv, q_dv)
            for hp in range(8):
                p.i("tensor", lambda tt, hp=hp: tt.transpose(
                    tp_ps[:, hp, :], ktm[:, hp * 128:(hp + 1) * 128], idn_b[:, :]),
                    inc=pe if hp == 7 else None)
            qtp_pe = p.v(pe)
            p.w("vector", pe, qtp_pe)
            p.i("vector", lambda v, t=t: v.tensor_scalar(
                qf[:, :, t * 128:(t + 1) * 128], tp_ps[:, :, :], w2col[:, :], None, ALU.mult), inc=dv)
            prev_ps_read = p.v(dv)
        pass1_dv = p.v(dv)
        pass1_ac = p.v(ac)

        p.w("gpsimd", ac, pass1_ac)
        p.w("gpsimd", dv, pass1_dv)
        for hp in range(8):
            for qb in range(NQB):
                p.i("gpsimd", lambda g, hp=hp, qb=qb: g.tensor_reduce(
                    cb_r[:, hp * 4 + qb:hp * 4 + qb + 1],
                    cmax[:, qb * 4:(qb + 1) * 4, 2 * hp:2 * hp + 2], AX.XYZWC, ALU.max),
                    inc=po if (hp == 7 and qb == NQB - 1) else None)
        p.w("vector", po)
        p.i("vector", lambda v: v.tensor_scalar(cb_r[:, :], cb_r[:, :], -1.0, None, ALU.mult), inc=dv)
        p.w("tensor", dv)
        cbm = p.i("tensor", lambda t: t.matmul(sc_ps[0][:, 0:32], ones_f[:, :], cb_r[:, :],
                                               start=True, stop=True), inc=pe)
        p.w("vector", pe, cbm)
        p.i("vector", lambda v: v.tensor_copy(cbias[:, :], sc_ps[0][:, 0:32]), inc=dv)
        cb_dv = p.v(dv)
        p.w("tensor", dv, cb_dv)


        p.w("tensor", dv, p.v(dv))
        p.w("tensor", ac, p.v(ac))
        _ln_ab(p, pb_sb[:, :], bn_sb[0:1, :], ab_sb[0:1, :], 1e-5, dv, ac)
        p.i("vector", lambda v: v.tensor_scalar(
            sA[0:1, :], pb_sb[:, :], ab_sb[0:1, 0:1], ab_sb[0:1, 1:2], ALU.mult, op1=ALU.add), inc=dv)
        p.w("vector", dv)
        p.i("vector", lambda v: v.tensor_copy(ktm[0:1, :], sA[0:1, :]), inc=dv)
        bnb_dv = p.v(dv)
        p.w("tensor", dv, bnb_dv)
        for c in range(8):
            p.i("tensor", lambda tt, c=c: tt.transpose(
                tp_ps[:, c, 0:1], ktm[0:1, c * 128:(c + 1) * 128], idn_b[0:1, 0:1]),
                inc=pe if c == 7 else None)
        bt_pe = p.v(pe)
        p.w("vector", pe, bt_pe)
        p.i("vector", lambda v: v.tensor_copy(bnd_fm[:, :, :], tp_ps[:, :, 0:1]), inc=dv)
        bfm_dv = p.v(dv)
        p.w("tensor", dv, bfm_dv)
        for nh in range(2):
            osl = slice(nh * 512, (nh + 1) * 512)
            for c in range(8):
                p.i("tensor", lambda tt, c=c, osl=osl: tt.matmul(
                    sc_ps[1][0:1, osl], bnd_fm[:, c, :], wsh[:, c, osl], start=(c == 0), stop=False))
            p.i("tensor", lambda tt, osl=osl: tt.matmul(
                sc_ps[1][0:1, osl], ones_b[0:1, 0:1], bq_r[:, osl], start=False, stop=True), inc=pe)
        bq_pe = p.v(pe)
        p.w("vector", pe, bq_pe)
        p.i("vector", lambda v: v.tensor_copy(sB[0:1, :], sc_ps[1][0:1, :]), inc=dv)
        p.w("scalar", dv)
        p.i("scalar", lambda s: s.activation(sA[0:1, :], sB[0:1, :], AF.Square), inc=ac)
        p.w("vector", ac)
        p.i("vector", lambda v: v.tensor_reduce(
            rsc[0:1, 0:NH], sA[0:1, :].rearrange("p (h d) -> p h d", h=NH), AX.X, ALU.add), inc=dv)
        p.w("vector", dv)
        p.i("vector", lambda v: v.tensor_scalar(rsc[0:1, NH:2 * NH], rsc[0:1, 0:NH], 1.0 / HD, 1e-6,
                                                ALU.mult, op1=ALU.add), inc=dv)
        p.w("scalar", dv)
        p.i("scalar", lambda s: s.activation(rsc[0:1, NH:2 * NH], rsc[0:1, NH:2 * NH], AF.Ln), inc=ac)
        p.w("scalar", ac)
        p.i("scalar", lambda s: s.activation(rq[0:1, 0:NH], rsc[0:1, NH:2 * NH], AF.Exp,
                                             scale=-0.5), inc=ac)
        p.w("vector", ac)
        for h in range(NH):
            p.i("vector", lambda v, h=h: v.tensor_scalar(
                sA[0:1, h * HD:(h + 1) * HD], sB[0:1, h * HD:(h + 1) * HD],
                rq[0:1, h:h + 1], None, ALU.mult), inc=dv if h == NH - 1 else None)
        p.w("vector", dv)
        p.w("scalar", dv)
        p.i("scalar", lambda s: s.activation(sB[0:1, :], sA[0:1, :], AF.Square), inc=ac)
        p.w("vector", ac)
        p.i("vector", lambda v: v.tensor_tensor(sB[0:1, :], sB[0:1, :], w2sq[0:1, :], ALU.mult), inc=dv)
        p.w("vector", dv)
        p.i("vector", lambda v: v.tensor_reduce(
            rsc[0:1, 0:NH], sB[0:1, :].rearrange("p (h d) -> p h d", h=NH), AX.X, ALU.add), inc=dv)
        p.w("scalar", dv)
        p.i("scalar", lambda s: s.activation(rsc[0:1, 0:NH], rsc[0:1, 0:NH], AF.Ln,
                                             scale=float(HD)), inc=ac)
        p.w("scalar", ac)
        p.i("scalar", lambda s: s.activation(rsc[0:1, 0:NH], rsc[0:1, 0:NH], AF.Exp,
                                             scale=0.5), inc=ac)
        p.w("vector", ac)
        p.i("vector", lambda v: v.tensor_scalar(rsc[0:1, NH:2 * NH], rsc[0:1, 0:NH], -1.0, None,
                                                ALU.mult), inc=dv)
        p.i("vector", lambda v: v.tensor_copy(ktm[0:1, :], sA[0:1, :]), inc=dv)
        bq2_dv = p.v(dv)
        p.w("tensor", dv, bq2_dv)
        for c in range(8):
            p.i("tensor", lambda tt, c=c: tt.transpose(
                tp_ps[:, c, 0:1], ktm[0:1, c * 128:(c + 1) * 128], idn_b[0:1, 0:1]),
                inc=pe if c == 7 else None)
        bt2_pe = p.v(pe)
        p.w("vector", pe, bt2_pe)
        p.i("vector", lambda v: v.tensor_scalar(
            bnd_fm[:, :, :], tp_ps[:, :, 0:1], w2col[:, :], None, ALU.mult), inc=dv)
        bfm2_dv = p.v(dv)
        p.w("tensor", dv, bfm2_dv)
        bexp_ac = p.v(ac)
        bkf_pe = 0
        for hp in range(8):
            if hp >= 1:
                p.w("sync", pe, bkf_pe)
            bkl = p.dma("sync", kf[:, 0, :], kfull[hp, :, :], inc=ks)
            p.w("tensor", ks, bkl)
            p.w("tensor", ac, bexp_ac)
            for hh in range(2):
                p.i("tensor", lambda tt, hp=hp, hh=hh: tt.matmul(
                    sc_ps[0][:, hh:hh + 1], ones_f[:, :], rsc[0:1, NH + 2 * hp + hh:NH + 2 * hp + hh + 1],
                    start=True, stop=True), inc=pe if hh == 1 else None)
            cbm = p.v(pe)
            p.w("vector", pe, cbm)
            p.i("vector", lambda v: v.tensor_copy(cbb[:, :], sc_ps[0][:, 0:2]), inc=dv)
            cbb_dv = p.v(dv)
            p.w("tensor", dv, cbb_dv)
            for kc in range(32):
                p.i("tensor", lambda tt, kc=kc, hp=hp: tt.matmul(
                    o_psA[:, kc:kc + 1], kf[0:64, 0, kc * 128:(kc + 1) * 128], bnd_fm[0:64, hp, :],
                    start=True, stop=True, tile_position=(0, 0)))
                p.i("tensor", lambda tt, kc=kc, hp=hp: tt.matmul(
                    o_psB[:, kc:kc + 1], kf[64:128, 0, kc * 128:(kc + 1) * 128], bnd_fm[64:128, hp, :],
                    start=True, stop=True, tile_position=(64, 0)), inc=pe if kc == 31 else None)
            bsc_pe = p.v(pe)
            bkf_pe = bsc_pe
            p.w("scalar", pe, bsc_pe)
            p.i("scalar", lambda s: s.activation(ubnd[:, 0:32], o_psA[:, 0:32], AF.Exp,
                                                 bias=cbb[:, 0:1]), inc=ac)
            p.i("scalar", lambda s: s.activation(ubnd[:, 32:64], o_psB[:, 0:32], AF.Exp,
                                                 bias=cbb[:, 1:2]), inc=ac)
            bexp_ac = p.v(ac)
            p.w("tensor", ac, bexp_ac)
            for hh in range(2):
                h = 2 * hp + hh
                for kc in range(32):
                    p.i("tensor", lambda tt, kc=kc, h=h, hh=hh: tt.matmul(
                        sc_ps[0][0:65, 2:3], vh[:, kc, h, :], ubnd[:, hh * 32 + kc:hh * 32 + kc + 1],
                        start=(kc == 0), stop=(kc == 31)), inc=pe if kc == 31 else None)
                bav = p.v(pe)
                p.w("vector", pe, bav)
                p.i("vector", lambda v, h=h: v.tensor_copy(obnd[:, h:h + 1], sc_ps[0][0:65, 2:3]), inc=dv)
                bo_dv = p.v(dv)
                p.w("tensor", dv, bo_dv)
        for h in range(NH):
            p.i("tensor", lambda tt, h=h: tt.transpose(
                tp_ps[0:1, 0, 0:65], obnd[:, h:h + 1], idn_b[0:65, 0:65]), inc=pe)
            btp = p.v(pe)
            p.w("vector", pe, btp)
            p.i("vector", lambda v: v.tensor_copy(hrow[:, :], tp_ps[0:1, 0, 0:65]), inc=dv)
            p.w("vector", dv)
            p.i("vector", lambda v: v.reciprocal(rq[0:1, 0:1], hrow[:, 64:65]), inc=dv)
            p.w("vector", dv)
            p.i("vector", lambda v, h=h: v.tensor_scalar(
                arow[:, h * 64:(h + 1) * 64], hrow[:, 0:64], rq[0:1, 0:1], None, ALU.mult), inc=dv)
            arw_dv = p.v(dv)
            p.w("tensor", dv, arw_dv)
        for c in range(8):
            p.i("tensor", lambda tt, c=c: tt.transpose(
                tp_ps[:, c, 0:1], arow[0:1, c * 128:(c + 1) * 128], idn_b[0:1, 0:1]),
                inc=pe if c == 7 else None)
        bt3 = p.v(pe)
        p.w("vector", pe, bt3)
        p.i("vector", lambda v: v.tensor_copy(arow_fm[:, :, :], tp_ps[:, :, 0:1]), inc=dv)
        afm3_dv = p.v(dv)
        p.w("tensor", dv, afm3_dv)
        for nh in range(2):
            osl = slice(nh * 512, (nh + 1) * 512)
            for c in range(8):
                p.i("tensor", lambda tt, c=c, osl=osl: tt.matmul(
                    sc_ps[1][0:1, osl], arow_fm[:, c, :], wsh[:, c, 1024 + osl.start:1024 + osl.stop],
                    start=(c == 0), stop=False))
            p.i("tensor", lambda tt, osl=osl: tt.matmul(
                sc_ps[1][0:1, osl], ones_b[0:1, 0:1], bpr_r[:, osl], start=False, stop=True), inc=pe)
        bpj = p.v(pe)
        p.w("vector", pe, bpj)
        p.i("vector", lambda v: v.tensor_copy(sB[0:1, :], sc_ps[1][0:1, :]), inc=dv)
        p.w("vector", dv)
        _ln_ab(p, sB[0:1, :], bn_sb[0:1, :], ab_sb[0:1, :], 1e-5, dv, ac)
        p.i("vector", lambda v: v.tensor_scalar(
            sA[0:1, :], sB[0:1, :], ab_sb[0:1, 0:1], ab_sb[0:1, 1:2], ALU.mult, op1=ALU.add), inc=dv)
        p.w("vector", dv)
        p.i("vector", lambda v: v.tensor_tensor(sA[0:1, :], sA[0:1, :], pb_sb[:, :], ALU.add), inc=dv)
        bnd_dv = p.v(dv)
        p.w("sync", dv, bnd_dv)
        bnd_st = p.dma("sync", bndd[:, :], sA[0:1, :], inc=st)
        p.w("sync", st, bnd_st)
        p.w("sync", pe, p.v(pe))

        st_val = 0
        o_dv_hist = {}
        prj_evac_dv = None
        u_free_pe = [0, 0]
        exp_ac = [0, 0]
        sc_free = [0, 0]
        o_read_dv = 0
        ahm_read_pe = 0
        kl_val = {}
        prev_hp_parity_exp = {}
        for qb in range(NQB):
            for hp in range(8):
                kr = 0
                if qb * 8 + hp >= 1:
                    p.w("sync", pe, kf_read_pe[kr])
                kl_val[(qb, hp)] = p.dma("sync", kf[:, kr, :], kfull[hp, :, :], inc=ks)
                p.w("tensor", ks, kl_val[(qb, hp)])
                if qb == 0 and hp == 0:
                    kf_read_pe = [0, 0]
                for kc in range(33):
                    if kc < 32:
                        sb = kc % 2
                        p.w("tensor", ac, sc_free[sb])
                        if kc == 0 and hp == 0 and prj_evac_dv is not None:
                            p.w("tensor", dv, prj_evac_dv)
                        p.i("tensor", lambda tt, hp=hp, kc=kc, qb=qb, kr=kr, sb=sb: tt.matmul(
                            sc_ps[sb][:, 0:512], kf[0:64, kr, kc * 128:(kc + 1) * 128],
                            qf[0:64, hp, qb * 512:(qb + 1) * 512],
                            start=True, stop=True, tile_position=(0, 0)))
                        smm = p.i("tensor", lambda tt, hp=hp, kc=kc, qb=qb, kr=kr, sb=sb: tt.matmul(
                            sc_ps[sb][:, 512:1024], kf[64:128, kr, kc * 128:(kc + 1) * 128],
                            qf[64:128, hp, qb * 512:(qb + 1) * 512],
                            start=True, stop=True, tile_position=(64, 0)), inc=pe)
                        p.w("scalar", pe, smm)
                        p.w("scalar", pe, u_free_pe[sb])
                        sc_free[sb] = p.i("scalar", lambda s, sb=sb, hp=hp, qb=qb: s.activation(
                            u_sb[:, sb, :], sc_ps[sb][:, :], AF.Exp,
                            bias=cbias[:, hp * 4 + qb:hp * 4 + qb + 1]), inc=ac)
                        exp_ac[sb] = sc_free[sb]
                    if kc >= 1:
                        kv = kc - 1
                        vb = kv % 2
                        p.w("tensor", ac, exp_ac[vb])
                        if kv == 0:
                            p.w("tensor", dv, o_read_dv)
                        p.i("tensor", lambda tt, hp=hp, kv=kv, vb=vb: tt.matmul(
                            o_psA[0:65, :], vh[:, kv, 2 * hp, :], u_sb[:, vb, 0:512],
                            start=(kv == 0), stop=(kv == 31)))
                        mmb = p.i("tensor", lambda tt, hp=hp, kv=kv, vb=vb: tt.matmul(
                            o_psB[0:65, :], vh[:, kv, 2 * hp + 1, :], u_sb[:, vb, 512:1024],
                            start=(kv == 0), stop=(kv == 31)), inc=pe)
                        u_free_pe[vb] = mmb
                kf_read_pe[kr] = p.v(pe)
                p.w("vector", pe, p.v(pe))
                p.w("vector", pe, ahm_read_pe)
                p.i("vector", lambda v, hp=hp: v.tensor_copy(ahm[2 * hp][:, :], o_psA[0:65, :]))
                p.i("vector", lambda v, hp=hp: v.tensor_copy(ahm[2 * hp + 1][:, :], o_psB[0:65, :]), inc=dv)
                o_read_dv = p.v(dv)

            for ti in range(4):
                tile = qb * 4 + ti
                r = ti % 2
                if tile >= 1:
                    p.w("sync", dv, o_dv_hist[tile - 1])
                lxv = p.dma("sync", x2_t[:, 0, :], x_own[tile * 128:(tile + 1) * 128, :], inc=lx)
                p.w("tensor", dv, o_read_dv)
                if tile >= 1:
                    p.w("tensor", dv, aug_read_dv)
                    p.w("tensor", dv, prev_afm_dv)
                for hh in range(2):
                    for h8 in range(8):
                        h = hh * 8 + h8
                        p.i("tensor", lambda tt, h=h, h8=h8, ti=ti: tt.transpose(
                            tp_ps[:, h8, 0:65], ahm[h][:, ti * 128:(ti + 1) * 128], idn_b[0:65, 0:65]),
                            inc=pe if h8 == 7 else None)
                    hv = p.v(pe)
                    p.w("vector", pe, hv)
                    p.i("vector", lambda v, hh=hh: v.tensor_copy(
                        at_aug[:, hh * 8:(hh + 1) * 8, :], tp_ps[:, :, 0:65]), inc=dv)
                    tpd = p.v(dv)
                    p.w("tensor", dv, tpd)
                ahm_read_pe = p.v(pe)
                p.w("vector", dv)
                p.i("vector", lambda v: v.reciprocal(den_r[:, :], at_aug[:, :, 64]), inc=dv)
                p.w("vector", dv)
                for h in range(NH):
                    p.i("vector", lambda v, h=h: v.tensor_scalar(
                        at2[:, h * 64:(h + 1) * 64], at_aug[:, h, 0:64], den_r[:, h:h + 1],
                        None, ALU.mult), inc=dv if h == NH - 1 else None)
                at2_dv = p.v(dv)
                aug_read_dv = at2_dv
                p.w("tensor", dv, at2_dv)
                for c in range(8):
                    p.i("tensor", lambda tt, c=c: tt.transpose(
                        tp_ps[:, c, :], at2[:, c * 128:(c + 1) * 128], idn_b[:, :]),
                        inc=pe if c == 7 else None)
                atp = p.v(pe)
                p.w("vector", pe, atp)
                p.i("vector", lambda v: v.tensor_copy(a_fm[:, :, :], tp_ps[:, :, :]), inc=dv)
                afm_dv = p.v(dv)
                prev_afm_dv = afm_dv
                p.w("tensor", dv, afm_dv)
                if prj_evac_dv is not None:
                    p.w("tensor", dv, prj_evac_dv)
                for nh in range(2):
                    osl = slice(nh * 512, (nh + 1) * 512)
                    for c in range(8):
                        p.i("tensor", lambda tt, c=c, osl=osl: tt.matmul(
                            sc_ps[0][:, osl], a_fm[:, c, :], wsh[:, c, 1024 + osl.start:1024 + osl.stop],
                            start=(c == 0), stop=False))
                    p.i("tensor", lambda tt, osl=osl: tt.matmul(
                        sc_ps[0][:, osl], ones_b[:, :], bpr_r[:, osl], start=False, stop=True),
                        inc=pe)
                prj = p.v(pe)
                p.w("vector", pe, prj)
                p.i("vector", lambda v: v.tensor_copy(sB[:, :], sc_ps[0][:, :]), inc=dv)
                prj_evac_dv = p.v(dv)
                p.w("vector", dv)
                _ln_ab(p, sB[:, :], bn_sb, ab_sb, 1e-5, dv, ac)
                p.i("vector", lambda v: v.tensor_scalar(
                    sA[:, :], sB[:, :], ab_sb[:, 0:1], ab_sb[:, 1:2], ALU.mult, op1=ALU.add), inc=dv)
                p.w("vector", lx, lxv)
                p.w("vector", dv)
                if tile >= 1:
                    p.w("vector", st, st_val)
                p.i("vector", lambda v, r=r: v.tensor_tensor(
                    out_t[:, 0, :], sA[:, :], x2_t[:, 0, :], ALU.add), inc=dv)
                o_dv = p.v(dv)
                o_dv_hist[tile] = o_dv
                p.w("sync", dv, o_dv)
                st_val = p.dma("sync", y[tile * 128:(tile + 1) * 128, :], out_t[:, 0, :], inc=st)

        p.w("sync", st, st_val)

def emit_kv(p, nc, E, x_in, wqkv, bqkv, idn, kf_out, vh_out):
    ld, lw, pe, dv, ac, st, lx, ks, po = "ld", "lw", "pe", "dv", "ac", "st", "lx", "ks", "po"
    if True:
        x_t = E(nc.sbuf_tensor("Kx_t", [128, 1, H_], F32))
        xn_tm = E(nc.sbuf_tensor("Kxn_tm", [128, H_], BF16))
        xn_fm = E(nc.sbuf_tensor("Kxn_fm", [128, 8, 128], BF16))
        wsh = E(nc.sbuf_tensor("Kwsh", [128, 8, 2048], BF16))
        bq_r = E(nc.sbuf_tensor("Kbq_r", [1, 2 * H_], BF16))
        idn_b = E(nc.sbuf_tensor("Kidn_b", [128, 128], BF16))
        ones_b = E(nc.sbuf_tensor("Kones_b", [1, 128], BF16))
        sA = E(nc.sbuf_tensor("KsA", [128, H_], F32))
        ktm = E(nc.sbuf_tensor("Kktm", [128, H_], BF16))
        kfm_t = E(nc.sbuf_tensor("Kkfm_t", [128, 8, 128], BF16))
        vh_t = E(nc.sbuf_tensor("Kvh_t", [128, 2, NH, 65], BF16))
        rq = E(nc.sbuf_tensor("Krq", [128, NH], F32))
        rsc = E(nc.sbuf_tensor("Krsc", [128, NH * 2], F32))
        bn_sb = E(nc.sbuf_tensor("Kbn_sb", [128, 16], F32))
        ab_sb = E(nc.sbuf_tensor("Kab_sb", [128, 2], F32))
        bnst_all = E(nc.sbuf_tensor("Kbnst_all", [128, NT, 12], F32))
        mv_all = E(nc.sbuf_tensor("Kmv_all", [128, NT, 2], F32))
        ap_all = E(nc.sbuf_tensor("Kap_all", [128, NT, 2], F32))
        tp_ps = E(nc.psum_tensor("Ktp_ps", [128, 8, 128], BF16))
        pk = E(nc.psum_tensor("Kpk", [128, H_], F32))
        pv = E(nc.psum_tensor("Kpv", [128, H_], F32))

        wl = p.dma("gpsimd", wsh[:, :, :], wqkv[:, H_:3 * H_].rearrange("(c p) o -> p c o", p=128), inc=lw)
        p.dma("gpsimd", bq_r[:, :], bqkv[None, H_:3 * H_], inc=lw)
        p.dma("gpsimd", idn_b[:, :], idn[:, :], inc=lw)
        p.i("gpsimd", lambda g: g.memset(ones_b[:, :], 1.0))
        p.i("gpsimd", lambda g: g.memset(vh_t[:, :, :, :], 1.0), inc=lw, n=1)
        lw0 = p.v(lw)
        p.w("tensor", lw, lw0)
        p.w("vector", lw, lw0)

        xn_read_dv = {}
        tp_pe = ktp_pe = 0
        prev_ps_read = 0
        vst = {}
        for t in range(NT):
            if t >= 1:
                p.w("sync", dv, xn_read_dv[t - 1])
            ldv = p.dma("sync", x_t[:, 0, :], x_in[t * 128:(t + 1) * 128, :], inc=ld)
            p.w("vector", ld, ldv)
            p.i("vector", lambda v, t=t: v.bn_stats(bnst_all[:, t, 0:6], x_t[:, 0, 0:512]))
            p.i("vector", lambda v, t=t: v.bn_stats(bnst_all[:, t, 6:12], x_t[:, 0, 512:1024]), inc=dv)
            xn_read_dv[t] = p.v(dv)
        p.w("vector", dv)
        for t in range(NT):
            p.i("vector", lambda v, t=t: v.bn_aggr(mv_all[:, t, :], bnst_all[:, t, :]),
                inc=dv if t == NT - 1 else None)
        p.w("vector", dv)
        p.i("vector", lambda v: v.tensor_scalar(bnst_all[:, :, 0], mv_all[:, :, 1], 1.0, 1e-5,
                                                ALU.mult, op1=ALU.add), inc=dv)
        p.w("scalar", dv)
        p.i("scalar", lambda s: s.activation(bnst_all[:, :, 0], bnst_all[:, :, 0], AF.Ln), inc=ac)
        p.w("scalar", ac)
        p.i("scalar", lambda s: s.activation(ap_all[:, :, 0], bnst_all[:, :, 0], AF.Exp, scale=-0.5), inc=ac)
        p.w("vector", ac)
        p.i("vector", lambda v: v.tensor_tensor(ap_all[:, :, 1], mv_all[:, :, 0], ap_all[:, :, 0], ALU.mult), inc=dv)
        p.w("vector", dv)
        p.i("vector", lambda v: v.tensor_scalar(ap_all[:, :, 1], ap_all[:, :, 1], -1.0, None, ALU.mult), inc=dv)
        pre_dv = p.v(dv)
        p.w("vector", dv, pre_dv)
        for t in range(NT):
            p.w("sync", dv, pre_dv if t == 0 else xn_read_dv[NT + t - 1])
            ldv = p.dma("sync", x_t[:, 0, :], x_in[t * 128:(t + 1) * 128, :], inc=ld)
            p.w("vector", ld, ldv)
            if t >= 1:
                p.w("vector", pe, tp_pe)
            p.i("vector", lambda v, t=t: v.tensor_scalar(
                xn_tm[:, :], x_t[:, 0, :], ap_all[:, t, 0:1], ap_all[:, t, 1:2],
                ALU.mult, op1=ALU.add), inc=dv)
            xn_dv = p.v(dv)
            xn_read_dv[NT + t] = xn_dv
            p.w("tensor", dv, xn_dv)
            for c in range(8):
                p.i("tensor", lambda tt, c=c: tt.transpose(
                    tp_ps[:, c, :], xn_tm[:, c * 128:(c + 1) * 128], idn_b[:, :]),
                    inc=pe if c == 7 else None)
            tp_pe = p.v(pe)
            p.w("vector", pe, tp_pe)
            p.i("vector", lambda v: v.tensor_copy(xn_fm[:, :, :], tp_ps[:, :, :]), inc=dv)
            xnfm_dv = p.v(dv)
            p.w("tensor", dv, xnfm_dv)
            p.w("tensor", dv, prev_ps_read)
            for half, ps in ((0, pk), (1, pv)):
                for nh in range(2):
                    sl = slice(half * 1024 + nh * 512, half * 1024 + (nh + 1) * 512)
                    osl = slice(nh * 512, (nh + 1) * 512)
                    for c in range(8):
                        p.i("tensor", lambda tt, c=c, sl=sl, osl=osl, ps=ps: tt.matmul(
                            ps[:, osl], xn_fm[:, c, :], wsh[:, c, sl], start=(c == 0), stop=False))
                    p.i("tensor", lambda tt, sl=sl, osl=osl, ps=ps: tt.matmul(
                        ps[:, osl], ones_b[:, :], bq_r[:, sl], start=False, stop=True), inc=pe)
            proj_pe = p.v(pe)
            p.w("scalar", pe, proj_pe)
            p.i("scalar", lambda s: s.activation(sA[:, :], pk[:, :], AF.Square), inc=ac)
            p.w("vector", ac)
            p.i("vector", lambda v: v.tensor_reduce(
                rsc[:, 0:NH], sA.rearrange("p (h d) -> p h d", h=NH), AX.X, ALU.add), inc=dv)
            p.w("vector", dv)
            p.i("vector", lambda v: v.tensor_scalar(rsc[:, NH:2 * NH], rsc[:, 0:NH], 1.0 / HD, 1e-6,
                                                    ALU.mult, op1=ALU.add), inc=dv)
            p.w("scalar", dv)
            p.i("scalar", lambda s: s.activation(rsc[:, NH:2 * NH], rsc[:, NH:2 * NH], AF.Ln), inc=ac)
            p.w("scalar", ac)
            p.i("scalar", lambda s: s.activation(rq[:, :], rsc[:, NH:2 * NH], AF.Exp,
                                                 scale=-0.5), inc=ac)
            p.w("vector", ac)
            if t >= 1:
                p.w("vector", pe, ktp_pe)
            for h in range(NH):
                p.i("vector", lambda v, h=h: v.tensor_scalar(
                    ktm[:, h * HD:(h + 1) * HD], pk[:, h * HD:(h + 1) * HD],
                    rq[:, h:h + 1], None, ALU.mult), inc=dv if h == NH - 1 else None)
            ktm_dv = p.v(dv)
            p.w("tensor", dv, ktm_dv)
            for c in range(8):
                p.i("tensor", lambda tt, c=c: tt.transpose(
                    tp_ps[:, c, :], ktm[:, c * 128:(c + 1) * 128], idn_b[:, :]),
                    inc=pe if c == 7 else None)
            ktp_pe = p.v(pe)
            p.w("vector", pe, ktp_pe)
            if t >= 1:
                p.w("vector", ks, kst)
            p.i("vector", lambda v: v.tensor_copy(kfm_t[:, :, :], tp_ps[:, :, :]), inc=dv)
            kfm_dv = p.v(dv)
            p.w("sync", dv, kfm_dv)
            kst = p.dma("sync",
                        kf_out[:, :, t * 128:(t + 1) * 128].rearrange("h p c -> p h c"),
                        kfm_t[:, :, :], inc=ks)
            r = t % 2
            if t >= 2:
                p.w("vector", st, vst[t - 2])
            p.i("vector", lambda v, r=r: v.tensor_copy(
                vh_t[:, r, :, 0:64], pv.rearrange("p (h d) -> p h d", h=NH)), inc=dv)
            vh_dv = p.v(dv)
            prev_ps_read = p.v(dv)
            p.w("sync", dv, vh_dv)
            vst[t] = p.dma("sync", vh_out[:, t, :, :], vh_t[:, r, :, :], inc=st)
        p.w("sync", st, p.v(st))
        p.w("sync", ks, p.v(ks))


def build_fused1():
    nc = bass.Bass("TRN2", target_bir_lowering=False)
    d = {}
    def inp(name, shape):
        d[name] = nc.dram_tensor(name, shape, F32, kind="ExternalInput").ap()
    for name, shape in (
        ("x_own", [T, H_]), ("x_oth", [T, H_]),
        ("awq", [H_, H_]), ("awk", [H_, H_]), ("awv", [H_, H_]), ("awd", [H_, H_]),
        ("abq", [H_]), ("abk", [H_]), ("abv", [H_]), ("abd", [H_]),
        ("agam", [H_]), ("abet", [H_]),
        ("fgam", [H_]), ("fbet", [H_]), ("fwr", [H_, 8]), ("fbr", [8]),
        ("fw1", [8, H_, 4096]), ("fb1", [8, 4096]), ("fw2", [8, 4096, H_]), ("fb2", [8, H_]),
        ("cwqkv", [H_, 3 * H_]), ("cbqkv", [3 * H_]),
        ("idn", [128, 128]), ("ltri", [128, 128]), ("iotap", [128, 1]),
        ("mg_init", [ROWS_E, 2]),
    ):
        inp(name, shape)
    yB = nc.dram_tensor("yB", [T, H_], F32, kind="ExternalOutput").ap()
    kf_out = nc.dram_tensor("kf_out", [8, 128, T], BF16, kind="ExternalOutput").ap()
    vh_out = nc.dram_tensor("vh_out", [128, NT, NH, 65], BF16, kind="ExternalOutput").ap()
    dbgx = bool(globals().get("DBG_XA", False))
    sAB = nc.dram_tensor("sAB", [T, H_], F32, kind="ExternalOutput" if dbgx else "Internal").ap()
    dbg = bool(globals().get("DBG_MOE", False))
    kindD = "ExternalOutput" if dbg else "Internal"
    xg_dram = nc.dram_tensor("xg_dram", [8, ROWS_E, H_], BF16, kind=kindD).ap()
    mg_dram = nc.dram_tensor("mg_dram", [8, ROWS_E, 2], F32, kind=kindD).ap()
    acc_dram = nc.dram_tensor("acc_dram", [T + 128, H_], F32, kind=kindD).ap()
    ydbg = nc.dram_tensor("ydbg", [8, CAP, H_], F32, kind=kindD).ap() if dbg else None

    with contextlib.ExitStack() as stack:
        p = Prog(nc, stack)
        for s in ("ld", "lw", "pe", "dv", "ac", "st", "lx", "ks", "po"):
            p.sem(s)
        with contextlib.ExitStack() as e1:
            emit_l0_attn(p, nc, e1.enter_context, d["x_own"], d["x_oth"],
                         d["awq"], d["awk"], d["awv"], d["awd"],
                         d["abq"], d["abk"], d["abv"], d["abd"],
                         d["agam"], d["abet"], d["idn"], sAB)
            p.barrier()
        with contextlib.ExitStack() as e2:
            emit_l0_ffn_sparse(p, nc, e2.enter_context, sAB, d["fgam"], d["fbet"], d["fwr"], d["fbr"],
                               d["fw1"], d["fb1"], d["fw2"], d["fb2"], d["idn"],
                               d["ltri"], d["iotap"], d["mg_init"], xg_dram, mg_dram, acc_dram, yB,
                               ydbg=ydbg)
            p.barrier()
        with contextlib.ExitStack() as e3:
            emit_kv(p, nc, e3.enter_context, yB, d["cwqkv"], d["cbqkv"], d["idn"], kf_out, vh_out)
            p.barrier()
        p.build_block()
    return nc


def build_fused2():
    nc = bass.Bass("TRN2", target_bir_lowering=False)
    x_own = nc.dram_tensor("x_own", [T, H_], F32, kind="ExternalInput").ap()
    kfull = nc.dram_tensor("kfull", [8, 128, S_], BF16, kind="ExternalInput").ap()
    vfull = nc.dram_tensor("vfull", [128, 32, NH, 65], BF16, kind="ExternalInput").ap()
    wqkv = nc.dram_tensor("wqkv", [H_, 3 * H_], F32, kind="ExternalInput").ap()
    bqkv = nc.dram_tensor("bqkv", [3 * H_], F32, kind="ExternalInput").ap()
    qn = nc.dram_tensor("qn", [HD], F32, kind="ExternalInput").ap()
    kn = nc.dram_tensor("kn", [HD], F32, kind="ExternalInput").ap()
    wpr = nc.dram_tensor("wpr", [H_, H_], F32, kind="ExternalInput").ap()
    bpr = nc.dram_tensor("bpr", [H_], F32, kind="ExternalInput").ap()
    pbrow = nc.dram_tensor("pbrow", [1, H_], F32, kind="ExternalInput").ap()
    msel = nc.dram_tensor("msel", [1, 1], F32, kind="ExternalInput").ap()
    dwr = nc.dram_tensor("dwr", [H_, H_], F32, kind="ExternalInput").ap()
    dwk = nc.dram_tensor("dwk", [H_, H_], F32, kind="ExternalInput").ap()
    dwv = nc.dram_tensor("dwv", [H_, H_], F32, kind="ExternalInput").ap()
    mur = nc.dram_tensor("mur", [H_], F32, kind="ExternalInput").ap()
    muk = nc.dram_tensor("muk", [H_], F32, kind="ExternalInput").ap()
    sh1 = nc.dram_tensor("sh1", [128, 128], F32, kind="ExternalInput").ap()
    sh2 = nc.dram_tensor("sh2", [128, 128], F32, kind="ExternalInput").ap()
    idn = nc.dram_tensor("idn", [128, 128], F32, kind="ExternalInput").ap()
    y = nc.dram_tensor("y", [T, H_], F32, kind="ExternalOutput").ap()
    sCD = nc.dram_tensor("sCD", [T, H_], F32, kind="Internal").ap()
    bndd = nc.dram_tensor("bndd", [1, H_], F32, kind="Internal").ap()

    with contextlib.ExitStack() as stack:
        p = Prog(nc, stack)
        for s in ("ld", "lw", "pe", "dv", "ac", "st", "lx", "ks", "po"):
            p.sem(s)
        with contextlib.ExitStack() as e1:
            emit_l1_attn2(p, nc, e1.enter_context, x_own, kfull, vfull, wqkv, bqkv,
                          qn, kn, wpr, bpr, pbrow, idn, bndd, sCD)
            p.barrier()
        with contextlib.ExitStack() as e2:
            emit_l1_ffn(p, nc, e2.enter_context, sCD, bndd, msel, dwr, dwk, dwv,
                        mur, muk, sh1, sh2, idn, y)
            p.barrier()
        p.build_block()
    return nc


def kernel_fused(inputs):
    f = lambda k: np.ascontiguousarray(np.asarray(inputs[k], np.float32))
    x = f("L0_attn_input_tensor")
    idn = np.eye(128, dtype=np.float32)
    nc1 = _get_nc("f1", build_fused1)
    com1 = dict(
        awq=np.ascontiguousarray(f("L0_attn_Wq").T), awk=np.ascontiguousarray(f("L0_attn_Wk").T),
        awv=np.ascontiguousarray(f("L0_attn_Wv").T), awd=np.ascontiguousarray(f("L0_attn_Wd").T),
        abq=f("L0_attn_bq"), abk=f("L0_attn_bk"), abv=f("L0_attn_bv"), abd=f("L0_attn_bd"),
        agam=f("L0_attn_gamma"), abet=f("L0_attn_beta"),
        fgam=f("L0_ffn_gamma"), fbet=f("L0_ffn_beta"), fwr=f("L0_ffn_Wr"), fbr=f("L0_ffn_br"),
        fw1=f("L0_ffn_W1"), fb1=f("L0_ffn_b1"), fw2=f("L0_ffn_W2"), fb2=f("L0_ffn_b2"),
        cwqkv=np.ascontiguousarray(f("L1_attn_w_qkv").T), cbqkv=f("L1_attn_b_qkv"),
        idn=idn,
        ltri=np.triu(np.ones((128, 128), np.float32), 1),
        iotap=np.arange(128, dtype=np.float32).reshape(128, 1),
        mg_init=np.stack([np.zeros(ROWS_E, np.float32),
                          (T + (np.arange(ROWS_E) % 128)).astype(np.float32)], axis=1),
    )
    in_maps = []
    for c in range(NCORES):
        b_, h = c // 2, c % 2
        m = dict(com1)
        m["x_own"] = np.ascontiguousarray(x[b_, h * T:(h + 1) * T])
        m["x_oth"] = np.ascontiguousarray(x[b_, (1 - h) * T:(2 - h) * T])
        in_maps.append(m)
    res1 = run_bass_kernel_spmd(nc1, in_maps, core_ids=list(range(NCORES)))
    yB = [res1.results[c]["yB"] for c in range(NCORES)]
    kf = [res1.results[c]["kf_out"] for c in range(NCORES)]
    vh = [res1.results[c]["vh_out"] for c in range(NCORES)]

    nc2 = _get_nc("f2", build_fused2)
    sh1 = np.eye(128, k=1, dtype=np.float32)
    sh2 = np.zeros((128, 128), dtype=np.float32)
    sh2[127, 0] = 1.0
    com2 = dict(
        wqkv=np.ascontiguousarray(f("L1_attn_w_qkv").T), bqkv=f("L1_attn_b_qkv"),
        qn=f("L1_attn_qn_w"), kn=f("L1_attn_kn_w"),
        wpr=np.ascontiguousarray(f("L1_attn_w_proj").T), bpr=f("L1_attn_b_proj"),
        dwr=np.ascontiguousarray(f("L1_ffn_W_r").T), dwk=np.ascontiguousarray(f("L1_ffn_W_k").T),
        dwv=np.ascontiguousarray(f("L1_ffn_W_v").T),
        mur=f("L1_ffn_mu_r"), muk=f("L1_ffn_mu_k"), sh1=sh1, sh2=sh2, idn=idn,
    )
    in_maps2 = []
    for c in range(NCORES):
        b_, h = c // 2, c % 2
        pair0, pair1 = 2 * b_, 2 * b_ + 1
        m = dict(com2)
        m["x_own"] = np.ascontiguousarray(yB[c])
        m["kfull"] = np.ascontiguousarray(np.concatenate([kf[pair0], kf[pair1]], axis=2))
        m["vfull"] = np.ascontiguousarray(np.concatenate([vh[pair0], vh[pair1]], axis=1))
        m["pbrow"] = np.ascontiguousarray(yB[pair0][T - 1:T])
        m["msel"] = np.full((1, 1), float(h), np.float32)
        in_maps2.append(m)
    res2 = run_bass_kernel_spmd(nc2, in_maps2, core_ids=list(range(NCORES)))
    out = np.empty((B_, S_, H_), np.float32)
    for c in range(NCORES):
        b_, h = c // 2, c % 2
        out[b_, h * T:(h + 1) * T] = res2.results[c]["y"]
    return out

CAP = 768
ROWS_E = CAP + 128


def emit_l0_ffn_sparse(p, nc, E, x_in, gam, bet, wr, br, w1, b1, w2, b2, idn,
                       ltri, iotap, mg_init, xg_dram, mg_dram, acc_dram, y, ydbg=None):
    ld, lw, pe, dv, ac, st, lx, ks, po = "ld", "lw", "pe", "dv", "ac", "st", "lx", "ks", "po"
    NSL = 4
    FS = 4096 // NSL
    if True:
        x_t = E(nc.sbuf_tensor("Bx_t", [128, 2, H_], F32))
        ab_all = E(nc.sbuf_tensor("Bab_all", [128, NT, 2], F32))
        xn_tm = E(nc.sbuf_tensor("Bxn_tm", [128, NT, H_], BF16))
        xn_fm = E(nc.sbuf_tensor("Bxn_fm", [128, 2, 8, 128], BF16))
        w1s = E(nc.sbuf_tensor("Bw1s", [128, 8, FS], BF16))
        w2s = E(nc.sbuf_tensor("Bw2s", [128, 8, H_], BF16))
        u_sb = E(nc.sbuf_tensor("Bu_sb", [128, 8, 512], BF16))
        gates = E(nc.sbuf_tensor("Bgates", [128, 128], F32))
        s_all = E(nc.sbuf_tensor("Bs_all", [128, NT], F32))
        gT = E(nc.sbuf_tensor("BgT", [9, T], BF16))
        dmat = E(nc.sbuf_tensor("Bdmat", [9, H_], BF16))
        dcol = E(nc.sbuf_tensor("Bdcol", [128, 8, 8], F32))
        b1_sb = E(nc.sbuf_tensor("Bb1_sb", [8, 2048], F32))
        b1_pt = E(nc.sbuf_tensor("Bb1_pt", [128, 8, 32], F32))
        brelu = E(nc.sbuf_tensor("Bbrelu", [128, 8, 32], BF16))
        b2_sb = E(nc.sbuf_tensor("Bb2_sb", [8, H_], F32))
        wr_sb = E(nc.sbuf_tensor("Bwr_sb", [128, 8, 8], BF16))
        br_r = E(nc.sbuf_tensor("Bbr_r", [1, 8], BF16))
        gam_r = E(nc.sbuf_tensor("Bgam_r", [1, H_], F32))
        bet_r = E(nc.sbuf_tensor("Bbet_r", [1, H_], F32))
        gam_bc = E(nc.sbuf_tensor("Bgam_bc", [128, H_], F32))
        bet_bc = E(nc.sbuf_tensor("Bbet_bc", [128, H_], F32))
        ones_b = E(nc.sbuf_tensor("Bones_b", [1, 128], BF16))
        ones_f = E(nc.sbuf_tensor("Bones_f", [1, 128], F32))
        ones_c = E(nc.sbuf_tensor("Bones_c", [128, 1], F32))
        ones8 = E(nc.sbuf_tensor("Bones8", [8, 1], F32))
        idn_b = E(nc.sbuf_tensor("Bidn_b", [128, 128], BF16))
        idn_f = E(nc.sbuf_tensor("Bidn_f", [128, 128], F32))
        ltri_b = E(nc.sbuf_tensor("Bltri_b", [128, 128], BF16))
        iota_c = E(nc.sbuf_tensor("Biota_c", [128, 1], F32))
        padv = E(nc.sbuf_tensor("Bpadv", [128, 1], F32))
        offs_bc = E(nc.sbuf_tensor("Boffs_bc", [128, 8], F32))
        mask_b = E(nc.sbuf_tensor("Bmask_b", [128, 8], BF16))
        posf = E(nc.sbuf_tensor("Bposf", [128, 8], F32))
        dmix = E(nc.sbuf_tensor("Bdmix", [128, 8], F32))
        d_u32 = E(nc.sbuf_tensor("Bd_u32", [128, 8], mybir.dt.uint32))
        mrow = E(nc.sbuf_tensor("Bmrow", [128, 8, 2], F32))
        tid_c = E(nc.sbuf_tensor("Btid_c", [128, 1], F32))
        tot_r = E(nc.sbuf_tensor("Btot_r", [1, 8], F32))
        zsb = E(nc.sbuf_tensor("Bzsb", [128, H_ // 2], F32))
        xg_sb = E(nc.sbuf_tensor("Bxg_sb", [128, 2, H_], BF16))
        xgf = E(nc.sbuf_tensor("Bxgf", [128, 8, CAP], BF16))
        mg_sb = E(nc.sbuf_tensor("Bmg_sb", [128, 2, CAP // 128, 2], F32))
        tok_u = E(nc.sbuf_tensor("Btok_u", [128, 2, CAP // 128], mybir.dt.uint32))
        ybuf = E(nc.sbuf_tensor("Bybuf", [128, CAP // 128, H_], F32))
        ysc = E(nc.sbuf_tensor("Bysc", [128, 2, H_], F32))
        sacc = E(nc.sbuf_tensor("Bsacc", [128, 1, H_], F32))
        sA = E(nc.sbuf_tensor("BsA", [128, H_], F32))
        sB = E(nc.sbuf_tensor("BsB", [128, H_], F32))
        x2_t = E(nc.sbuf_tensor("Bx2_t", [128, 1, H_], F32))
        out_t = E(nc.sbuf_tensor("Bout_t", [128, 1, H_], F32))
        bn_sb = E(nc.sbuf_tensor("Bbn_sb", [128, 16], F32))
        bnst_all = E(nc.sbuf_tensor("Bbnst_all", [128, NT, 12], F32))
        mv_all = E(nc.sbuf_tensor("Bmv_all", [128, NT, 2], F32))
        ap_all = E(nc.sbuf_tensor("Bap_all", [128, NT, 2], F32))
        ab_sb = E(nc.sbuf_tensor("Bab_sb", [128, 2], F32))
        rt = E(nc.sbuf_tensor("Brt", [128, 56], F32))
        gaug_b = E(nc.sbuf_tensor("Bgaug_b", [128, 16], BF16))
        tp_ps = E(nc.psum_tensor("Btp_ps", [128, 8, 128], BF16))
        h_ps = [E(nc.psum_tensor(f"Bh_ps{i}", [128, 512], F32)) for i in range(2)]
        y_ps = [E(nc.psum_tensor(f"By_ps{i}", [128, 512], F32)) for i in range(4)]
        d_ps = E(nc.psum_tensor("Bd_ps", [128, 512], F32))

        p.dma("gpsimd", wr_sb[:, :, :], wr.rearrange("(c p) o -> p c o", p=128), inc=lw)
        p.dma("gpsimd", br_r[:, :], br[None, :], inc=lw)
        p.dma("gpsimd", gam_r[:, :], gam[None, :], inc=lw)
        p.dma("gpsimd", bet_r[:, :], bet[None, :], inc=lw)
        p.dma("gpsimd", idn_b[:, :], idn[:, :], inc=lw)
        p.dma("gpsimd", idn_f[:, :], idn[:, :], inc=lw)
        p.dma("gpsimd", b2_sb[:, :], b2[:, :], inc=lw)
        p.dma("gpsimd", ltri_b[:, :], ltri[:, :], inc=lw)
        p.dma("gpsimd", iota_c[:, :], iotap[:, :], inc=lw)
        for e in range(8):
            p.dma("gpsimd", mg_dram[e, :, :], mg_init[:, :], inc=lw)
        p.i("gpsimd", lambda g: g.memset(ones_b[:, :], 1.0))
        p.i("gpsimd", lambda g: g.memset(ones_f[:, :], 1.0))
        p.i("gpsimd", lambda g: g.memset(ones_c[:, :], 1.0))
        p.i("gpsimd", lambda g: g.memset(ones8[:, :], 1.0))
        p.i("gpsimd", lambda g: g.memset(zsb[:, :], 0.0))
        p.i("gpsimd", lambda g: g.memset(offs_bc[:, :], 0.0), inc=lw, n=1)
        lw0 = p.v(lw)
        p.w("sync", lw, lw0)
        for i in range(17):
            p.dma("sync", acc_dram[i * 128:(i + 1) * 128, 0:512], zsb[:, :], inc=ks)
            p.dma("sync", acc_dram[i * 128:(i + 1) * 128, 512:1024], zsb[:, :], inc=ks)
        for e in range(8):
            for a in range(CAP // 128):
                p.i("gpsimd", lambda g, e=e, a=a: g.dma_start(
                    out=xg_dram[e, a * 128:(a + 1) * 128, 0:512], in_=zsb[:, :]), inc=ks, n=16)
                p.i("gpsimd", lambda g, e=e, a=a: g.dma_start(
                    out=xg_dram[e, a * 128:(a + 1) * 128, 512:1024], in_=zsb[:, :]), inc=ks, n=16)
        zfill_ks = p.v(ks)
        p.w("vector", lw, lw0)
        p.i("vector", lambda v: v.tensor_scalar(padv[:, :], iota_c[:, :], float(CAP), None, ALU.add), inc=dv)

        p.w("tensor", lw, lw0)
        for nm_r, nm_bc in ((gam_r, gam_bc), (bet_r, bet_bc)):
            for half in range(2):
                sl = slice(half * 512, (half + 1) * 512)
                mm = p.i("tensor", lambda t, sl=sl, r_=nm_r: t.matmul(
                    d_ps[:, 0:512], ones_f[:, :], r_[:, sl], start=True, stop=True), inc=pe)
                p.w("vector", pe, mm)
                cp = p.i("vector", lambda v, sl=sl, bc=nm_bc: v.tensor_copy(bc[:, sl], d_ps[:, 0:512]), inc=dv)
                p.w("tensor", dv, cp)

        for half in range(2):
            bl = p.dma("gpsimd", b1_sb[:, :], b1[:, half * 2048:(half + 1) * 2048], inc=lw)
            p.w("tensor", lw, bl)
            for c in range(16):
                p.i("tensor", lambda t, c=c: t.transpose(
                    d_ps[:, c * 8:(c + 1) * 8], b1_sb[0:8, c * 128:(c + 1) * 128], idn_f[0:8, 0:8]),
                    inc=pe if c == 15 else None)
            p.w("vector", pe)
            for c in range(16):
                cg = half * 16 + c
                p.i("vector", lambda v, c=c, cg=cg: v.tensor_copy(
                    b1_pt[:, :, cg], d_ps[:, c * 8:(c + 1) * 8]), inc=dv if c == 15 else None)
            p.w("gpsimd", dv)
            p.w("tensor", dv)
        p.i("vector", lambda v: v.tensor_scalar(brelu[:, :, :], b1_pt[:, :, :], 0.0, None, ALU.max), inc=dv)
        brelu_dv = p.v(dv)

        for half in range(2):
            sl = slice(half * 512, (half + 1) * 512)
            mm = p.i("tensor", lambda t, sl=sl: t.matmul(
                d_ps[0:1, 0:512], ones8[:, :], b2_sb[:, sl], start=True, stop=True), inc=pe)
            p.w("vector", pe, mm)
            cp = p.i("vector", lambda v, sl=sl: v.tensor_copy(sA[0:1, sl], d_ps[0:1, 0:512]), inc=dv)
            p.w("tensor", dv, cp)
        p.i("vector", lambda v: v.tensor_copy(xn_tm[0:1, 0, :], sA[0:1, :]), inc=dv)
        p.w("gpsimd", dv)
        b2s_lw = p.dma("gpsimd", dmat[8:9, :], xn_tm[0:1, 0, :], inc=lw)

        xn_read_dv = {}
        rtr_pe_hist = {}
        rtr_evac_hist = {}
        sc_hist = {}
        xn_tp_pe = 0
        sc_ks = 0
        for t in range(NT):
            r = t % 2
            if t >= 2:
                p.w("sync", dv, xn_read_dv[t - 2])
            ldv = p.dma("sync", x_t[:, r, :], x_in[t * 128:(t + 1) * 128, :], inc=ld)
            p.w("vector", ld, ldv)
            p.i("vector", lambda v, t=t, rr=r: v.bn_stats(bnst_all[:, t, 0:6], x_t[:, rr, 0:512]))
            p.i("vector", lambda v, t=t, rr=r: v.bn_stats(bnst_all[:, t, 6:12], x_t[:, rr, 512:1024]), inc=dv)
            xn_read_dv[t] = p.v(dv)
        p.w("vector", dv)
        for t in range(NT):
            p.i("vector", lambda v, t=t: v.bn_aggr(mv_all[:, t, :], bnst_all[:, t, :]),
                inc=dv if t == NT - 1 else None)
        p.w("vector", dv)
        p.i("vector", lambda v: v.tensor_scalar(bnst_all[:, :, 0], mv_all[:, :, 1], 1.0, 1e-5,
                                                ALU.mult, op1=ALU.add), inc=dv)
        p.i("vector", lambda v: v.tensor_scalar(bnst_all[:, :, 1], mv_all[:, :, 1], 1.00001, 1e-10,
                                                ALU.mult, op1=ALU.add), inc=dv)
        p.w("scalar", dv)
        p.i("scalar", lambda s: s.activation(bnst_all[:, :, 0], bnst_all[:, :, 0], AF.Ln))
        p.i("scalar", lambda s: s.activation(bnst_all[:, :, 1], bnst_all[:, :, 1], AF.Ln), inc=ac)
        p.w("scalar", ac)
        p.i("scalar", lambda s: s.activation(ab_all[:, :, 0], bnst_all[:, :, 0], AF.Exp, scale=-0.5))
        p.i("scalar", lambda s: s.activation(ap_all[:, :, 0], bnst_all[:, :, 1], AF.Exp, scale=-0.5), inc=ac)
        p.w("vector", ac)
        p.i("vector", lambda v: v.tensor_tensor(ab_all[:, :, 1], mv_all[:, :, 0], ab_all[:, :, 0], ALU.mult), inc=dv)
        p.i("vector", lambda v: v.tensor_tensor(ap_all[:, :, 1], mv_all[:, :, 0], ap_all[:, :, 0], ALU.mult), inc=dv)
        p.w("vector", dv)
        p.i("vector", lambda v: v.tensor_scalar(ab_all[:, :, 1], ab_all[:, :, 1], -1.0, None, ALU.mult))
        p.i("vector", lambda v: v.tensor_scalar(ap_all[:, :, 1], ap_all[:, :, 1], -1.0, None, ALU.mult), inc=dv)
        pre_dv = p.v(dv)
        p.w("vector", dv, pre_dv)
        p.w("vector", lw, b2s_lw)

        for t in range(NT):
            r = t % 2
            p.w("sync", dv, pre_dv if t < 2 else xn_read_dv[NT + t - 2])
            ldv = p.dma("sync", x_t[:, r, :], x_in[t * 128:(t + 1) * 128, :], inc=ld)
            p.w("vector", ld, ldv)
            p.i("vector", lambda v, t=t, rr=r: v.tensor_scalar(
                sA[:, :], x_t[:, rr, :], ap_all[:, t, 0:1], ap_all[:, t, 1:2],
                ALU.mult, op1=ALU.add), inc=dv)
            xn_read_dv[NT + t] = p.v(dv)
            p.w("vector", dv)
            p.i("vector", lambda v: v.tensor_tensor(sB[:, :], sA[:, :], gam_bc[:, :], ALU.mult), inc=dv)
            p.w("vector", dv)
            p.i("vector", lambda v, t=t: v.tensor_tensor(xn_tm[:, t, :], sB[:, :], bet_bc[:, :], ALU.add), inc=dv)
        xn_all_dv = p.v(dv)
        p.w("tensor", dv, xn_all_dv)

        for t in range(NT):
            r = t % 2
            if t >= 1:
                p.w("tensor", dv, rtr_evac_hist[t - 1])
            for c in range(8):
                p.i("tensor", lambda tt, c=c, t=t: tt.transpose(
                    tp_ps[:, c, :], xn_tm[:, t, c * 128:(c + 1) * 128], idn_b[:, :]),
                    inc=pe if c == 7 else None)
            xn_tp_pe = p.v(pe)
            p.w("vector", pe, xn_tp_pe)
            if t >= 2:
                p.w("vector", pe, rtr_pe_hist[t - 2])
            p.i("vector", lambda v, t=t, rr=r: v.tensor_copy(
                xn_fm[:, rr, :, :], tp_ps[:, :, :]), inc=dv)
            xnfm_dv = p.v(dv)
            rtr_evac_hist[t] = xnfm_dv

            p.w("tensor", dv, xnfm_dv)
            if t >= 1:
                p.w("tensor", ac, rt_exp_ac)
                p.w("tensor", dv, rt_read_dv)
            for c in range(8):
                p.i("tensor", lambda tt, c=c, rr=r: tt.matmul(
                    d_ps[:, 0:8], xn_fm[:, rr, c, :], wr_sb[:, c, :],
                    start=(c == 0), stop=False))
            rmm = p.i("tensor", lambda tt: tt.matmul(
                d_ps[:, 0:8], ones_b[:, :], br_r[:, :], start=False, stop=True), inc=pe)
            rtr_pe_hist[t] = rmm
            p.w("vector", pe, rmm)
            rt_m1_dv = p.i("vector", lambda v: v.tensor_reduce(rt[:, 8:9], d_ps[:, 0:8], AX.X, ALU.max), inc=dv)
            p.w("vector", dv)
            p.i("vector", lambda v: v.tensor_scalar(rt[:, 9:10], rt[:, 8:9], -1.0, None, ALU.mult), inc=dv)
            p.w("scalar", dv)
            rt_exp_ac = p.i("scalar", lambda s: s.activation(rt[:, 0:8], d_ps[:, 0:8], AF.Exp,
                                                             bias=rt[:, 9:10], accum_out=rt[:, 10:11]), inc=ac)
            p.w("vector", ac)
            p.i("vector", lambda v: v.reciprocal(rt[:, 11:12], rt[:, 10:11]), inc=dv)
            p.w("vector", dv)
            p.i("vector", lambda v: v.tensor_scalar(rt[:, 0:8], rt[:, 0:8], rt[:, 11:12], None,
                                                    ALU.mult), inc=dv)
            p.w("vector", dv)
            p.i("vector", lambda v: v.tensor_reduce(rt[:, 12:13], rt[:, 0:8], AX.X, ALU.max), inc=dv)
            p.w("vector", dv)
            p.i("vector", lambda v: v.tensor_scalar(rt[:, 16:24], rt[:, 0:8], rt[:, 12:13], None,
                                                    ALU.is_ge), inc=dv)
            p.w("vector", dv)
            p.i("vector", lambda v: v.tensor_reduce(rt[:, 13:14], rt[:, 16:24], AX.X, ALU.add))
            p.i("vector", lambda v: v.tensor_scalar(rt[:, 24:32], rt[:, 16:24], -1e30, None,
                                                    ALU.mult), inc=dv)
            p.w("vector", dv)
            p.i("vector", lambda v: v.tensor_tensor(rt[:, 24:32], rt[:, 0:8], rt[:, 24:32], ALU.add), inc=dv)
            p.w("vector", dv)
            p.i("vector", lambda v: v.tensor_reduce(rt[:, 14:15], rt[:, 24:32], AX.X, ALU.max))
            p.i("vector", lambda v: v.tensor_scalar(rt[:, 15:16], rt[:, 13:14], 2.0, None,
                                                    ALU.is_lt), inc=dv)
            p.w("vector", dv)
            p.i("vector", lambda v: v.tensor_scalar(rt[:, 32:40], rt[:, 0:8], rt[:, 14:15],
                                                    rt[:, 15:16], ALU.is_ge, op1=ALU.mult), inc=dv)
            p.w("vector", dv)
            p.i("vector", lambda v: v.tensor_tensor(rt[:, 32:40], rt[:, 16:24], rt[:, 32:40], ALU.max), inc=dv)
            p.w("vector", dv)
            p.i("vector", lambda v: v.tensor_tensor(rt[:, 40:48], rt[:, 0:8], rt[:, 32:40], ALU.mult), inc=dv)
            p.w("vector", dv)
            p.i("vector", lambda v, t=t: v.tensor_copy(gates[:, t::16], rt[:, 40:48]))
            p.i("vector", lambda v, t=t: v.tensor_reduce(s_all[:, t:t + 1], rt[:, 40:48], AX.X, ALU.add), inc=dv)
            p.w("vector", dv)
            p.i("vector", lambda v, t=t: v.tensor_scalar(
                gaug_b[:, 0:8], rt[:, 40:48], s_all[:, t:t + 1], -1.0, ALU.subtract, op1=ALU.mult), inc=dv)
            p.i("vector", lambda v, t=t: v.tensor_copy(gaug_b[:, 8:9], s_all[:, t:t + 1]), inc=dv)
            gaug_dv = p.v(dv)
            p.w("tensor", dv, gaug_dv)
            gt_mm = p.i("tensor", lambda tt: tt.transpose(
                tp_ps[0:16, 0, :], gaug_b[:, 0:16], idn_b[:, :]), inc=pe)
            xn_tp_pe = gt_mm
            p.w("vector", pe, gt_mm)
            p.i("vector", lambda v, t=t: v.tensor_copy(
                gT[:, t * 128:(t + 1) * 128], tp_ps[0:9, 0, :]), inc=dv)
            p.w("tensor", dv)

            p.i("vector", lambda v: v.tensor_copy(mask_b[:, :], rt[:, 32:40]), inc=dv)
            mb_dv = p.v(dv)
            p.w("tensor", dv, mb_dv)
            p.w("tensor", ac, rt_exp_ac)
            pf_mm = p.i("tensor", lambda tt: tt.matmul(
                d_ps[:, 8:16], ltri_b[:, :], mask_b[:, :], start=True, stop=True), inc=pe)
            tot_mm = p.i("tensor", lambda tt: tt.matmul(
                d_ps[0:1, 16:24], ones_c[:, :], rt[:, 32:40], start=True, stop=True), inc=pe)
            p.w("vector", pe, tot_mm)
            p.i("vector", lambda v: v.tensor_tensor(posf[:, :], d_ps[:, 8:16], offs_bc[:, :], ALU.add), inc=dv)
            p.w("vector", dv)
            p.i("vector", lambda v: v.tensor_scalar(dmix[:, :], posf[:, :], padv[:, 0:1], None,
                                                    ALU.subtract), inc=dv)
            p.w("vector", dv)
            p.i("vector", lambda v: v.tensor_tensor(dmix[:, :], dmix[:, :], rt[:, 32:40], ALU.mult), inc=dv)
            p.w("vector", dv)
            p.i("vector", lambda v: v.tensor_scalar(dmix[:, :], dmix[:, :], padv[:, 0:1], None,
                                                    ALU.add), inc=dv)
            p.w("vector", dv)
            if t >= 1:
                p.w("vector", ks, sc_hist[t - 1])
            p.i("vector", lambda v: v.tensor_copy(d_u32[:, :], dmix[:, :]), inc=dv)
            rt_read_dv = p.v(dv)
            p.i("vector", lambda v, t=t: v.tensor_scalar(tid_c[:, :], iota_c[:, :], float(t * 128),
                                                         None, ALU.add), inc=dv)
            p.w("vector", dv)
            p.i("vector", lambda v: v.tensor_copy(mrow[:, :, 0], rt[:, 40:48]))
            p.i("vector", lambda v: v.tensor_scalar(mrow[:, :, 1], rt[:, 32:40], 0.0, tid_c[:, 0:1],
                                                    ALU.mult, op1=ALU.add), inc=dv)
            mrow_dv = p.v(dv)
            p.i("vector", lambda v: v.tensor_copy(tot_r[:, :], d_ps[0:1, 16:24]), inc=dv)
            trd = p.v(dv)
            p.w("tensor", dv, trd)
            bc_mm = p.i("tensor", lambda tt: tt.matmul(
                d_ps[:, 24:32], ones_f[:, :], tot_r[:, :], start=True, stop=True), inc=pe)
            p.w("vector", pe, bc_mm)
            p.i("vector", lambda v: v.tensor_tensor(offs_bc[:, :], offs_bc[:, :], d_ps[:, 24:32],
                                                    ALU.add), inc=dv)
            offs_dv = p.v(dv)
            p.w("tensor", dv, offs_dv)
            p.w("gpsimd", dv, mrow_dv)
            if t == 0:
                p.w("gpsimd", ks, zfill_ks)
            xgflat = xg_dram.rearrange("e r f -> (e r) f")
            mgflat = mg_dram.rearrange("e r c -> (e r) c")
            for e in range(8):
                p.i("gpsimd", lambda g, e=e, t=t: g.indirect_dma_start(
                    out=xgflat,
                    out_offset=bass.IndirectOffsetOnAxis(d_u32[:, e:e + 1], 0),
                    in_=xn_tm[:, t, :], in_offset=None,
                    element_offset=e * ROWS_E * H_), inc=ks, n=16)
                p.i("gpsimd", lambda g, e=e: g.indirect_dma_start(
                    out=mgflat,
                    out_offset=bass.IndirectOffsetOnAxis(d_u32[:, e:e + 1], 0),
                    in_=mrow[:, e, :], in_offset=None,
                    element_offset=e * ROWS_E * 2), inc=ks, n=16)
            sc_ks = p.v(ks)
            sc_hist[t] = sc_ks
        pass1_dv = p.v(dv)
        pass1_pe = p.v(pe)
        p.w("scalar", dv, pass1_dv)
        p.w("vector", lw, lw0)
        p.w("tensor", ac, p.v(ac))

        p.w("sync", ks, sc_ks)

        w1_lw = w2_lw = None
        dps_free_dv = pass1_dv
        hps_free_ac = [0, 0]
        yps_free = [pass1_dv] * 4
        u_free_pe = pass1_pe
        ybuf_free_ks = 0
        ysc_free_ks = [0, 0]
        prev_scop = [0, 0]
        prev_sadd = [0, 0]
        adone = {}
        xg_lw = {}
        sadd_ks = p.v(ks)
        w1s_free_pe = pass1_pe
        w2s_free_pe = pass1_pe
        xg_read_pe = 0
        for e in range(8):
            er = e % 2
            p.w("sync", ac, prev_scop[er])
            p.w("sync", ks, prev_sadd[er])
            gl2 = p.dma("sync", mg_sb[:, er, :, :],
                        mg_dram[e, 0:CAP, :].rearrange("(a p) c -> p a c", p=128), inc=ld)
            p.w("vector", ld, gl2)
            p.w("vector", ks, prev_sadd[er])
            p.i("vector", lambda v, er=er: v.tensor_copy(tok_u[:, er, :], mg_sb[:, er, :, 1]), inc=dv)
            tok_dv = p.v(dv)
            gl_hist = {}
            for a in range(CAP // 128):
                ar = a % 2
                if a >= 2:
                    p.w("sync", pe, xg_read_hist[a - 2])
                gl_hist[a] = p.dma("sync", xg_sb[:, ar, :],
                                   xg_dram[e, a * 128:(a + 1) * 128, :], inc=ld)
                if a == 0:
                    xg_read_hist = {}
                p.w("tensor", ld, gl_hist[a])
                p.w("tensor", dv, xgf_evac if (e > 0 or a > 0) else 0)
                for c in range(8):
                    p.i("tensor", lambda tt, ar=ar, c=c: tt.transpose(
                        tp_ps[:, c, :], xg_sb[:, ar, c * 128:(c + 1) * 128], idn_b[:, :]),
                        inc=pe if c == 7 else None)
                tpv = p.v(pe)
                xg_read_hist[a] = tpv
                p.w("vector", pe, tpv)
                p.i("vector", lambda v, a=a: v.tensor_copy(
                    xgf[:, :, a * 128:(a + 1) * 128], tp_ps[:, :, :]), inc=dv)
                xgf_evac = p.v(dv)
            xgf_dv = p.v(dv)
            p.w("tensor", dv, xgf_dv)
            for s in range(NSL):
                fsl = slice(s * FS, (s + 1) * FS)
                p.w("gpsimd", pe, w1s_free_pe)
                w1_lw = p.dma("gpsimd", w1s[:, :, :],
                              w1[e, :, fsl].rearrange("(c p) f -> p c f", p=128), inc=lw)
                p.w("gpsimd", pe, w2s_free_pe)
                w2_lw = p.dma("gpsimd", w2s[:, :, :],
                              w2[e, fsl, :].rearrange("(c p) o -> p c o", p=128), inc=lw)
                for g2 in range(2):
                    gw = 512 if g2 == 0 else CAP - 512
                    gcols = slice(g2 * 512, g2 * 512 + gw)
                    p.w("tensor", lw, w1_lw)
                    for ft in range(FS // 128):
                        hb = ft % 2
                        p.w("tensor", ac, hps_free_ac[hb])
                        for c in range(8):
                            p.i("tensor", lambda tt, c=c, ft=ft, hb=hb, gcols=gcols, s=s, gw=gw: tt.matmul(
                                h_ps[hb][:, 0:gw], w1s[:, c, ft * 128:(ft + 1) * 128],
                                xgf[:, c, gcols], start=(c == 0), stop=(c == 7)),
                                inc=pe if c == 7 else None)
                        hmm = p.v(pe)
                        p.w("scalar", pe, hmm)
                        cg = s * (FS // 128) + ft
                        hps_free_ac[hb] = p.i("scalar", lambda sc, ft=ft, hb=hb, e=e, cg=cg, gw=gw: sc.activation(
                            u_sb[:, ft, 0:gw], h_ps[hb][:, 0:gw], AF.Relu,
                            bias=b1_pt[:, e, cg:cg + 1]), inc=ac)
                    relu_done = p.v(ac)
                    if g2 == 1:
                        w1s_free_pe = p.v(pe)
                    p.w("tensor", ac, relu_done)
                    p.w("tensor", lw, w2_lw)
                    for half in range(2):
                        osl = slice(half * 512, (half + 1) * 512)
                        for tt_ in range(gw // 128):
                            a = g2 * 4 + tt_
                            p.w("tensor", dv, yps_free[tt_])
                            for fc in range(FS // 128):
                                p.i("tensor", lambda tt, fc=fc, tt_=tt_, osl=osl: tt.matmul(
                                    y_ps[tt_][:, 0:512], u_sb[:, fc, tt_ * 128:(tt_ + 1) * 128],
                                    w2s[:, fc, osl], start=(fc == 0), stop=(fc == FS // 128 - 1)),
                                    inc=pe if fc == FS // 128 - 1 else None)
                            ymm = p.v(pe)
                            if s == 0:
                                p.w("scalar", pe, ymm)
                                if e > 0:
                                    p.w("scalar", ks, ybuf_free_ks)
                                acop = p.i("scalar", lambda sc, a=a, osl=osl, tt_=tt_: sc.activation(
                                    ybuf[:, a, osl], y_ps[tt_][:, 0:512], AF.Copy), inc=ac)
                                p.w("vector", ac, acop)
                                p.i("vector", lambda v: v.tensor_copy(bn_sb[:, 0:1], bn_sb[:, 0:1]), inc=dv)
                                yps_free[tt_] = p.v(dv)
                            else:
                                p.w("vector", pe, ymm)
                                p.i("vector", lambda v, a=a, osl=osl, tt_=tt_: v.tensor_tensor(
                                    ybuf[:, a, osl], ybuf[:, a, osl], y_ps[tt_][:, 0:512],
                                    ALU.add), inc=dv)
                                yps_free[tt_] = p.v(dv)
                                if s == NSL - 1 and half == 1:
                                    yr = a % 2
                                    p.w("scalar", dv, yps_free[tt_])
                                    p.w("scalar", dv, tok_dv)
                                    p.w("scalar", ks, ysc_free_ks[yr])
                                    scop = p.i("scalar", lambda sc, a=a, yr=yr, er=er: sc.activation(
                                        ysc[:, yr, :], ybuf[:, a, :], AF.Copy,
                                        scale=mg_sb[:, er, a, 0:1]), inc=ac)
                                    p.w("gpsimd", ac, scop)
                                    p.w("gpsimd", ks, sadd_ks)
                                    sadd_ks = p.i("gpsimd", lambda g, a=a, yr=yr, er=er: g.indirect_dma_start(
                                        out=acc_dram[:, :],
                                        out_offset=bass.IndirectOffsetOnAxis(tok_u[:, er, a:a + 1], 0),
                                        in_=ysc[:, yr, :], in_offset=None,
                                        compute_op=ALU.add), inc=ks, n=16)
                                    ysc_free_ks[yr] = sadd_ks
                                    adone[a] = scop
                for ot in range(8):
                    p.w("tensor", dv, dps_free_dv)
                    for fc in range(FS // 128):
                        p.i("tensor", lambda tt, fc=fc, ot=ot, e=e, s=s: tt.matmul(
                            d_ps[:, 0:1], w2s[:, fc, ot * 128:(ot + 1) * 128],
                            brelu[:, e, s * (FS // 128) + fc:s * (FS // 128) + fc + 1],
                            start=(fc == 0), stop=(fc == FS // 128 - 1)),
                            inc=pe if fc == FS // 128 - 1 else None)
                    dmm = p.v(pe)
                    p.w("vector", pe, dmm)
                    if s == 0:
                        p.i("vector", lambda v, ot=ot, e=e: v.tensor_copy(
                            dcol[:, ot, e:e + 1], d_ps[:, 0:1]), inc=dv)
                    else:
                        p.i("vector", lambda v, ot=ot, e=e: v.tensor_tensor(
                            dcol[:, ot, e:e + 1], dcol[:, ot, e:e + 1], d_ps[:, 0:1],
                            ALU.add), inc=dv)
                    dps_free_dv = p.v(dv)
                w2s_free_pe = p.v(pe)
            prev_scop[er] = max(adone.values()) if adone else 0
            prev_sadd[er] = sadd_ks
            ybuf_free_ks = sadd_ks
            adone = {}

        p.w("tensor", dv, p.v(dv))
        for ot in range(8):
            dt_mm = p.i("tensor", lambda tt, ot=ot: tt.transpose(
                d_ps[0:8, 0:128], dcol[:, ot, :], idn_f[:, :]), inc=pe)
            p.w("vector", pe, dt_mm)
            p.i("vector", lambda v, ot=ot: v.tensor_copy(
                dmat[0:8, ot * 128:(ot + 1) * 128], d_ps[0:8, 0:128]), inc=dv)
            p.w("tensor", dv)
        dmat_dv = p.v(dv)

        p.w("sync", ks, sadd_ks)
        p.w("tensor", lw, b2s_lw)
        st_val = 0
        o_dv_hist = {}
        corr_free = [dmat_dv, dmat_dv]
        for tile in range(NT):
            r = 0
            if tile >= 1:
                p.w("sync", dv, o_dv_hist[tile - 1])
            lxv = p.dma("sync", x2_t[:, 0, :], x_in[tile * 128:(tile + 1) * 128, :], inc=lx)
            lav = p.dma("sync", sacc[:, 0, :], acc_dram[tile * 128:(tile + 1) * 128, :], inc=lx)
            if tile == 0:
                p.w("tensor", dv, dmat_dv)
            p.w("tensor", dv, corr_free[0])
            p.w("tensor", dv, corr_free[1])
            for half in range(2):
                osl = slice(half * 512, (half + 1) * 512)
                p.i("tensor", lambda tt, half=half, tile=tile, osl=osl: tt.matmul(
                    y_ps[half][:, 0:512], gT[:, tile * 128:(tile + 1) * 128], dmat[:, osl],
                    start=True, stop=True), inc=pe if half == 1 else None)
            cmm = p.v(pe)
            p.w("vector", pe, cmm)
            p.w("vector", lx, lav)
            p.i("vector", lambda v, r=r, tile=tile: v.tensor_scalar(
                sA[:, :], x2_t[:, r, :], ab_all[:, tile, 0:1], ab_all[:, tile, 1:2],
                ALU.mult, op1=ALU.add), inc=dv)
            p.w("vector", dv)
            for half in range(2):
                osl = slice(half * 512, (half + 1) * 512)
                p.i("vector", lambda v, half=half, osl=osl, r=r: v.tensor_tensor(
                    sB[:, osl], sacc[:, r, osl], y_ps[half][:, 0:512], ALU.add),
                    inc=dv if half == 1 else None)
            corr_free[0] = corr_free[1] = p.v(dv)
            p.w("vector", dv)
            p.i("vector", lambda v: v.tensor_tensor(sB[:, :], sA[:, :], sB[:, :], ALU.add), inc=dv)
            p.w("vector", dv)
            _ln_ab(p, sB[:, :], bn_sb, ab_sb, 1e-5, dv, ac)
            p.i("vector", lambda v: v.tensor_scalar(
                sA[:, :], sB[:, :], ab_sb[:, 0:1], ab_sb[:, 1:2], ALU.mult, op1=ALU.add), inc=dv)
            p.w("vector", dv)
            if tile >= 1:
                p.w("vector", st, st_val)
            p.i("vector", lambda v: v.tensor_tensor(
                out_t[:, 0, :], sA[:, :], x2_t[:, 0, :], ALU.add), inc=dv)
            o_dv = p.v(dv)
            o_dv_hist[tile] = o_dv
            p.w("sync", dv, o_dv)
            st_val = p.dma("sync", y[tile * 128:(tile + 1) * 128, :], out_t[:, 0, :], inc=st)
        p.w("sync", st, st_val)
